# version 07
# baseline (speedup 1.0000x reference)
import numpy as np
import ml_dtypes
import concourse.bass as bass
import concourse.mybir as mybir
from concourse.bass_utils import run_bass_kernel_spmd
from contextlib import ExitStack

F32 = mybir.dt.float32
BF16 = mybir.dt.bfloat16
AF = mybir.ActivationFunctionType
ALU = mybir.AluOpType
AX = mybir.AxisListType

NCORES = 8
TB = 2
T = 2048
D = 1024
DC = 8
NT = 16
NE = 8
HH = 1024
NHT = 8
NEG = -3.0e38
ENGS = ("sync", "gpsimd", "tensor", "scalar", "vector")


def build_program(dbg=False):
    nc = bass.Bass("TRN2", target_bir_lowering=False)

    x_hbm = nc.dram_tensor("x", [TB * T, D], F32, kind="ExternalInput").ap()
    wq_h = nc.dram_tensor("wq", [D, D], F32, kind="ExternalInput").ap()
    wk_h = nc.dram_tensor("wk", [D, D], F32, kind="ExternalInput").ap()
    wv_h = nc.dram_tensor("wv", [D, D], F32, kind="ExternalInput").ap()
    bqT_h = nc.dram_tensor("bqT", [128, DC], F32, kind="ExternalInput").ap()
    bkT_h = nc.dram_tensor("bkT", [128, DC], F32, kind="ExternalInput").ap()
    bvb_h = nc.dram_tensor("bvb", [128, D], F32, kind="ExternalInput").ap()
    rw_h = nc.dram_tensor("rw", [D, NE], F32, kind="ExternalInput").ap()
    gw_h = nc.dram_tensor("gw", [NE, D, 2 * HH], F32, kind="ExternalInput").ap()
    uw_h = nc.dram_tensor("uw", [NE, D, 2 * HH], F32, kind="ExternalInput").ap()
    dw_h = nc.dram_tensor("dw", [NE, 2 * HH, D], F32, kind="ExternalInput").ap()
    ident_h = nc.dram_tensor("ident", [128, 128], BF16, kind="ExternalInput").ap()
    ident32_h = nc.dram_tensor("ident32", [128, 128], F32, kind="ExternalInput").ap()
    maskd_h = nc.dram_tensor("maskd", [128, 128], F32, kind="ExternalInput").ap()
    ones_c_h = nc.dram_tensor("ones_c", [128, 1], BF16, kind="ExternalInput").ap()
    ones_r_h = nc.dram_tensor("ones_r", [1, 128], F32, kind="ExternalInput").ap()
    eps_h = nc.dram_tensor("eps", [128, 1], F32, kind="ExternalInput").ap()
    out_hbm = nc.dram_tensor("out", [TB * T, D], F32, kind="ExternalOutput").ap()
    if dbg:
        dbg_xnT = nc.dram_tensor("dbg_xnT", [128, DC * T], BF16, kind="ExternalOutput").ap()
        dbg_yat = nc.dram_tensor("dbg_yat", [128, NT * D], F32, kind="ExternalOutput").ap()
        dbg_c = nc.dram_tensor("dbg_c", [128, NT * NE], F32, kind="ExternalOutput").ap()
        dbg_s = nc.dram_tensor("dbg_s", [1, 8], F32, kind="ExternalOutput").ap()

    stack = ExitStack()
    with stack:
        xnT = stack.enter_context(nc.sbuf_tensor("xnT", [128, DC * T], BF16)).ap()
        y = stack.enter_context(nc.sbuf_tensor("y", [128, NT * D], F32)).ap()
        scr_t = stack.enter_context(nc.sbuf_tensor("scr", [128, 53248], BF16))
        scr = scr_t.ap()
        scrf = scr_t.ap().bitcast(F32)
        xnb_t = stack.enter_context(nc.sbuf_tensor("xnb_t", [128, 2048], BF16)).ap()

        identity = stack.enter_context(nc.sbuf_tensor("identity_s", [128, 128], BF16)).ap()
        ident32 = stack.enter_context(nc.sbuf_tensor("ident32_s", [128, 128], F32)).ap()
        maskd = stack.enter_context(nc.sbuf_tensor("maskd_s", [128, 128], F32)).ap()
        ones_c = stack.enter_context(nc.sbuf_tensor("ones_c_s", [128, 1], BF16)).ap()
        ones_r = stack.enter_context(nc.sbuf_tensor("ones_r_s", [1, 128], F32)).ap()
        eps_t = stack.enter_context(nc.sbuf_tensor("eps_t", [128, 1], F32)).ap()
        bqT = stack.enter_context(nc.sbuf_tensor("bqT_s", [128, DC], F32)).ap()
        bkT = stack.enter_context(nc.sbuf_tensor("bkT_s", [128, DC], F32)).ap()
        rw_s = stack.enter_context(nc.sbuf_tensor("rw_s", [128, DC * NE], F32)).ap()
        c_all = stack.enter_context(nc.sbuf_tensor("c_all", [128, NT * NE], F32)).ap()
        lg_all = stack.enter_context(nc.sbuf_tensor("lg_all", [128, NT * NE], F32)).ap()
        l2_all = stack.enter_context(nc.sbuf_tensor("l2_all", [128, NT * NE], F32)).ap()
        rts = stack.enter_context(nc.sbuf_tensor("rts", [128, 48], F32)).ap()
        nacc = rts[0:1, 0:16]
        zacc = rts[0:1, 16:32]
        s1v = rts[0:1, 32:40]
        sca = rts[:, 40:42]

        stats = l2_all[:, 0:32]

        pb = [stack.enter_context(nc.psum_tensor("pb%d" % i, [128, 512], F32)).ap()
              for i in range(8)]
        pbh = [p.bitcast(BF16) for p in pb]

        kT = scr[:, 0:16384]
        v_s = scr[:, 16384:32768]
        w0 = scr[:, 32768:40960]
        w1 = scr[:, 40960:49152]
        qT = scr[:, 49152:53248]
        pt = [scr[:, 40960:41216], scr[:, 41216:41472]]
        sqb = [scr[:, 41472:41984], scr[:, 41984:42496]]
        xst = [scrf[:, 24576:25600], scrf[:, 25600:26624]]
        xr = [scrf[:, 24576:25600], scrf[:, 25600:26624],
              scrf[:, 21504:22528], scrf[:, 22528:23552], scrf[:, 23552:24576]]
        bvb_v = scrf[:, 24576:25600]
        xnb = [xnb_t[:, 0:1024], xnb_t[:, 1024:2048]]
        gbuf = scr[:, 0:8192]
        ubuf = scr[:, 8192:16384]
        dbuf = scr[:, 16384:24576]
        hid = scr[:, 24576:40960]
        silu = [scrf[:, 20480:20992], scrf[:, 20992:21504]]
        xn32 = [scrf[:, 20480:21504], scrf[:, 21504:22528]]
        xnT32 = scrf[:, 22528:23552]

        ops = {e: [] for e in ENGS}
        sems = {}
        cnt = {}
        for nm in ("sC", "sX0", "sX1", "sW0", "sW1", "sGU", "sWD",
                   "sO0", "sO1", "sT", "sA", "sV", "sP",
                   "sR0", "sR1", "sR2", "sR3", "sR4",
                   "sQ0", "sQ1", "sQ2", "sQ3"):
            sems[nm] = stack.enter_context(nc.semaphore(nm))
            cnt[nm] = 0

        def add(e, fn):
            ops[e].append(fn)

        def wait(e, snm, val):
            if val > 0:
                add(e, lambda eng, s=sems[snm], v=val: eng.wait_ge(s, v))

        def inc(e, snm, fn, k=1):
            cnt[snm] += k
            add(e, lambda eng, f=fn, s=sems[snm], kk=k: f(eng).then_inc(s, kk))
            return cnt[snm]

        def emit(e, fn):
            add(e, fn)

        for dst, csrc in ((identity, ident_h), (ident32, ident32_h), (maskd, maskd_h),
                          (ones_c, ones_c_h), (ones_r, ones_r_h), (eps_t, eps_h),
                          (bqT, bqT_h), (bkT, bkT_h)):
            inc("sync", "sC", lambda e, d=dst, s=csrc: e.dma_start(out=d, in_=s), 16)
        inc("sync", "sC",
            lambda e: e.dma_start(out=rw_s.rearrange("p (c e) -> p c e", c=DC),
                                  in_=rw_h.rearrange("(c p) e -> p c e", p=128)), 16)
        n_const = cnt["sC"]
        wait("scalar", "sC", n_const)
        wait("vector", "sC", n_const)
        wait("tensor", "sC", n_const)

        end_prev = {"sT": 0, "sA": 0, "sV": 0, "sO0": 0, "sO1": 0, "hid": 0}
        resc_last = 0

        for b in range(TB):
            B0 = b * T

            xcast = [0] * NT
            xnT_cp = [0] * NT
            tr_done = [0] * NT
            f32tr_done = [0] * NT
            xn32cp = [0] * NT
            rt_mm = [0] * NT
            rt_ev = [0] * NT
            for i in range(NT):
                p = i % 2
                o = p * 16
                MEAN, VAR, STD, RSTD, NB = o + 12, o + 13, o + 14, o + 15, o + 0
                if i >= 2:
                    wait("sync", "sA", xcast[i - 2])
                elif b > 0:
                    wait("sync", "sV", resc_last)
                sxp = "sX%d" % p
                nx = inc("sync", sxp,
                         lambda e, d=xst[p], s=x_hbm[B0 + i * 128:B0 + (i + 1) * 128, :]:
                         e.dma_start(out=d, in_=s), 16)
                wait("vector", sxp, nx)
                emit("vector", lambda e, d=stats[:, o:o + 6], s=xst[p][:, 0:512]:
                     e.bn_stats(d, s))
                wbn = inc("vector", "sV",
                          lambda e, d=stats[:, o + 6:o + 12], s=xst[p][:, 512:1024]:
                          e.bn_stats(d, s))
                wait("vector", "sV", wbn)
                wag = inc("vector", "sV",
                          lambda e, d=stats[:, MEAN:MEAN + 2], s=stats[:, o:o + 12]:
                          e.bn_aggr(d, s))
                wait("scalar", "sV", wag)
                wstd = inc("scalar", "sA",
                           lambda e, d=stats[:, STD:STD + 1], s=stats[:, VAR:VAR + 1]:
                           e.activation(out=d, in_=s, func=AF.Sqrt, bias=eps_t))
                wait("vector", "sA", wstd)
                wrs = inc("vector", "sV",
                          lambda e, d=stats[:, RSTD:RSTD + 1], s=stats[:, STD:STD + 1]:
                          e.reciprocal(out=d, in_=s))
                wait("vector", "sV", wrs)
                wnb = inc("vector", "sV",
                          lambda e, d=stats[:, NB:NB + 1], s=stats[:, MEAN:MEAN + 1],
                          r=stats[:, RSTD:RSTD + 1]:
                          e.scalar_tensor_tensor(out=d, in0=s, scalar=-1.0, in1=r,
                                                 op0=ALU.mult, op1=ALU.mult))
                wait("scalar", "sV", wnb)
                if i >= 2:
                    wait("scalar", "sT", f32tr_done[i - 2])
                elif b > 0:
                    wait("scalar", "sV", end_prev["hid"])
                a32 = inc("scalar", "sA",
                          lambda e, d=xn32[p], s=xst[p], sc=stats[:, RSTD:RSTD + 1],
                          bi=stats[:, NB:NB + 1]:
                          e.activation(out=d, in_=s, func=AF.Identity, scale=sc, bias=bi))
                wait("scalar", "sA", a32)
                if i >= 2:
                    wait("scalar", "sT", tr_done[i - 2])
                xcast[i] = inc("scalar", "sA",
                               lambda e, d=xnb[p], s=xn32[p]:
                               e.activation(out=d, in_=s, func=AF.Copy))
                wait("tensor", "sA", xcast[i])
                if i < 2:
                    if b > 0:
                        wait("tensor", "sA", end_prev["sA"])
                else:
                    wait("tensor", "sV", xnT_cp[i - 2])
                for dc in range(DC):
                    fn = (lambda e, d=pbh[p][:, dc * 128:(dc + 1) * 128],
                          s=xnb[p][:, dc * 128:(dc + 1) * 128]:
                          e.transpose(d, s, identity))
                    if dc == DC - 1:
                        tr_done[i] = inc("tensor", "sT", fn)
                    else:
                        emit("tensor", fn)
                if i > 0:
                    wait("tensor", "sV", xn32cp[i - 1])
                elif b > 0:
                    wait("tensor", "sV", end_prev["hid"])
                for dc in range(DC):
                    fn = (lambda e,
                          d=pb[2 + dc // 4][:, (dc % 4) * 128:((dc % 4) + 1) * 128],
                          s=xn32[p][:, dc * 128:(dc + 1) * 128]:
                          e.transpose(d, s, ident32))
                    if dc == DC - 1:
                        f32tr_done[i] = inc("tensor", "sT", fn)
                    else:
                        emit("tensor", fn)
                wait("vector", "sT", tr_done[i])
                xnT_cp[i] = inc("vector", "sV",
                                lambda e,
                                d=xnT.rearrange("p (c t) -> p c t", c=DC)[:, :, i * 128:(i + 1) * 128],
                                s=pbh[p].rearrange("p (c t) -> p c t", c=DC):
                                e.tensor_copy(out=d, in_=s))
                wait("vector", "sT", f32tr_done[i])
                if i > 0:
                    wait("vector", "sT", rt_mm[i - 1])
                emit("vector",
                     lambda e: e.tensor_copy(out=xnT32[:, 0:512], in_=pb[2][:, 0:512]))
                xn32cp[i] = inc("vector", "sV",
                                lambda e: e.tensor_copy(out=xnT32[:, 512:1024],
                                                        in_=pb[3][:, 0:512]))
                lb = 4 + (i % 2)
                wait("tensor", "sV", xn32cp[i])
                if i > 1:
                    wait("tensor", "sA", rt_ev[i - 2])
                elif b > 0:
                    wait("tensor", "sV", end_prev["sV"])
                for kc in range(DC):
                    fn = (lambda e, o_=pb[lb][:, 0:8],
                          l=xnT32[:, kc * 128:(kc + 1) * 128],
                          r=rw_s[:, kc * 8:(kc + 1) * 8],
                          st=(kc == 0), sp=(kc == DC - 1):
                          e.matmul(o_, l, r, start=st, stop=sp))
                    if kc == DC - 1:
                        rt_mm[i] = inc("tensor", "sT", fn)
                    else:
                        emit("tensor", fn)
                if i > 0:
                    wait("scalar", "sT", rt_mm[i - 1])
                    rt_ev[i - 1] = inc("scalar", "sA",
                                       lambda e, d=lg_all[:, (i - 1) * 8:i * 8],
                                       s=pb[4 + ((i - 1) % 2)][:, 0:8]:
                                       e.activation(out=d, in_=s, func=AF.Copy))
                if i == NT - 1:
                    wait("scalar", "sT", rt_mm[i])
                    rt_ev[i] = inc("scalar", "sA",
                                   lambda e, d=lg_all[:, i * 8:(i + 1) * 8],
                                   s=pb[lb][:, 0:8]:
                                   e.activation(out=d, in_=s, func=AF.Copy))

            lg3 = lg_all.rearrange("p (t e) -> p t e", t=NT)
            l23 = l2_all.rearrange("p (t e) -> p t e", t=NT)
            c3 = c_all.rearrange("p (t e) -> p t e", t=NT)
            def rbc(col):
                return rts[:, col:col + 16].rearrange("p (t o) -> p t o", o=1) \
                          .broadcast_to([128, NT, NE])
            wait("vector", "sA", rt_ev[NT - 1])
            wv_ = inc("vector", "sV",
                      lambda e: e.tensor_reduce(rts[:, 0:16], lg3, axis=AX.X, op=ALU.max))
            wait("vector", "sV", wv_)
            wv_ = inc("vector", "sV",
                      lambda e: e.tensor_tensor(out=c3, in0=lg3, in1=rbc(0), op=ALU.is_equal))
            wait("vector", "sV", wv_)
            wv_ = inc("vector", "sV",
                      lambda e: e.scalar_tensor_tensor(out=l23, in0=c3, scalar=NEG,
                                                       in1=lg3, op0=ALU.mult, op1=ALU.add))
            wait("vector", "sV", wv_)
            wv_ = inc("vector", "sV",
                      lambda e: e.tensor_reduce(rts[:, 16:32], l23, axis=AX.X, op=ALU.max))
            wait("vector", "sV", wv_)
            wv_ = inc("vector", "sV",
                      lambda e: e.tensor_sub(rts[:, 32:48], rts[:, 16:32], rts[:, 0:16]))
            wait("scalar", "sV", wv_)
            wa_ = inc("scalar", "sA",
                      lambda e: e.activation(out=rts[:, 0:16], in_=rts[:, 32:48],
                                             func=AF.Sigmoid, scale=-1.0))
            wait("scalar", "sA", wa_)
            wa_ = inc("scalar", "sA",
                      lambda e: e.activation(out=rts[:, 32:48], in_=rts[:, 32:48],
                                             func=AF.Sigmoid))
            wait("vector", "sA", wa_)
            wv_ = inc("vector", "sV",
                      lambda e: e.tensor_tensor(out=c3, in0=c3, in1=rbc(0), op=ALU.mult))
            wait("vector", "sV", wv_)
            wv_ = inc("vector", "sV",
                      lambda e: e.tensor_tensor(out=l23, in0=l23, in1=rbc(16), op=ALU.is_equal))
            wait("vector", "sV", wv_)
            wv_ = inc("vector", "sV",
                      lambda e: e.tensor_tensor(out=l23, in0=l23, in1=rbc(32), op=ALU.mult))
            wait("vector", "sV", wv_)
            c_rdy = inc("vector", "sV",
                        lambda e: e.tensor_add(c_all, c_all, l2_all))

            if dbg and b == 0:
                wait("sync", "sV", xnT_cp[NT - 1])
                inc("sync", "sC", lambda e: e.dma_start(out=dbg_xnT, in_=xnT), 16)
                wait("sync", "sV", c_rdy)
                inc("sync", "sC", lambda e: e.dma_start(out=dbg_c, in_=c_all), 16)

            wait("sync", "sA", xcast[NT - 1])
            bvb_ld = inc("sync", "sC",
                         lambda e: e.dma_start(out=bvb_v, in_=bvb_h), 16)

            if b > 0:
                wait("gpsimd", "sT", end_prev["sT"])
            wk_ld = inc("gpsimd", "sW0",
                        lambda e: e.dma_start(
                            out=w0.rearrange("p (c d) -> p c d", c=DC),
                            in_=wk_h.rearrange("(c p) d -> p c d", p=128)), 16)
            wait("gpsimd", "sT", rt_mm[NT - 1])
            wait("gpsimd", "sA", xcast[NT - 1])
            wv_ld = inc("gpsimd", "sW1",
                        lambda e: e.dma_start(
                            out=w1.rearrange("p (c d) -> p c d", c=DC),
                            in_=wv_h.rearrange("(c p) d -> p c d", p=128)), 16)

            kT_ev = {}
            g_ev = [0, 0, 0, 0]
            gi = 0
            last_kT_mm = 0
            first_kt = True
            for dt in range(DC):
                for tc4 in range(4):
                    g = gi % 4
                    gi += 1
                    wait("tensor", "sW0", wk_ld)
                    wait("tensor", "sV", xnT_cp[tc4 * 4 + 3])
                    if first_kt:
                        wait("tensor", "sV", xn32cp[NT - 1])
                        first_kt = False
                    if g_ev[g]:
                        wait("tensor", "sA", g_ev[g])
                    for kc in range(DC):
                        fn = (lambda e, o_=pb[g][:, 0:512],
                              l=w0[:, kc * 1024 + dt * 128: kc * 1024 + (dt + 1) * 128],
                              r=xnT[:, kc * 2048 + tc4 * 512: kc * 2048 + (tc4 + 1) * 512],
                              st=(kc == 0), sp=(kc == DC - 1):
                              e.matmul(o_, l, r, start=st, stop=sp))
                        if kc == DC - 1:
                            last_kT_mm = inc("tensor", "sT", fn)
                        else:
                            emit("tensor", fn)
                    wait("scalar", "sT", last_kT_mm)
                    g_ev[g] = inc("scalar", "sA",
                                  lambda e,
                                  d=kT[:, dt * 2048 + tc4 * 512: dt * 2048 + (tc4 + 1) * 512],
                                  s=pb[g][:, 0:512], bi=bkT[:, dt:dt + 1]:
                                  e.activation(out=d, in_=s, func=AF.Identity, bias=bi))
                    kT_ev[(dt, tc4)] = g_ev[g]
            kT_done_A = cnt["sA"]

            wait("gpsimd", "sT", last_kT_mm)
            wq_ld = inc("gpsimd", "sW0",
                        lambda e: e.dma_start(
                            out=w0.rearrange("p (c d) -> p c d", c=DC),
                            in_=wq_h.rearrange("(c p) d -> p c d", p=128)), 16)

            g_evv = [0, 0, 0, 0]
            gi = 0
            last_v_mm = 0
            first_v = True
            for st_ in range(NT):
                for dh in range(2):
                    g = gi % 4
                    gi += 1
                    wait("tensor", "sW1", wv_ld)
                    wait("tensor", "sV", xnT_cp[st_])
                    if first_v:
                        wait("tensor", "sA", kT_done_A)
                    if g_evv[g]:
                        wait("tensor", "sV", g_evv[g])
                    for kc in range(DC):
                        fn = (lambda e, o_=pb[g][:, 0:512],
                              l=xnT[:, kc * 2048 + st_ * 128: kc * 2048 + (st_ + 1) * 128],
                              r=w1[:, kc * 1024 + dh * 512: kc * 1024 + (dh + 1) * 512],
                              st=(kc == 0), sp=(kc == DC - 1):
                              e.matmul(o_, l, r, start=st, stop=sp))
                        if kc == DC - 1:
                            last_v_mm = inc("tensor", "sT", fn)
                        else:
                            emit("tensor", fn)
                    wait("vector", "sT", last_v_mm)
                    if first_v:
                        wait("vector", "sC", bvb_ld)
                        first_v = False
                    g_evv[g] = inc("vector", "sV",
                                   lambda e,
                                   d=v_s[:, st_ * 1024 + dh * 512: st_ * 1024 + (dh + 1) * 512],
                                   s=pb[g][:, 0:512], bi=bvb_v[:, dh * 512:(dh + 1) * 512]:
                                   e.tensor_tensor(out=d, in0=s, in1=bi, op=ALU.add))
            v_done_V = cnt["sV"]

            sq_mmT = 0
            kn_ev = 0
            wait("scalar", "sT", last_v_mm)
            for j in range(4):
                for dt in range(DC):
                    jj = dt % 2
                    wait("scalar", "sA", kT_ev[(dt, j)])
                    if sq_mmT:
                        wait("scalar", "sT", sq_mmT)
                    a_k = inc("scalar", "sA",
                              lambda e, d=sqb[jj],
                              s=kT[:, dt * 2048 + j * 512: dt * 2048 + (j + 1) * 512]:
                              e.square(out=d, in_=s))
                    wait("tensor", "sA", a_k)
                    if j == 0 and dt == 0:
                        wait("tensor", "sA", rt_ev[NT - 1])
                    if dt == 0 and kn_ev:
                        wait("tensor", "sV", kn_ev)
                    sq_mmT = inc("tensor", "sT",
                                 lambda e, r=sqb[jj], st=(dt == 0), sp=(dt == DC - 1):
                                 e.matmul(pb[4][0:1, 0:512], ones_c, r, start=st, stop=sp))
                wait("vector", "sT", sq_mmT)
                kn_ev = inc("vector", "sV",
                            lambda e, d=nacc[0:1, 8 + j:9 + j]:
                            e.reduce_max(d, pb[4][0:1, 0:512], axis=AX.X))

            def qT_proj(cc, qb_wA, qb_wV, scalar_pre_V=0):
                evA = 0
                lastT = 0
                first = True
                for dt in range(DC):
                    g = 6 + dt % 2
                    wait("tensor", "sW0", wq_ld)
                    if qb_wA[dt % 2]:
                        wait("tensor", "sA", qb_wA[dt % 2])
                    if qb_wV[dt % 2]:
                        wait("tensor", "sV", qb_wV[dt % 2])
                        qb_wV[dt % 2] = 0
                    for kc in range(DC):
                        fn = (lambda e, o_=pb[g][:, 0:512],
                              l=w0[:, kc * 1024 + dt * 128: kc * 1024 + (dt + 1) * 128],
                              r=xnT[:, kc * 2048 + cc * 512: kc * 2048 + (cc + 1) * 512],
                              st=(kc == 0), sp=(kc == DC - 1):
                              e.matmul(o_, l, r, start=st, stop=sp))
                        if kc == DC - 1:
                            lastT = inc("tensor", "sT", fn)
                        else:
                            emit("tensor", fn)
                    wait("scalar", "sT", lastT)
                    if first and scalar_pre_V:
                        wait("scalar", "sV", scalar_pre_V)
                        first = False
                    evA = inc("scalar", "sA",
                              lambda e, d=qT[:, dt * 512:(dt + 1) * 512],
                              s=pb[g][:, 0:512], bi=bqT[:, dt:dt + 1]:
                              e.activation(out=d, in_=s, func=AF.Identity, bias=bi))
                    qb_wA[dt % 2] = evA
                return evA, lastT

            qb_wA = [0, 0]
            qn_ev = 0
            for cc in range(4):
                q_evA, _ = qT_proj(cc, qb_wA, [0, 0],
                                   scalar_pre_V=(v_done_V if cc == 0 else 0))
                for dt in range(DC):
                    jj = dt % 2
                    wait("scalar", "sA", q_evA)
                    if sq_mmT:
                        wait("scalar", "sT", sq_mmT)
                    a_q = inc("scalar", "sA",
                              lambda e, d=sqb[jj], s=qT[:, dt * 512:(dt + 1) * 512]:
                              e.square(out=d, in_=s))
                    wait("tensor", "sA", a_q)
                    if cc == 0 and dt == 0:
                        wait("tensor", "sV", kn_ev)
                    if dt == 0 and qn_ev:
                        wait("tensor", "sV", qn_ev)
                    sq_mmT = inc("tensor", "sT",
                                 lambda e, r=sqb[jj], st=(dt == 0), sp=(dt == DC - 1):
                                 e.matmul(pb[5][0:1, 0:512], ones_c, r, start=st, stop=sp))
                wait("vector", "sT", sq_mmT)
                qn_ev = inc("vector", "sV",
                            lambda e, d=nacc[0:1, cc:cc + 1]:
                            e.reduce_max(d, pb[5][0:1, 0:512], axis=AX.X))

            wait("vector", "sV", qn_ev)
            wv_ = inc("vector", "sV",
                      lambda e: e.reduce_max(s1v[0:1, 0:1], nacc[0:1, 0:4], axis=AX.X))
            wait("vector", "sV", wv_)
            wv_ = inc("vector", "sV",
                      lambda e: e.reduce_max(s1v[0:1, 1:2], nacc[0:1, 8:12], axis=AX.X))
            wait("scalar", "sV", wv_)
            wa_ = inc("scalar", "sA",
                      lambda e: e.activation(out=s1v[0:1, 2:3], in_=s1v[0:1, 0:1], func=AF.Sqrt))
            wait("scalar", "sA", wa_)
            wa_ = inc("scalar", "sA",
                      lambda e: e.activation(out=s1v[0:1, 3:4], in_=s1v[0:1, 1:2], func=AF.Sqrt))
            wait("vector", "sA", wa_)
            wv_ = inc("vector", "sV",
                      lambda e: e.scalar_tensor_tensor(out=s1v[0:1, 4:5], in0=s1v[0:1, 2:3],
                                                       scalar=-1.0 / 32.0, in1=s1v[0:1, 3:4],
                                                       op0=ALU.mult, op1=ALU.mult))
            wait("tensor", "sV", wv_)
            bc_mm = inc("tensor", "sT",
                        lambda e: e.matmul(pb[5][:, 0:1], ones_r, s1v[0:1, 4:5],
                                           start=True, stop=True))
            wait("vector", "sT", bc_mm)
            negC_rdy = inc("vector", "sV",
                           lambda e: e.tensor_copy(out=sca[:, 0:1], in_=pb[5][:, 0:1]))

            out_ev = [0, 0, 0, 0]
            sc_exA = [0, 0]
            pt_rdT = [0, 0]
            zr_evV = 0
            last_pv_mm = 0
            for cc in range(4):
                qb_wV2 = [zr_evV, zr_evV]
                q_evA, _ = qT_proj(cc, qb_wA, qb_wV2)
                for c2 in range(2):
                    c = 2 * cc + c2
                    toff = c2 * 256
                    nk = 2 * c + 2
                    for k in range(nk):
                        sb_ = 4 + (k % 2)
                        wait("tensor", "sA", q_evA)
                        if sc_exA[k % 2]:
                            wait("tensor", "sA", sc_exA[k % 2])
                        elif k % 2 == 0 and cc == 0 and c2 == 0:
                            wait("tensor", "sV", kn_ev)
                        elif k % 2 == 1 and cc == 0 and c2 == 0:
                            wait("tensor", "sV", negC_rdy)
                        if k == 0:
                            for gg in range(4):
                                if out_ev[gg]:
                                    wait("tensor", "sV", out_ev[gg])
                            if b > 0 and cc == 0 and c2 == 0:
                                wait("tensor", "sO0", end_prev["sO0"])
                                wait("tensor", "sO1", end_prev["sO1"])
                        lastS = 0
                        for dt in range(DC):
                            fn = (lambda e, o_=pb[sb_][:, 0:256],
                                  l=kT[:, dt * 2048 + k * 128: dt * 2048 + (k + 1) * 128],
                                  r=qT[:, dt * 512 + toff: dt * 512 + toff + 256],
                                  st=(dt == 0), sp=(dt == DC - 1):
                                  e.matmul(o_, l, r, start=st, stop=sp))
                            if dt == DC - 1:
                                lastS = inc("tensor", "sT", fn)
                            else:
                                emit("tensor", fn)
                        mk = 0
                        if k == 2 * c:
                            wait("vector", "sT", lastS)
                            mk = inc("vector", "sV",
                                     lambda e, o_=pb[sb_][:, 0:128]:
                                     e.tensor_tensor(out=o_, in0=o_, in1=maskd, op=ALU.add))
                        elif k == 2 * c + 1:
                            wait("vector", "sT", lastS)
                            emit("vector",
                                 lambda e, o_=pb[sb_][:, 0:128]: e.memset(o_, NEG))
                            mk = inc("vector", "sV",
                                     lambda e, o_=pb[sb_][:, 128:256]:
                                     e.tensor_tensor(out=o_, in0=o_, in1=maskd, op=ALU.add))
                        if mk:
                            wait("scalar", "sV", mk)
                        else:
                            wait("scalar", "sT", lastS)
                        if pt_rdT[k % 2]:
                            wait("scalar", "sT", pt_rdT[k % 2])
                        ex = inc("scalar", "sA",
                                 lambda e, d=pt[k % 2], s=pb[sb_][:, 0:256]:
                                 e.activation(out=d, in_=s, func=AF.Exp,
                                              scale=1.0 / 32.0, bias=sca[:, 0:1]))
                        sc_exA[k % 2] = ex
                        wait("tensor", "sA", ex)
                        if k == 0 and zr_evV:
                            wait("tensor", "sV", zr_evV)
                        emit("tensor",
                             lambda e, o_=pb[6][0:1, toff:toff + 256], r=pt[k % 2],
                             st=(k == 0), sp=(k == nk - 1):
                             e.matmul(o_, ones_c, r, start=st, stop=sp))
                        if cc == 0 and c2 == 0 and k == 0:
                            wait("tensor", "sV", v_done_V)
                        for sub in range(2):
                            gsub = 2 * c + sub
                            if k > gsub:
                                continue
                            for dh in range(2):
                                gg = sub * 2 + dh
                                fn = (lambda e, o_=pb[gg][:, 0:512],
                                      l=pt[k % 2][:, sub * 128:(sub + 1) * 128],
                                      r=v_s[:, k * 1024 + dh * 512: k * 1024 + (dh + 1) * 512],
                                      st=(k == 0), sp=(k == 2 * c + sub):
                                      e.matmul(o_, l, r, start=st, stop=sp))
                                if sub == 1 and dh == 1:
                                    pt_rdT[k % 2] = inc("tensor", "sT", fn)
                                else:
                                    emit("tensor", fn)
                        last_pv_mm = pt_rdT[k % 2]
                    for sub in range(2):
                        for dh in range(2):
                            gg = sub * 2 + dh
                            wait("vector", "sT", last_pv_mm)
                            out_ev[gg] = inc("vector", "sV",
                                             lambda e,
                                             d=y[:, (2 * c + sub) * 1024 + dh * 512:
                                                 (2 * c + sub) * 1024 + (dh + 1) * 512],
                                             s=pb[gg][:, 0:512]:
                                             e.tensor_copy(out=d, in_=s))
                    wait("vector", "sT", last_pv_mm)
                    zr_evV = inc("vector", "sV",
                                 lambda e, d=zacc[0:1, c:c + 1],
                                 s=pb[6][0:1, toff:toff + 256]:
                                 e.reduce_sum(d, s, axis=AX.X))

            wait("vector", "sV", zr_evV)
            wv_ = inc("vector", "sV",
                      lambda e: e.reduce_sum(s1v[0:1, 5:6], zacc[0:1, 0:8], axis=AX.X))
            wait("vector", "sV", wv_)
            wv_ = inc("vector", "sV",
                      lambda e: e.reciprocal(out=s1v[0:1, 6:7], in_=s1v[0:1, 5:6]))
            wait("tensor", "sV", wv_)
            wait("tensor", "sA", sc_exA[0])
            wait("tensor", "sA", sc_exA[1])
            bc2 = inc("tensor", "sT",
                      lambda e: e.matmul(pb[4][:, 0:1], ones_r, s1v[0:1, 6:7],
                                         start=True, stop=True))
            wait("vector", "sT", bc2)
            zinv_rdy = inc("vector", "sV",
                           lambda e: e.tensor_copy(out=sca[:, 1:2], in_=pb[4][:, 0:1]))

            def load_phase_w(e_, h2_, wait_T_gu, wait_T_d):
                if wait_T_gu:
                    wait("gpsimd", "sT", wait_T_gu)
                inc("gpsimd", "sGU",
                    lambda e, ee=e_, hh=h2_: e.dma_start(
                        out=gbuf.rearrange("p (c h) -> p c h", c=DC),
                        in_=gw_h[ee].rearrange("(c p) h -> p c h", p=128)[:, :, hh * HH:(hh + 1) * HH]), 16)
                n1 = inc("gpsimd", "sGU",
                         lambda e, ee=e_, hh=h2_: e.dma_start(
                             out=ubuf.rearrange("p (c h) -> p c h", c=DC),
                             in_=uw_h[ee].rearrange("(c p) h -> p c h", p=128)[:, :, hh * HH:(hh + 1) * HH]), 16)
                if wait_T_d:
                    wait("gpsimd", "sT", wait_T_d)
                n2 = inc("gpsimd", "sWD",
                         lambda e, ee=e_, hh=h2_: e.dma_start(
                             out=dbuf.rearrange("p (c d) -> p c d", c=DC),
                             in_=dw_h[ee][hh * HH:(hh + 1) * HH, :].rearrange("(c p) d -> p c d", p=128)), 16)
                return n1, n2

            wait("gpsimd", "sT", last_pv_mm)
            ph0_half = {}
            for idx, (snm, th, lo) in enumerate((("sQ0", gw_h, 0), ("sQ2", uw_h, 0),
                                                 ("sQ1", gw_h, 4), ("sQ3", uw_h, 4))):
                dst = gbuf if th is gw_h else ubuf
                ph0_half[snm] = inc("gpsimd", snm,
                                    lambda e, d=dst, t=th, l_=lo: e.dma_start(
                                        out=d.rearrange("p (c h) -> p c h", c=DC)[:, l_:l_ + 4, :],
                                        in_=t[0].rearrange("(c p) h -> p c h", p=128)[:, l_:l_ + 4, 0:HH]), 16)
            gu_ld = 0
            n2 = inc("gpsimd", "sWD",
                     lambda e: e.dma_start(
                         out=dbuf.rearrange("p (c d) -> p c d", c=DC),
                         in_=dw_h[0][0:HH, :].rearrange("(c p) d -> p c d", p=128)), 16)
            d_ld = n2

            resc_j = [0] * NT
            resc_nx = [0] * NT

            def emit_resc_dma(j):
                p = j % 5
                srp = "sR%d" % p
                if j < 5:
                    wait("sync", "sT", last_pv_mm)
                else:
                    wait("sync", "sV", resc_j[j - 5])
                resc_nx[j] = inc("sync", srp,
                                 lambda e, d=xr[p], s=x_hbm[B0 + j * 128:B0 + (j + 1) * 128, :]:
                                 e.dma_start(out=d, in_=s), 16)

            def emit_resc_stt(j):
                p = j % 5
                wait("vector", "sR%d" % p, resc_nx[j])
                resc_j[j] = inc("vector", "sV",
                                lambda e, d=y[:, j * 1024:(j + 1) * 1024], x_=xr[p]:
                                e.scalar_tensor_tensor(out=d, in0=d, scalar=sca[:, 1:2],
                                                       in1=x_, op0=ALU.mult, op1=ALU.add))
                if j + 5 < NT:
                    emit_resc_dma(j + 5)

            for j in range(5):
                emit_resc_dma(j)

            if dbg and b == 0:
                wait("sync", "sV", zinv_rdy)
                dbg_sX = inc("sync", "sC", lambda e: e.dma_start(out=dbg_s, in_=s1v), 16)


            if dbg and b == 0:
                wait("gpsimd", "sC", dbg_sX)
                wait("tensor", "sC", dbg_sX)

            gate_bank_A = [0, 0]
            up_bank_V = [0, 0]
            silu_war_V = [0, 0]
            dn_ev_V = [0, 0, 0, 0]
            ffn_last_mm = 0
            ffn_hid_last = 0
            first_ffn_mm = True
            fin_gate = [0] * NT
            for ph in range(16):
                e_, h2_ = ph // 2, ph % 2
                hid_last_V = 0
                last_gu_mm = 0
                resc_todo = list(range(NT)) if ph == 0 else []
                for tc4 in range(4):
                    for ht in range(NHT):
                        pgi = ht % 2
                        first_grp = (ph == 0 and tc4 == 0 and ht == 0)
                        if ph == 0 and not first_grp and tc4 == 0 and ht == 1:
                            wait("tensor", "sQ1", ph0_half["sQ1"])
                            wait("tensor", "sQ3", ph0_half["sQ3"])
                        if ph > 0:
                            wait("tensor", "sGU", gu_ld)
                        if first_ffn_mm:
                            wait("tensor", "sV", zinv_rdy)
                            first_ffn_mm = False
                        if gate_bank_A[pgi]:
                            wait("tensor", "sA", gate_bank_A[pgi])
                        mg = 0
                        for kc in range(DC):
                            if first_grp and kc == 0:
                                wait("tensor", "sQ0", ph0_half["sQ0"])
                            if first_grp and kc == 4:
                                wait("tensor", "sQ1", ph0_half["sQ1"])
                            fn = (lambda e, o_=pb[pgi][:, 0:512],
                                  l=gbuf[:, kc * 1024 + ht * 128: kc * 1024 + (ht + 1) * 128],
                                  r=xnT[:, kc * 2048 + tc4 * 512: kc * 2048 + (tc4 + 1) * 512],
                                  st=(kc == 0), sp=(kc == DC - 1):
                                  e.matmul(o_, l, r, start=st, stop=sp))
                            if kc == DC - 1:
                                mg = inc("tensor", "sT", fn)
                            else:
                                emit("tensor", fn)
                        if up_bank_V[pgi]:
                            wait("tensor", "sV", up_bank_V[pgi])
                        mu = 0
                        for kc in range(DC):
                            if first_grp and kc == 0:
                                wait("tensor", "sQ2", ph0_half["sQ2"])
                            if first_grp and kc == 4:
                                wait("tensor", "sQ3", ph0_half["sQ3"])
                            fn = (lambda e, o_=pb[2 + pgi][:, 0:512],
                                  l=ubuf[:, kc * 1024 + ht * 128: kc * 1024 + (ht + 1) * 128],
                                  r=xnT[:, kc * 2048 + tc4 * 512: kc * 2048 + (tc4 + 1) * 512],
                                  st=(kc == 0), sp=(kc == DC - 1):
                                  e.matmul(o_, l, r, start=st, stop=sp))
                            if kc == DC - 1:
                                mu = inc("tensor", "sT", fn)
                            else:
                                emit("tensor", fn)
                        last_gu_mm = mu
                        wait("scalar", "sT", mg)
                        if silu_war_V[pgi]:
                            wait("scalar", "sV", silu_war_V[pgi])
                        asl = inc("scalar", "sA",
                                  lambda e, d=silu[pgi], s=pb[pgi][:, 0:512]:
                                  e.activation(out=d, in_=s, func=AF.Silu))
                        gate_bank_A[pgi] = asl
                        wait("vector", "sA", asl)
                        wait("vector", "sT", mu)
                        hv = inc("vector", "sV",
                                 lambda e,
                                 d=hid[:, ht * 2048 + tc4 * 512: ht * 2048 + (tc4 + 1) * 512],
                                 s=silu[pgi], u=pb[2 + pgi][:, 0:512]:
                                 e.tensor_tensor(out=d, in0=s, in1=u, op=ALU.mult))
                        up_bank_V[pgi] = hv
                        silu_war_V[pgi] = hv
                        hid_last_V = hv
                        if resc_todo:
                            emit_resc_stt(resc_todo.pop(0))
                last_dn_mm = 0
                for tc8 in range(8):
                    for sub in range(2):
                        g4 = tc8 * 2 + sub
                        for dh in range(2):
                            gg = 4 + sub * 2 + dh
                            wait("tensor", "sWD", d_ld)
                            wait("tensor", "sV", hid_last_V)
                            if dn_ev_V[sub * 2 + dh]:
                                wait("tensor", "sV", dn_ev_V[sub * 2 + dh])
                            md = 0
                            for ht in range(NHT):
                                fn = (lambda e, o_=pb[gg][:, 0:512],
                                      l=hid[:, ht * 2048 + tc8 * 256 + sub * 128:
                                            ht * 2048 + tc8 * 256 + (sub + 1) * 128],
                                      r=dbuf[:, ht * 1024 + dh * 512: ht * 1024 + (dh + 1) * 512],
                                      st=(ht == 0), sp=(ht == NHT - 1):
                                      e.matmul(o_, l, r, start=st, stop=sp))
                                if ht == NHT - 1:
                                    md = inc("tensor", "sT", fn)
                                else:
                                    emit("tensor", fn)
                            last_dn_mm = md
                            wait("vector", "sT", md)
                            dn_ev_V[sub * 2 + dh] = inc(
                                "vector", "sV",
                                lambda e,
                                d=y[:, g4 * 1024 + dh * 512: g4 * 1024 + (dh + 1) * 512],
                                s=pb[gg][:, 0:512],
                                cs=c_all[:, g4 * 8 + e_: g4 * 8 + e_ + 1]:
                                e.scalar_tensor_tensor(out=d, in0=s, scalar=cs, in1=d,
                                                       op0=ALU.mult, op1=ALU.add))
                            if ph == 15 and dh == 1:
                                fin_gate[g4] = dn_ev_V[sub * 2 + dh]
                if ph < 15:
                    gu_ld, d_ld = load_phase_w((ph + 1) // 2, (ph + 1) % 2,
                                               last_gu_mm, last_dn_mm)
                ffn_last_mm = last_dn_mm
                if ph == 15:
                    ffn_hid_last = hid_last_V

            fin_st = [0, 0]
            for i in range(NT):
                p = i % 2
                wait("gpsimd", "sV", fin_gate[i])
                fin_st[p] = inc("gpsimd", "sO%d" % p,
                                lambda e, d=out_hbm[B0 + i * 128:B0 + (i + 1) * 128, :],
                                s=y[:, i * 1024:(i + 1) * 1024]: e.dma_start(out=d, in_=s), 16)

            resc_last = resc_j[NT - 1]
            end_prev = {"sT": ffn_last_mm, "sA": cnt["sA"], "sV": cnt["sV"],
                        "sO0": cnt["sO0"], "sO1": cnt["sO1"], "hid": ffn_hid_last}

        wait("sync", "sO0", cnt["sO0"])
        wait("sync", "sO1", cnt["sO1"])

        with nc.Block() as block:
            @block.sync
            def _(e):
                for f in ops["sync"]:
                    f(e)

            @block.gpsimd
            def _(e):
                for f in ops["gpsimd"]:
                    f(e)

            @block.tensor
            def _(e):
                for f in ops["tensor"]:
                    f(e)

            @block.scalar
            def _(e):
                for f in ops["scalar"]:
                    f(e)

            @block.vector
            def _(e):
                for f in ops["vector"]:
                    f(e)

    return nc


def make_consts():
    s = np.arange(128)[:, None]
    t = np.arange(128)[None, :]
    return {
        "ident": np.eye(128, dtype=ml_dtypes.bfloat16),
        "ident32": np.eye(128, dtype=np.float32),
        "maskd": np.ascontiguousarray(np.where(s <= t, 0.0, NEG).astype(np.float32)),
        "ones_c": np.ones((128, 1), dtype=ml_dtypes.bfloat16),
        "ones_r": np.ones((1, 128), dtype=np.float32),
        "eps": np.full((128, 1), 1e-5, dtype=np.float32),
    }


def make_in_maps(inputs, consts):
    x = np.asarray(inputs["L0_attn_x"], dtype=np.float32)
    shared = {
        "wq": np.ascontiguousarray(np.asarray(inputs["L0_attn_Wq"], np.float32)),
        "wk": np.ascontiguousarray(np.asarray(inputs["L0_attn_Wk"], np.float32)),
        "wv": np.ascontiguousarray(np.asarray(inputs["L0_attn_Wv"], np.float32)),
        "bqT": np.ascontiguousarray(np.asarray(inputs["L0_attn_bq"], np.float32).reshape(DC, 128).T),
        "bkT": np.ascontiguousarray(np.asarray(inputs["L0_attn_bk"], np.float32).reshape(DC, 128).T),
        "bvb": np.ascontiguousarray(np.broadcast_to(np.asarray(inputs["L0_attn_bv"], np.float32), (128, D))),
        "rw": np.ascontiguousarray(np.asarray(inputs["L0_ffn_router_w"], np.float32)),
        "gw": np.ascontiguousarray(np.asarray(inputs["L0_ffn_gate_w"], np.float32)),
        "uw": np.ascontiguousarray(np.asarray(inputs["L0_ffn_up_w"], np.float32)),
        "dw": np.ascontiguousarray(np.asarray(inputs["L0_ffn_down_w"], np.float32)),
    }
    shared.update(consts)
    in_maps = []
    for i in range(NCORES):
        m = dict(shared)
        m["x"] = np.ascontiguousarray(x[2 * i:2 * i + 2].reshape(TB * T, D))
        in_maps.append(m)
    return in_maps


def kernel(**inputs):
    nc = build_program(dbg=False)
    in_maps = make_in_maps(inputs, make_consts())
    res = run_bass_kernel_spmd(nc, in_maps, core_ids=list(range(NCORES)))
    outs = [res.results[i]["out"].reshape(TB, T, D) for i in range(NCORES)]
    return np.concatenate(outs, axis=0).astype(np.float32)


# version 08
# speedup vs baseline: 1.1344x; 1.1344x over previous
import numpy as np
import ml_dtypes
import concourse.bass as bass
import concourse.mybir as mybir
from concourse.bass_utils import run_bass_kernel_spmd
from concourse.bass_core_rust import IndirectOffsetOnAxis
from contextlib import ExitStack

F32 = mybir.dt.float32
BF16 = mybir.dt.bfloat16
I32 = mybir.dt.int32
AF = mybir.ActivationFunctionType
ALU = mybir.AluOpType
AX = mybir.AxisListType

NCORES = 8
TB = 2
T = 2048
D = 1024
DC = 8
NT = 16
NE = 8
HH = 1024
NHT = 8
NEG = -3.0e38
ENGS = ("sync", "gpsimd", "tensor", "scalar", "vector")


def build_program(dbg=False):
    nc = bass.Bass("TRN2", target_bir_lowering=False)

    x_hbm = nc.dram_tensor("x", [TB * T, D], F32, kind="ExternalInput").ap()
    wq_h = nc.dram_tensor("wq", [D, D], F32, kind="ExternalInput").ap()
    wk_h = nc.dram_tensor("wk", [D, D], F32, kind="ExternalInput").ap()
    wv_h = nc.dram_tensor("wv", [D, D], F32, kind="ExternalInput").ap()
    bqT_h = nc.dram_tensor("bqT", [128, DC], F32, kind="ExternalInput").ap()
    bkT_h = nc.dram_tensor("bkT", [128, DC], F32, kind="ExternalInput").ap()
    bvb_h = nc.dram_tensor("bvb", [128, D], F32, kind="ExternalInput").ap()
    rw_h = nc.dram_tensor("rw", [D, NE], F32, kind="ExternalInput").ap()
    gw_h = nc.dram_tensor("gw", [NE, D, 2 * HH], F32, kind="ExternalInput").ap()
    uw_h = nc.dram_tensor("uw", [NE, D, 2 * HH], F32, kind="ExternalInput").ap()
    dw_h = nc.dram_tensor("dw", [NE, 2 * HH, D], F32, kind="ExternalInput").ap()
    ident_h = nc.dram_tensor("ident", [128, 128], BF16, kind="ExternalInput").ap()
    ident32_h = nc.dram_tensor("ident32", [128, 128], F32, kind="ExternalInput").ap()
    maskd_h = nc.dram_tensor("maskd", [128, 128], F32, kind="ExternalInput").ap()
    lt_h = nc.dram_tensor("lt", [128, 128], BF16, kind="ExternalInput").ap()
    ones_c_h = nc.dram_tensor("ones_c", [128, 1], BF16, kind="ExternalInput").ap()
    ones_r_h = nc.dram_tensor("ones_r", [1, 128], F32, kind="ExternalInput").ap()
    eps_h = nc.dram_tensor("eps", [128, 1], F32, kind="ExternalInput").ap()
    zsrc_h = nc.dram_tensor("zsrc", [T, D], F32, kind="ExternalInput").ap()
    tidc_h = nc.dram_tensor("tidc", [128, 16], F32, kind="ExternalInput").ap()
    out_hbm = nc.dram_tensor("out", [TB * T, D], F32, kind="ExternalOutput").ap()
    xnd = [nc.dram_tensor("xnd%d" % i, [T, D], BF16, kind="Internal").ap()
           for i in range(2)]
    y12 = nc.dram_tensor("y12d", [T, D], F32, kind="Internal").ap()
    if dbg:
        dbg_xnT = nc.dram_tensor("dbg_xnT", [128, DC * T], BF16, kind="ExternalOutput").ap()
        dbg_yat = nc.dram_tensor("dbg_yat", [128, NT * D], F32, kind="ExternalOutput").ap()
        dbg_c = nc.dram_tensor("dbg_c", [128, NT * NE], F32, kind="ExternalOutput").ap()
        dbg_s = nc.dram_tensor("dbg_s", [1, 8], F32, kind="ExternalOutput").ap()
        dbg_idx = nc.dram_tensor("dbg_idx", [128, 40], F32, kind="ExternalOutput").ap()
        dbg_cg = nc.dram_tensor("dbg_cg", [128, 40], F32, kind="ExternalOutput").ap()
        dbg_y12 = nc.dram_tensor("dbg_y12", [T, D], F32, kind="ExternalOutput").ap()
        dbg_m = nc.dram_tensor("dbg_m", [128, 1280], F32, kind="ExternalOutput").ap()

    stack = ExitStack()
    with stack:
        xnT = stack.enter_context(nc.sbuf_tensor("xnT", [128, DC * T], BF16)).ap()
        y = stack.enter_context(nc.sbuf_tensor("y", [128, NT * D], F32)).ap()
        scr_t = stack.enter_context(nc.sbuf_tensor("scr", [128, 53248], BF16))
        scr = scr_t.ap()
        scrf = scr_t.ap().bitcast(F32)
        xnb_t = stack.enter_context(nc.sbuf_tensor("xnb_t", [128, 2048], BF16)).ap()

        identity = stack.enter_context(nc.sbuf_tensor("identity_s", [128, 128], BF16)).ap()
        ident32 = stack.enter_context(nc.sbuf_tensor("ident32_s", [128, 128], F32)).ap()
        maskd = stack.enter_context(nc.sbuf_tensor("maskd_s", [128, 128], F32)).ap()
        ones_c = stack.enter_context(nc.sbuf_tensor("ones_c_s", [128, 1], BF16)).ap()
        ones_r = stack.enter_context(nc.sbuf_tensor("ones_r_s", [1, 128], F32)).ap()
        eps_t = stack.enter_context(nc.sbuf_tensor("eps_t", [128, 1], F32)).ap()
        bqT = stack.enter_context(nc.sbuf_tensor("bqT_s", [128, DC], F32)).ap()
        bkT = stack.enter_context(nc.sbuf_tensor("bkT_s", [128, DC], F32)).ap()
        rw_s = stack.enter_context(nc.sbuf_tensor("rw_s", [128, DC * NE], F32)).ap()
        c_all = stack.enter_context(nc.sbuf_tensor("c_all", [128, NT * NE], F32)).ap()
        lg_all = stack.enter_context(nc.sbuf_tensor("lg_all", [128, NT * NE], F32)).ap()
        l2_all = stack.enter_context(nc.sbuf_tensor("l2_all", [128, NT * NE], F32)).ap()
        rts = stack.enter_context(nc.sbuf_tensor("rts", [128, 48], F32)).ap()
        nacc = rts[0:1, 0:16]
        zacc = rts[0:1, 16:32]
        s1v = rts[0:1, 32:40]
        sca = rts[:, 40:42]

        stats = l2_all[:, 0:32]

        pb = [stack.enter_context(nc.psum_tensor("pb%d" % i, [128, 512], F32)).ap()
              for i in range(8)]
        pbh = [p.bitcast(BF16) for p in pb]

        kT = scr[:, 0:16384]
        v_s = scr[:, 16384:32768]
        w0 = scr[:, 32768:40960]
        w1 = scr[:, 40960:49152]
        qT = scr[:, 49152:53248]
        pt = [scr[:, 40960:41216], scr[:, 41216:41472]]
        sqb = [scr[:, 41472:41984], scr[:, 41984:42496]]
        xst = [scrf[:, 24576:25600], scrf[:, 25600:26624]]
        xr = [scrf[:, 24576:25600], scrf[:, 25600:26624],
              scrf[:, 21504:22528], scrf[:, 22528:23552], scrf[:, 23552:24576]]
        bvb_v = scrf[:, 24576:25600]
        xnb = [xnb_t[:, 0:1024], xnb_t[:, 1024:2048]]
        scri = scr_t.ap().bitcast(I32)
        CAP = 640
        NSL = 5
        gbuf = scr[:, 0:8192]
        ubuf = scr[:, 8192:16384]
        dbuf = scr[:, 16384:24576]
        hid = scr[:, 24576:29696]
        oh_s = scrf[:, 12288:13312]
        xeT = scr[:, 29696:34816]
        silu = [scrf[:, 17408:17920], scrf[:, 17920:18432]]
        xe_tm = [scr[:, 36864:37888], scr[:, 37888:38912]]
        idxf = scrf[:, 19456:19496]
        cg_s = scrf[:, 19496:19536]
        idxi = scri[:, 19536:19576]
        acc3 = scrf[:, 19576:19579]
        rhs48 = scrf[:, 19584:19632]
        pos_s = scrf[:, 19632:19648]
        m_s = scrf[:, 19648:19664]
        sc_s = scrf[:, 19664:19680]
        b_s = scrf[:, 19680:19681]
        rt_b = scr[:, 39424:39425]
        tidc = scrf[:, 19696:19712]
        stage = [scrf[:, 20480:21504], scrf[:, 21504:22528]]
        iota_f = scrf[:, 20480:21120]
        iota_i = scri[:, 21504:22144]
        lt_s = scr[:, 44288:44416]
        xn32 = scrf[:, 22528:23552]
        xnT32 = scrf[:, 23552:24576]

        ops = {e: [] for e in ENGS}
        sems = {}
        cnt = {}
        for nm in ("sC", "sX0", "sX1", "sW0", "sW1", "sGU", "sWD",
                   "sO0", "sO1", "sT", "sA", "sV", "sP",
                   "sR0", "sR1",
                   "sQ0", "sQ1", "sQ2", "sQ3",
                   "sXN0", "sXN1", "sZ", "sSC", "sGA0", "sGA1"):
            sems[nm] = stack.enter_context(nc.semaphore(nm))
            cnt[nm] = 0

        def add(e, fn):
            ops[e].append(fn)

        def wait(e, snm, val):
            if val > 0:
                add(e, lambda eng, s=sems[snm], v=val: eng.wait_ge(s, v))

        def inc(e, snm, fn, k=1):
            cnt[snm] += k
            add(e, lambda eng, f=fn, s=sems[snm], kk=k: f(eng).then_inc(s, kk))
            return cnt[snm]

        def emit(e, fn):
            add(e, fn)

        for dst, csrc in ((identity, ident_h), (ident32, ident32_h), (maskd, maskd_h),
                          (ones_c, ones_c_h), (ones_r, ones_r_h), (eps_t, eps_h),
                          (bqT, bqT_h), (bkT, bkT_h)):
            inc("sync", "sC", lambda e, d=dst, s=csrc: e.dma_start(out=d, in_=s), 16)
        inc("sync", "sC",
            lambda e: e.dma_start(out=rw_s.rearrange("p (c e) -> p c e", c=DC),
                                  in_=rw_h.rearrange("(c p) e -> p c e", p=128)), 16)
        n_const = cnt["sC"]
        wait("scalar", "sC", n_const)
        wait("vector", "sC", n_const)
        wait("tensor", "sC", n_const)

        end_prev = {"sT": 0, "sA": 0, "sV": 0, "sO0": 0, "sO1": 0, "hid": 0,
                    "sSC": 0}
        resc_last = 0

        for b in range(TB):
            B0 = b * T

            xcast = [0] * NT
            xn_st = [0] * NT
            xnT_cp = [0] * NT
            tr_done = [0] * NT
            f32tr_done = [0] * NT
            xn32cp = [0] * NT
            rt_mm = [0] * NT
            rt_ev = [0] * NT
            for i in range(NT):
                p = i % 2
                o = p * 16
                MEAN, VAR, STD, RSTD, NB = o + 12, o + 13, o + 14, o + 15, o + 0
                if i >= 2:
                    wait("sync", "sA", xcast[i - 2])
                elif b > 0:
                    wait("sync", "sV", resc_last)
                sxp = "sX%d" % p
                nx = inc("sync", sxp,
                         lambda e, d=xst[p], s=x_hbm[B0 + i * 128:B0 + (i + 1) * 128, :]:
                         e.dma_start(out=d, in_=s), 16)
                wait("vector", sxp, nx)
                emit("vector", lambda e, d=stats[:, o:o + 6], s=xst[p][:, 0:512]:
                     e.bn_stats(d, s))
                wbn = inc("vector", "sV",
                          lambda e, d=stats[:, o + 6:o + 12], s=xst[p][:, 512:1024]:
                          e.bn_stats(d, s))
                wait("vector", "sV", wbn)
                wag = inc("vector", "sV",
                          lambda e, d=stats[:, MEAN:MEAN + 2], s=stats[:, o:o + 12]:
                          e.bn_aggr(d, s))
                wait("scalar", "sV", wag)
                wstd = inc("scalar", "sA",
                           lambda e, d=stats[:, STD:STD + 1], s=stats[:, VAR:VAR + 1]:
                           e.activation(out=d, in_=s, func=AF.Sqrt, bias=eps_t))
                wait("vector", "sA", wstd)
                wrs = inc("vector", "sV",
                          lambda e, d=stats[:, RSTD:RSTD + 1], s=stats[:, STD:STD + 1]:
                          e.reciprocal(out=d, in_=s))
                wait("vector", "sV", wrs)
                wnb = inc("vector", "sV",
                          lambda e, d=stats[:, NB:NB + 1], s=stats[:, MEAN:MEAN + 1],
                          r=stats[:, RSTD:RSTD + 1]:
                          e.scalar_tensor_tensor(out=d, in0=s, scalar=-1.0, in1=r,
                                                 op0=ALU.mult, op1=ALU.mult))
                wait("scalar", "sV", wnb)
                if i >= 1:
                    wait("scalar", "sT", f32tr_done[i - 1])
                a32 = inc("scalar", "sA",
                          lambda e, d=xn32, s=xst[p], sc=stats[:, RSTD:RSTD + 1],
                          bi=stats[:, NB:NB + 1]:
                          e.activation(out=d, in_=s, func=AF.Identity, scale=sc, bias=bi))
                wait("scalar", "sA", a32)
                if i >= 2:
                    wait("scalar", "sT", tr_done[i - 2])
                    wait("scalar", "sXN%d" % p, xn_st[i - 2])
                xcast[i] = inc("scalar", "sA",
                               lambda e, d=xnb[p], s=xn32:
                               e.activation(out=d, in_=s, func=AF.Copy))
                wait("sync", "sA", xcast[i])
                xn_st[i] = inc("sync", "sXN%d" % p,
                               lambda e, d=xnd[b % 2][i * 128:(i + 1) * 128, :],
                               s=xnb[p]: e.dma_start(out=d, in_=s), 16)
                wait("tensor", "sA", xcast[i])
                if i < 2:
                    if b > 0:
                        wait("tensor", "sA", end_prev["sA"])
                else:
                    wait("tensor", "sV", xnT_cp[i - 2])
                for dc in range(DC):
                    fn = (lambda e, d=pbh[p][:, dc * 128:(dc + 1) * 128],
                          s=xnb[p][:, dc * 128:(dc + 1) * 128]:
                          e.transpose(d, s, identity))
                    if dc == DC - 1:
                        tr_done[i] = inc("tensor", "sT", fn)
                    else:
                        emit("tensor", fn)
                if i > 0:
                    wait("tensor", "sV", xn32cp[i - 1])
                elif b > 0:
                    wait("tensor", "sV", end_prev["hid"])
                for dc in range(DC):
                    fn = (lambda e,
                          d=pb[2 + dc // 4][:, (dc % 4) * 128:((dc % 4) + 1) * 128],
                          s=xn32[:, dc * 128:(dc + 1) * 128]:
                          e.transpose(d, s, ident32))
                    if dc == DC - 1:
                        f32tr_done[i] = inc("tensor", "sT", fn)
                    else:
                        emit("tensor", fn)
                wait("vector", "sT", tr_done[i])
                xnT_cp[i] = inc("vector", "sV",
                                lambda e,
                                d=xnT.rearrange("p (c t) -> p c t", c=DC)[:, :, i * 128:(i + 1) * 128],
                                s=pbh[p].rearrange("p (c t) -> p c t", c=DC):
                                e.tensor_copy(out=d, in_=s))
                wait("vector", "sT", f32tr_done[i])
                if i > 0:
                    wait("vector", "sT", rt_mm[i - 1])
                emit("vector",
                     lambda e: e.tensor_copy(out=xnT32[:, 0:512], in_=pb[2][:, 0:512]))
                xn32cp[i] = inc("vector", "sV",
                                lambda e: e.tensor_copy(out=xnT32[:, 512:1024],
                                                        in_=pb[3][:, 0:512]))
                lb = 4 + (i % 2)
                wait("tensor", "sV", xn32cp[i])
                if i > 1:
                    wait("tensor", "sA", rt_ev[i - 2])
                elif b > 0:
                    wait("tensor", "sV", end_prev["sV"])
                for kc in range(DC):
                    fn = (lambda e, o_=pb[lb][:, 0:8],
                          l=xnT32[:, kc * 128:(kc + 1) * 128],
                          r=rw_s[:, kc * 8:(kc + 1) * 8],
                          st=(kc == 0), sp=(kc == DC - 1):
                          e.matmul(o_, l, r, start=st, stop=sp))
                    if kc == DC - 1:
                        rt_mm[i] = inc("tensor", "sT", fn)
                    else:
                        emit("tensor", fn)
                if i > 0:
                    wait("scalar", "sT", rt_mm[i - 1])
                    rt_ev[i - 1] = inc("scalar", "sA",
                                       lambda e, d=lg_all[:, (i - 1) * 8:i * 8],
                                       s=pb[4 + ((i - 1) % 2)][:, 0:8]:
                                       e.activation(out=d, in_=s, func=AF.Copy))
                if i == NT - 1:
                    wait("scalar", "sT", rt_mm[i])
                    rt_ev[i] = inc("scalar", "sA",
                                   lambda e, d=lg_all[:, i * 8:(i + 1) * 8],
                                   s=pb[lb][:, 0:8]:
                                   e.activation(out=d, in_=s, func=AF.Copy))

            lg3 = lg_all.rearrange("p (t e) -> p t e", t=NT)
            l23 = l2_all.rearrange("p (t e) -> p t e", t=NT)
            c3 = c_all.rearrange("p (t e) -> p t e", t=NT)
            def rbc(col):
                return rts[:, col:col + 16].rearrange("p (t o) -> p t o", o=1) \
                          .broadcast_to([128, NT, NE])
            wait("vector", "sA", rt_ev[NT - 1])
            wv_ = inc("vector", "sV",
                      lambda e: e.tensor_reduce(rts[:, 0:16], lg3, axis=AX.X, op=ALU.max))
            wait("vector", "sV", wv_)
            wv_ = inc("vector", "sV",
                      lambda e: e.tensor_tensor(out=c3, in0=lg3, in1=rbc(0), op=ALU.is_equal))
            wait("vector", "sV", wv_)
            wv_ = inc("vector", "sV",
                      lambda e: e.scalar_tensor_tensor(out=l23, in0=c3, scalar=NEG,
                                                       in1=lg3, op0=ALU.mult, op1=ALU.add))
            wait("vector", "sV", wv_)
            wv_ = inc("vector", "sV",
                      lambda e: e.tensor_reduce(rts[:, 16:32], l23, axis=AX.X, op=ALU.max))
            wait("vector", "sV", wv_)
            wv_ = inc("vector", "sV",
                      lambda e: e.tensor_sub(rts[:, 32:48], rts[:, 16:32], rts[:, 0:16]))
            wait("scalar", "sV", wv_)
            wa_ = inc("scalar", "sA",
                      lambda e: e.activation(out=rts[:, 0:16], in_=rts[:, 32:48],
                                             func=AF.Sigmoid, scale=-1.0))
            wait("scalar", "sA", wa_)
            wa_ = inc("scalar", "sA",
                      lambda e: e.activation(out=rts[:, 32:48], in_=rts[:, 32:48],
                                             func=AF.Sigmoid))
            wait("vector", "sA", wa_)
            wv_ = inc("vector", "sV",
                      lambda e: e.tensor_tensor(out=c3, in0=c3, in1=rbc(0), op=ALU.mult))
            wait("vector", "sV", wv_)
            wv_ = inc("vector", "sV",
                      lambda e: e.tensor_tensor(out=l23, in0=l23, in1=rbc(16), op=ALU.is_equal))
            wait("vector", "sV", wv_)
            wv_ = inc("vector", "sV",
                      lambda e: e.tensor_tensor(out=l23, in0=l23, in1=rbc(32), op=ALU.mult))
            wait("vector", "sV", wv_)
            c_rdy = inc("vector", "sV",
                        lambda e: e.tensor_add(c_all, c_all, l2_all))

            if dbg and b == 0:
                wait("sync", "sV", xnT_cp[NT - 1])
                inc("sync", "sC", lambda e: e.dma_start(out=dbg_xnT, in_=xnT), 16)
                wait("sync", "sV", c_rdy)
                inc("sync", "sC", lambda e: e.dma_start(out=dbg_c, in_=c_all), 16)

            wait("sync", "sA", xcast[NT - 1])
            bvb_ld = inc("sync", "sC",
                         lambda e: e.dma_start(out=bvb_v, in_=bvb_h), 16)

            if b > 0:
                wait("gpsimd", "sT", end_prev["sT"])
                wait("gpsimd", "sSC", end_prev["sSC"])
            wk_ld = inc("gpsimd", "sW0",
                        lambda e: e.dma_start(
                            out=w0.rearrange("p (c d) -> p c d", c=DC),
                            in_=wk_h.rearrange("(c p) d -> p c d", p=128)), 16)
            wait("gpsimd", "sT", rt_mm[NT - 1])
            wait("gpsimd", "sA", xcast[NT - 1])
            wv_ld = inc("gpsimd", "sW1",
                        lambda e: e.dma_start(
                            out=w1.rearrange("p (c d) -> p c d", c=DC),
                            in_=wv_h.rearrange("(c p) d -> p c d", p=128)), 16)

            kT_ev = {}
            g_ev = [0, 0, 0, 0]
            gi = 0
            last_kT_mm = 0
            first_kt = True
            for dt in range(DC):
                for tc4 in range(4):
                    g = gi % 4
                    gi += 1
                    wait("tensor", "sW0", wk_ld)
                    wait("tensor", "sV", xnT_cp[tc4 * 4 + 3])
                    if first_kt:
                        wait("tensor", "sV", xn32cp[NT - 1])
                        first_kt = False
                    if g_ev[g]:
                        wait("tensor", "sA", g_ev[g])
                    for kc in range(DC):
                        fn = (lambda e, o_=pb[g][:, 0:512],
                              l=w0[:, kc * 1024 + dt * 128: kc * 1024 + (dt + 1) * 128],
                              r=xnT[:, kc * 2048 + tc4 * 512: kc * 2048 + (tc4 + 1) * 512],
                              st=(kc == 0), sp=(kc == DC - 1):
                              e.matmul(o_, l, r, start=st, stop=sp))
                        if kc == DC - 1:
                            last_kT_mm = inc("tensor", "sT", fn)
                        else:
                            emit("tensor", fn)
                    wait("scalar", "sT", last_kT_mm)
                    g_ev[g] = inc("scalar", "sA",
                                  lambda e,
                                  d=kT[:, dt * 2048 + tc4 * 512: dt * 2048 + (tc4 + 1) * 512],
                                  s=pb[g][:, 0:512], bi=bkT[:, dt:dt + 1]:
                                  e.activation(out=d, in_=s, func=AF.Identity, bias=bi))
                    kT_ev[(dt, tc4)] = g_ev[g]
            kT_done_A = cnt["sA"]

            wait("gpsimd", "sT", last_kT_mm)
            wq_ld = inc("gpsimd", "sW0",
                        lambda e: e.dma_start(
                            out=w0.rearrange("p (c d) -> p c d", c=DC),
                            in_=wq_h.rearrange("(c p) d -> p c d", p=128)), 16)

            g_evv = [0, 0, 0, 0]
            gi = 0
            last_v_mm = 0
            first_v = True
            for st_ in range(NT):
                for dh in range(2):
                    g = gi % 4
                    gi += 1
                    wait("tensor", "sW1", wv_ld)
                    wait("tensor", "sV", xnT_cp[st_])
                    if first_v:
                        wait("tensor", "sA", kT_done_A)
                    if g_evv[g]:
                        wait("tensor", "sV", g_evv[g])
                    for kc in range(DC):
                        fn = (lambda e, o_=pb[g][:, 0:512],
                              l=xnT[:, kc * 2048 + st_ * 128: kc * 2048 + (st_ + 1) * 128],
                              r=w1[:, kc * 1024 + dh * 512: kc * 1024 + (dh + 1) * 512],
                              st=(kc == 0), sp=(kc == DC - 1):
                              e.matmul(o_, l, r, start=st, stop=sp))
                        if kc == DC - 1:
                            last_v_mm = inc("tensor", "sT", fn)
                        else:
                            emit("tensor", fn)
                    wait("vector", "sT", last_v_mm)
                    if first_v:
                        wait("vector", "sC", bvb_ld)
                        first_v = False
                    g_evv[g] = inc("vector", "sV",
                                   lambda e,
                                   d=v_s[:, st_ * 1024 + dh * 512: st_ * 1024 + (dh + 1) * 512],
                                   s=pb[g][:, 0:512], bi=bvb_v[:, dh * 512:(dh + 1) * 512]:
                                   e.tensor_tensor(out=d, in0=s, in1=bi, op=ALU.add))
            v_done_V = cnt["sV"]

            sq_mmT = 0
            kn_ev = 0
            wait("scalar", "sT", last_v_mm)
            if b > 0:
                wait("scalar", "sSC", end_prev["sSC"])
            for j in range(4):
                for dt in range(DC):
                    jj = dt % 2
                    wait("scalar", "sA", kT_ev[(dt, j)])
                    if sq_mmT:
                        wait("scalar", "sT", sq_mmT)
                    a_k = inc("scalar", "sA",
                              lambda e, d=sqb[jj],
                              s=kT[:, dt * 2048 + j * 512: dt * 2048 + (j + 1) * 512]:
                              e.square(out=d, in_=s))
                    wait("tensor", "sA", a_k)
                    if j == 0 and dt == 0:
                        wait("tensor", "sA", rt_ev[NT - 1])
                    if dt == 0 and kn_ev:
                        wait("tensor", "sV", kn_ev)
                    sq_mmT = inc("tensor", "sT",
                                 lambda e, r=sqb[jj], st=(dt == 0), sp=(dt == DC - 1):
                                 e.matmul(pb[4][0:1, 0:512], ones_c, r, start=st, stop=sp))
                wait("vector", "sT", sq_mmT)
                kn_ev = inc("vector", "sV",
                            lambda e, d=nacc[0:1, 8 + j:9 + j]:
                            e.reduce_max(d, pb[4][0:1, 0:512], axis=AX.X))

            def qT_proj(cc, qb_wA, qb_wV, scalar_pre_V=0):
                evA = 0
                lastT = 0
                first = True
                for dt in range(DC):
                    g = 6 + dt % 2
                    wait("tensor", "sW0", wq_ld)
                    if qb_wA[dt % 2]:
                        wait("tensor", "sA", qb_wA[dt % 2])
                    if qb_wV[dt % 2]:
                        wait("tensor", "sV", qb_wV[dt % 2])
                        qb_wV[dt % 2] = 0
                    for kc in range(DC):
                        fn = (lambda e, o_=pb[g][:, 0:512],
                              l=w0[:, kc * 1024 + dt * 128: kc * 1024 + (dt + 1) * 128],
                              r=xnT[:, kc * 2048 + cc * 512: kc * 2048 + (cc + 1) * 512],
                              st=(kc == 0), sp=(kc == DC - 1):
                              e.matmul(o_, l, r, start=st, stop=sp))
                        if kc == DC - 1:
                            lastT = inc("tensor", "sT", fn)
                        else:
                            emit("tensor", fn)
                    wait("scalar", "sT", lastT)
                    if first and scalar_pre_V:
                        wait("scalar", "sV", scalar_pre_V)
                        first = False
                    evA = inc("scalar", "sA",
                              lambda e, d=qT[:, dt * 512:(dt + 1) * 512],
                              s=pb[g][:, 0:512], bi=bqT[:, dt:dt + 1]:
                              e.activation(out=d, in_=s, func=AF.Identity, bias=bi))
                    qb_wA[dt % 2] = evA
                return evA, lastT

            qb_wA = [0, 0]
            qn_ev = 0
            for cc in range(4):
                q_evA, _ = qT_proj(cc, qb_wA, [0, 0],
                                   scalar_pre_V=(v_done_V if cc == 0 else 0))
                for dt in range(DC):
                    jj = dt % 2
                    wait("scalar", "sA", q_evA)
                    if sq_mmT:
                        wait("scalar", "sT", sq_mmT)
                    a_q = inc("scalar", "sA",
                              lambda e, d=sqb[jj], s=qT[:, dt * 512:(dt + 1) * 512]:
                              e.square(out=d, in_=s))
                    wait("tensor", "sA", a_q)
                    if cc == 0 and dt == 0:
                        wait("tensor", "sV", kn_ev)
                    if dt == 0 and qn_ev:
                        wait("tensor", "sV", qn_ev)
                    sq_mmT = inc("tensor", "sT",
                                 lambda e, r=sqb[jj], st=(dt == 0), sp=(dt == DC - 1):
                                 e.matmul(pb[5][0:1, 0:512], ones_c, r, start=st, stop=sp))
                wait("vector", "sT", sq_mmT)
                qn_ev = inc("vector", "sV",
                            lambda e, d=nacc[0:1, cc:cc + 1]:
                            e.reduce_max(d, pb[5][0:1, 0:512], axis=AX.X))

            wait("vector", "sV", qn_ev)
            wv_ = inc("vector", "sV",
                      lambda e: e.reduce_max(s1v[0:1, 0:1], nacc[0:1, 0:4], axis=AX.X))
            wait("vector", "sV", wv_)
            wv_ = inc("vector", "sV",
                      lambda e: e.reduce_max(s1v[0:1, 1:2], nacc[0:1, 8:12], axis=AX.X))
            wait("scalar", "sV", wv_)
            wa_ = inc("scalar", "sA",
                      lambda e: e.activation(out=s1v[0:1, 2:3], in_=s1v[0:1, 0:1], func=AF.Sqrt))
            wait("scalar", "sA", wa_)
            wa_ = inc("scalar", "sA",
                      lambda e: e.activation(out=s1v[0:1, 3:4], in_=s1v[0:1, 1:2], func=AF.Sqrt))
            wait("vector", "sA", wa_)
            wv_ = inc("vector", "sV",
                      lambda e: e.scalar_tensor_tensor(out=s1v[0:1, 4:5], in0=s1v[0:1, 2:3],
                                                       scalar=-1.0 / 32.0, in1=s1v[0:1, 3:4],
                                                       op0=ALU.mult, op1=ALU.mult))
            wait("tensor", "sV", wv_)
            bc_mm = inc("tensor", "sT",
                        lambda e: e.matmul(pb[5][:, 0:1], ones_r, s1v[0:1, 4:5],
                                           start=True, stop=True))
            wait("vector", "sT", bc_mm)
            negC_rdy = inc("vector", "sV",
                           lambda e: e.tensor_copy(out=sca[:, 0:1], in_=pb[5][:, 0:1]))

            out_ev = [0, 0, 0, 0]
            sc_exA = [0, 0]
            pt_rdT = [0, 0]
            zr_evV = 0
            last_pv_mm = 0
            for cc in range(4):
                qb_wV2 = [zr_evV, zr_evV]
                q_evA, _ = qT_proj(cc, qb_wA, qb_wV2)
                for c2 in range(2):
                    c = 2 * cc + c2
                    toff = c2 * 256
                    nk = 2 * c + 2
                    for k in range(nk):
                        sb_ = 4 + (k % 2)
                        wait("tensor", "sA", q_evA)
                        if sc_exA[k % 2]:
                            wait("tensor", "sA", sc_exA[k % 2])
                        elif k % 2 == 0 and cc == 0 and c2 == 0:
                            wait("tensor", "sV", kn_ev)
                        elif k % 2 == 1 and cc == 0 and c2 == 0:
                            wait("tensor", "sV", negC_rdy)
                        if k == 0:
                            for gg in range(4):
                                if out_ev[gg]:
                                    wait("tensor", "sV", out_ev[gg])
                            if b > 0 and cc == 0 and c2 == 0:
                                wait("tensor", "sO0", end_prev["sO0"])
                                wait("tensor", "sO1", end_prev["sO1"])
                        lastS = 0
                        for dt in range(DC):
                            fn = (lambda e, o_=pb[sb_][:, 0:256],
                                  l=kT[:, dt * 2048 + k * 128: dt * 2048 + (k + 1) * 128],
                                  r=qT[:, dt * 512 + toff: dt * 512 + toff + 256],
                                  st=(dt == 0), sp=(dt == DC - 1):
                                  e.matmul(o_, l, r, start=st, stop=sp))
                            if dt == DC - 1:
                                lastS = inc("tensor", "sT", fn)
                            else:
                                emit("tensor", fn)
                        mk = 0
                        if k == 2 * c:
                            wait("vector", "sT", lastS)
                            mk = inc("vector", "sV",
                                     lambda e, o_=pb[sb_][:, 0:128]:
                                     e.tensor_tensor(out=o_, in0=o_, in1=maskd, op=ALU.add))
                        elif k == 2 * c + 1:
                            wait("vector", "sT", lastS)
                            emit("vector",
                                 lambda e, o_=pb[sb_][:, 0:128]: e.memset(o_, NEG))
                            mk = inc("vector", "sV",
                                     lambda e, o_=pb[sb_][:, 128:256]:
                                     e.tensor_tensor(out=o_, in0=o_, in1=maskd, op=ALU.add))
                        if mk:
                            wait("scalar", "sV", mk)
                        else:
                            wait("scalar", "sT", lastS)
                        if pt_rdT[k % 2]:
                            wait("scalar", "sT", pt_rdT[k % 2])
                        ex = inc("scalar", "sA",
                                 lambda e, d=pt[k % 2], s=pb[sb_][:, 0:256]:
                                 e.activation(out=d, in_=s, func=AF.Exp,
                                              scale=1.0 / 32.0, bias=sca[:, 0:1]))
                        sc_exA[k % 2] = ex
                        wait("tensor", "sA", ex)
                        if k == 0 and zr_evV:
                            wait("tensor", "sV", zr_evV)
                        emit("tensor",
                             lambda e, o_=pb[6][0:1, toff:toff + 256], r=pt[k % 2],
                             st=(k == 0), sp=(k == nk - 1):
                             e.matmul(o_, ones_c, r, start=st, stop=sp))
                        if cc == 0 and c2 == 0 and k == 0:
                            wait("tensor", "sV", v_done_V)
                        for sub in range(2):
                            gsub = 2 * c + sub
                            if k > gsub:
                                continue
                            for dh in range(2):
                                gg = sub * 2 + dh
                                fn = (lambda e, o_=pb[gg][:, 0:512],
                                      l=pt[k % 2][:, sub * 128:(sub + 1) * 128],
                                      r=v_s[:, k * 1024 + dh * 512: k * 1024 + (dh + 1) * 512],
                                      st=(k == 0), sp=(k == 2 * c + sub):
                                      e.matmul(o_, l, r, start=st, stop=sp))
                                if sub == 1 and dh == 1:
                                    pt_rdT[k % 2] = inc("tensor", "sT", fn)
                                else:
                                    emit("tensor", fn)
                        last_pv_mm = pt_rdT[k % 2]
                    for sub in range(2):
                        for dh in range(2):
                            gg = sub * 2 + dh
                            wait("vector", "sT", last_pv_mm)
                            out_ev[gg] = inc("vector", "sV",
                                             lambda e,
                                             d=y[:, (2 * c + sub) * 1024 + dh * 512:
                                                 (2 * c + sub) * 1024 + (dh + 1) * 512],
                                             s=pb[gg][:, 0:512]:
                                             e.tensor_copy(out=d, in_=s))
                    wait("vector", "sT", last_pv_mm)
                    zr_evV = inc("vector", "sV",
                                 lambda e, d=zacc[0:1, c:c + 1],
                                 s=pb[6][0:1, toff:toff + 256]:
                                 e.reduce_sum(d, s, axis=AX.X))

            wait("vector", "sV", zr_evV)
            wv_ = inc("vector", "sV",
                      lambda e: e.reduce_sum(s1v[0:1, 5:6], zacc[0:1, 0:8], axis=AX.X))
            wait("vector", "sV", wv_)
            wv_ = inc("vector", "sV",
                      lambda e: e.reciprocal(out=s1v[0:1, 6:7], in_=s1v[0:1, 5:6]))
            wait("tensor", "sV", wv_)
            wait("tensor", "sA", sc_exA[0])
            wait("tensor", "sA", sc_exA[1])
            bc2 = inc("tensor", "sT",
                      lambda e: e.matmul(pb[4][:, 0:1], ones_r, s1v[0:1, 6:7],
                                         start=True, stop=True))
            wait("vector", "sT", bc2)
            zinv_rdy = inc("vector", "sV",
                           lambda e: e.tensor_copy(out=sca[:, 1:2], in_=pb[4][:, 0:1]))

            def load_phase_w(e_, h2_, wait_T_gu, wait_T_d):
                if wait_T_gu:
                    wait("gpsimd", "sT", wait_T_gu)
                inc("gpsimd", "sGU",
                    lambda e, ee=e_, hh=h2_: e.dma_start(
                        out=gbuf.rearrange("p (c h) -> p c h", c=DC),
                        in_=gw_h[ee].rearrange("(c p) h -> p c h", p=128)[:, :, hh * HH:(hh + 1) * HH]), 16)
                n1 = inc("gpsimd", "sGU",
                         lambda e, ee=e_, hh=h2_: e.dma_start(
                             out=ubuf.rearrange("p (c h) -> p c h", c=DC),
                             in_=uw_h[ee].rearrange("(c p) h -> p c h", p=128)[:, :, hh * HH:(hh + 1) * HH]), 16)
                if wait_T_d:
                    wait("gpsimd", "sT", wait_T_d)
                n2 = inc("gpsimd", "sWD",
                         lambda e, ee=e_, hh=h2_: e.dma_start(
                             out=dbuf.rearrange("p (c d) -> p c d", c=DC),
                             in_=dw_h[ee][hh * HH:(hh + 1) * HH, :].rearrange("(c p) d -> p c d", p=128)), 16)
                return n1, n2

            wait("gpsimd", "sT", last_pv_mm)
            ph0_half = {}
            for snm, th, lo in (("sQ0", gw_h, 0), ("sQ2", uw_h, 0),
                                ("sQ1", gw_h, 4), ("sQ3", uw_h, 4)):
                dst = gbuf if th is gw_h else ubuf
                ph0_half[snm] = inc("gpsimd", snm,
                                    lambda e, d=dst, t=th, l_=lo: e.dma_start(
                                        out=d.rearrange("p (c h) -> p c h", c=DC)[:, l_:l_ + 4, :],
                                        in_=t[0].rearrange("(c p) h -> p c h", p=128)[:, l_:l_ + 4, 0:HH]), 16)
            gu_ld = 0
            d_ld = inc("gpsimd", "sWD",
                       lambda e: e.dma_start(
                           out=dbuf.rearrange("p (c d) -> p c d", c=DC),
                           in_=dw_h[0][0:HH, :].rearrange("(c p) d -> p c d", p=128)), 16)

            resc_j = [0] * NT
            resc_nx = [0] * NT

            def emit_resc_dma(j):
                p = j % 2
                srp = "sR%d" % p
                if j < 2:
                    wait("sync", "sT", last_pv_mm)
                else:
                    wait("sync", "sV", resc_j[j - 2])
                resc_nx[j] = inc("sync", srp,
                                 lambda e, d=xst[p], s=x_hbm[B0 + j * 128:B0 + (j + 1) * 128, :]:
                                 e.dma_start(out=d, in_=s), 16)

            def emit_resc_stt(j):
                p = j % 2
                wait("vector", "sR%d" % p, resc_nx[j])
                resc_j[j] = inc("vector", "sV",
                                lambda e, d=y[:, j * 1024:(j + 1) * 1024], x_=xst[p]:
                                e.scalar_tensor_tensor(out=d, in0=d, scalar=sca[:, 1:2],
                                                       in1=x_, op0=ALU.mult, op1=ALU.add))
                if j + 2 < NT:
                    emit_resc_dma(j + 2)

            for j in range(2):
                emit_resc_dma(j)

            if dbg and b == 0:
                wait("sync", "sV", zinv_rdy)
                dbg_sX = inc("sync", "sC", lambda e: e.dma_start(out=dbg_s, in_=s1v), 16)

            if dbg and b == 0:
                wait("gpsimd", "sC", dbg_sX)
                wait("tensor", "sC", dbg_sX)

            if b > 0:
                wait("sync", "sSC", end_prev["sSC"])
            zr = inc("sync", "sZ", lambda e: e.dma_start(out=y12, in_=zsrc_h), 16)

            wait("sync", "sT", last_pv_mm)
            lt_ld = inc("sync", "sC", lambda e: e.dma_start(out=lt_s, in_=lt_h), 16)
            tid_ld = inc("sync", "sC", lambda e: e.dma_start(out=tidc, in_=tidc_h), 16)
            wait("tensor", "sC", lt_ld)
            wait("vector", "sC", tid_ld)
            wait("gpsimd", "sT", last_pv_mm)
            g_io = inc("gpsimd", "sP",
                       lambda e: e.iota(iota_i, pattern=[[1, CAP]], base=0,
                                        channel_multiplier=0))
            wait("vector", "sP", g_io)
            wait("vector", "sT", last_pv_mm)
            wv_ = inc("vector", "sV", lambda e: e.tensor_copy(out=iota_f, in_=iota_i))
            wait("vector", "sV", wv_)
            wait("tensor", "sA", qb_wA[1] if qb_wA[1] else qb_wA[0])
            idxi_done = [0] * NE
            xeT_ev = [0] * NE
            bcop = 0
            evac_last = 0
            mm_prev = 0
            for e_ in range(NE):
                wv_ = inc("vector", "sV",
                          lambda e, ee=e_: e.tensor_single_scalar(
                              out=m_s, in_=c_all.rearrange("p (t e) -> p t e", e=NE)[:, :, ee],
                              scalar=0.0, op=ALU.is_gt))
                wait("vector", "sV", wv_)
                wv_ = inc("vector", "sV",
                          lambda e: e.tensor_tensor_scan(out=sc_s, data0=m_s, data1=m_s,
                                                         initial=0.0, op0=ALU.add,
                                                         op1=ALU.bypass))
                wait("vector", "sV", wv_)
                wv_ = inc("vector", "sV", lambda e: e.tensor_copy(out=rt_b, in_=sc_s[:, 15:16]))
                wait("tensor", "sV", wv_)
                if bcop:
                    wait("tensor", "sV", evac_last)
                bmm = inc("tensor", "sT",
                          lambda e: e.matmul(pb[7][:, 0:1], lt_s, rt_b, start=True, stop=True))
                wait("vector", "sT", bmm)
                bcop = inc("vector", "sV", lambda e: e.tensor_copy(out=b_s, in_=pb[7][:, 0:1]))
                wait("vector", "sV", bcop)
                wv_ = inc("vector", "sV",
                          lambda e: e.tensor_scalar(out=pos_s, in0=sc_s, scalar1=b_s,
                                                    scalar2=-10000.0, op0=ALU.add, op1=ALU.add))
                wait("vector", "sV", wv_)
                wv_ = inc("vector", "sV", lambda e: e.tensor_mul(pos_s, pos_s, m_s))
                wait("vector", "sV", wv_)
                wv_ = inc("vector", "sV", lambda e: e.tensor_scalar_add(pos_s, pos_s, 9999.0))
                wait("vector", "sV", wv_)
                r3 = rhs48.rearrange("p (t c) -> p t c", c=3)
                wv_ = inc("vector", "sV", lambda e: e.memset(r3[:, :, 1:2], 1.0))
                wait("vector", "sV", wv_)
                wv_ = inc("vector", "sV",
                          lambda e: e.tensor_copy(out=r3[:, :, 0:1],
                                                  in_=tidc.rearrange("p (t o) -> p t o", o=1)))
                wait("vector", "sV", wv_)
                wv_ = inc("vector", "sV",
                          lambda e, ee=e_: e.tensor_copy(
                              out=r3[:, :, 2:3],
                              in_=c_all.rearrange("p (t e) -> p t e", e=NE)[:, :, ee:ee + 1]))
                rhs_rdy = wv_
                for qc in range(NSL):
                    mm_half = 0
                    for half in range(2):
                        ohv = 0
                        if half == 1:
                            wait("vector", "sT", mm_half)
                        elif mm_prev:
                            wait("vector", "sT", mm_prev)
                        for tl8 in range(8):
                            tl = half * 8 + tl8
                            ohv = inc("vector", "sV",
                                      lambda e, t_=tl, t8=tl8, q_=qc: e.tensor_scalar(
                                          out=oh_s[:, t8 * 128:(t8 + 1) * 128],
                                          in0=iota_f[:, q_ * 128:(q_ + 1) * 128],
                                          scalar1=pos_s[:, t_:t_ + 1], scalar2=None,
                                          op0=ALU.is_equal))
                        wait("tensor", "sV", ohv)
                        if half == 0 and (qc > 0 or e_ > 0):
                            wait("tensor", "sV", evac_last)
                        for tl8 in range(8):
                            tl = half * 8 + tl8
                            fn = (lambda e, t_=tl, t8=tl8: e.matmul(
                                pb[7][:, 4:7], oh_s[:, t8 * 128:(t8 + 1) * 128],
                                rhs48[:, t_ * 3:t_ * 3 + 3],
                                start=(t_ == 0), stop=(t_ == 15)))
                            if tl8 == 7:
                                mm_half = inc("tensor", "sT", fn)
                            else:
                                emit("tensor", fn)
                    mm_prev = mm_half
                    wait("vector", "sT", mm_half)
                    wv_ = inc("vector", "sV", lambda e: e.tensor_copy(out=acc3, in_=pb[7][:, 4:7]))
                    wait("vector", "sV", wv_)
                    col = e_ * NSL + qc
                    wv_ = inc("vector", "sV",
                              lambda e, c_=col: e.scalar_tensor_tensor(
                                  out=idxf[:, c_:c_ + 1], in0=acc3[:, 1:2], scalar=-2047.0,
                                  in1=acc3[:, 0:1], op0=ALU.mult, op1=ALU.add))
                    wait("vector", "sV", wv_)
                    wv_ = inc("vector", "sV",
                              lambda e, c_=col: e.tensor_scalar_add(
                                  idxf[:, c_:c_ + 1], idxf[:, c_:c_ + 1], 2047.0))
                    wait("vector", "sV", wv_)
                    evac_last = inc("vector", "sV",
                                    lambda e, c_=col: e.tensor_copy(out=cg_s[:, c_:c_ + 1],
                                                                    in_=acc3[:, 2:3]))
                wait("vector", "sV", evac_last)
                idxi_done[e_] = inc("vector", "sV",
                                    lambda e, ee=e_: e.tensor_copy(
                                        out=idxi[:, ee * NSL:(ee + 1) * NSL],
                                        in_=idxf[:, ee * NSL:(ee + 1) * NSL]))

            ga_ring = [0, 0]
            tr_prev = [0, 0]


            def emit_xeT(e_, bank_waitA, bank_waitV):
                evs = 0
                for s in range(NSL):
                    ring = s % 2
                    wait("gpsimd", "sV", idxi_done[e_])
                    wait("gpsimd", "sXN0", xn_st[NT - 2])
                    wait("gpsimd", "sXN1", xn_st[NT - 1])
                    if ga_ring[ring]:
                        wait("gpsimd", "sT", ga_ring[ring])
                    ga = inc("gpsimd", "sGA%d" % (s % 2),
                             lambda e, ee=e_, s_=s, xb=xnd[b % 2]: e.indirect_dma_start(
                                 out=xe_tm[s_ % 2], out_offset=None,
                                 in_=xb,
                                 in_offset=IndirectOffsetOnAxis(
                                     ap=idxi[:, ee * NSL + s_:ee * NSL + s_ + 1], axis=0)), 16)
                    wait("tensor", "sGA%d" % (s % 2), ga)
                    if bank_waitA[s % 2]:
                        wait("tensor", "sA", bank_waitA[s % 2])
                        bank_waitA[s % 2] = 0
                    if bank_waitV[s % 2]:
                        wait("tensor", "sV", bank_waitV[s % 2])
                        bank_waitV[s % 2] = 0
                    if tr_prev[s % 2]:
                        wait("tensor", "sV", tr_prev[s % 2])
                    trd = 0
                    for dc in range(DC):
                        fn = (lambda e, d=pbh[s % 2][:, dc * 128:(dc + 1) * 128],
                              m_=xe_tm[s % 2][:, dc * 128:(dc + 1) * 128]:
                              e.transpose(d, m_, identity))
                        if dc == DC - 1:
                            trd = inc("tensor", "sT", fn)
                        else:
                            emit("tensor", fn)
                    ga_ring[ring] = trd
                    wait("vector", "sT", trd)
                    evs = inc("vector", "sV",
                              lambda e, ee=e_, s_=s:
                              e.tensor_copy(
                                  out=xeT.rearrange("p (c t) -> p c t", c=DC)[:, :, s_ * 128:(s_ + 1) * 128],
                                  in_=pbh[s_ % 2].rearrange("p (c t) -> p c t", c=DC)))
                    tr_prev[s % 2] = evs
                xeT_ev[e_] = evs

            emit_xeT(0, [0, 0], [zinv_rdy, zinv_rdy])

            gate_bank_A = [0, 0]
            up_bank_V = [0, 0]
            silu_war_V = [0, 0]
            dn_ev_V = [0, 0, 0, 0]
            ffn_last_mm = 0
            ffn_hid_last = 0
            first_ffn_mm = True
            sc_prev = 0
            sc_stage = [0, 0]
            first_sc = True
            TCS = ((0, 512), (512, 128))
            for ph in range(16):
                e_, h2_ = ph // 2, ph % 2
                hid_last_V = 0
                last_gu_mm = 0
                resc_todo = list(range(NT)) if ph == 0 else []
                first_in_ph = True
                for tof, tsz in TCS:
                    for ht in range(NHT):
                        pgi = ht % 2
                        first_grp = (ph == 0 and tof == 0 and ht == 0)
                        if first_in_ph:
                            wait("tensor", "sV", xeT_ev[e_])
                            first_in_ph = False
                        if ph == 0 and tof == 0 and ht == 1:
                            wait("tensor", "sQ1", ph0_half["sQ1"])
                            wait("tensor", "sQ3", ph0_half["sQ3"])
                        if ph > 0:
                            wait("tensor", "sGU", gu_ld)
                        if first_ffn_mm:
                            wait("tensor", "sV", zinv_rdy)
                            first_ffn_mm = False
                        if gate_bank_A[pgi]:
                            wait("tensor", "sA", gate_bank_A[pgi])
                        mg = 0
                        for kc in range(DC):
                            if first_grp and kc == 0:
                                wait("tensor", "sQ0", ph0_half["sQ0"])
                            if first_grp and kc == 4:
                                wait("tensor", "sQ1", ph0_half["sQ1"])
                            fn = (lambda e, o_=pb[pgi][:, 0:tsz],
                                  l=gbuf[:, kc * 1024 + ht * 128: kc * 1024 + (ht + 1) * 128],
                                  r=xeT[:, kc * CAP + tof: kc * CAP + tof + tsz],
                                  st=(kc == 0), sp=(kc == DC - 1):
                                  e.matmul(o_, l, r, start=st, stop=sp))
                            if kc == DC - 1:
                                mg = inc("tensor", "sT", fn)
                            else:
                                emit("tensor", fn)
                        if up_bank_V[pgi]:
                            wait("tensor", "sV", up_bank_V[pgi])
                        mu = 0
                        for kc in range(DC):
                            if first_grp and kc == 0:
                                wait("tensor", "sQ2", ph0_half["sQ2"])
                            if first_grp and kc == 4:
                                wait("tensor", "sQ3", ph0_half["sQ3"])
                            fn = (lambda e, o_=pb[2 + pgi][:, 0:tsz],
                                  l=ubuf[:, kc * 1024 + ht * 128: kc * 1024 + (ht + 1) * 128],
                                  r=xeT[:, kc * CAP + tof: kc * CAP + tof + tsz],
                                  st=(kc == 0), sp=(kc == DC - 1):
                                  e.matmul(o_, l, r, start=st, stop=sp))
                            if kc == DC - 1:
                                mu = inc("tensor", "sT", fn)
                            else:
                                emit("tensor", fn)
                        last_gu_mm = mu
                        wait("scalar", "sT", mg)
                        if silu_war_V[pgi]:
                            wait("scalar", "sV", silu_war_V[pgi])
                        asl = inc("scalar", "sA",
                                  lambda e, d=silu[pgi][:, 0:tsz], s=pb[pgi][:, 0:tsz]:
                                  e.activation(out=d, in_=s, func=AF.Silu))
                        gate_bank_A[pgi] = asl
                        wait("vector", "sA", asl)
                        wait("vector", "sT", mu)
                        hv = inc("vector", "sV",
                                 lambda e,
                                 d=hid[:, ht * CAP + tof: ht * CAP + tof + tsz],
                                 s=silu[pgi][:, 0:tsz], u=pb[2 + pgi][:, 0:tsz]:
                                 e.tensor_tensor(out=d, in0=s, in1=u, op=ALU.mult))
                        up_bank_V[pgi] = hv
                        silu_war_V[pgi] = hv
                        hid_last_V = hv
                        if resc_todo:
                            emit_resc_stt(resc_todo.pop(0))
                last_dn_mm = 0
                for tile in range(NSL):
                    ring = tile % 2
                    evd = [0, 0]
                    for dh in range(2):
                        gg = 4 + (tile % 2) * 2 + dh
                        wait("tensor", "sWD", d_ld)
                        wait("tensor", "sV", hid_last_V)
                        if dn_ev_V[(tile % 2) * 2 + dh]:
                            wait("tensor", "sV", dn_ev_V[(tile % 2) * 2 + dh])
                        md = 0
                        for ht in range(NHT):
                            fn = (lambda e, o_=pb[gg][:, 0:512],
                                  l=hid[:, ht * CAP + tile * 128: ht * CAP + (tile + 1) * 128],
                                  r=dbuf[:, ht * 1024 + dh * 512: ht * 1024 + (dh + 1) * 512],
                                  st=(ht == 0), sp=(ht == NHT - 1):
                                  e.matmul(o_, l, r, start=st, stop=sp))
                            if ht == NHT - 1:
                                md = inc("tensor", "sT", fn)
                            else:
                                emit("tensor", fn)
                        last_dn_mm = md
                        wait("vector", "sT", md)
                        if sc_stage[ring]:
                            wait("vector", "sSC", sc_stage[ring])
                        evd[dh] = inc("vector", "sV",
                                      lambda e, rg=ring, d_=dh,
                                      s=pb[gg][:, 0:512],
                                      cs=cg_s[:, e_ * NSL + tile:e_ * NSL + tile + 1]:
                                      e.tensor_scalar_mul(stage[rg][:, d_ * 512:(d_ + 1) * 512],
                                                          s, cs))
                        dn_ev_V[(tile % 2) * 2 + dh] = evd[dh]
                    wait("gpsimd", "sV", evd[1])
                    if first_sc:
                        wait("gpsimd", "sZ", zr)
                        first_sc = False
                    if sc_prev:
                        wait("gpsimd", "sSC", sc_prev)
                    sc_stage[ring] = cnt["sSC"] + 16
                    sc_prev = inc("gpsimd", "sSC",
                                  lambda e, ee=e_, t_=tile, rg=ring: e.indirect_dma_start(
                                      out=y12,
                                      out_offset=IndirectOffsetOnAxis(
                                          ap=idxi[:, ee * NSL + t_:ee * NSL + t_ + 1], axis=0),
                                      in_=stage[rg], in_offset=None,
                                      compute_op=ALU.add), 16)
                if ph < 15:
                    gu_ld, d_ld = load_phase_w((ph + 1) // 2, (ph + 1) % 2,
                                               last_gu_mm, last_dn_mm)
                if h2_ == 1 and e_ < NE - 1:
                    emit_xeT(e_ + 1, [gate_bank_A[0], gate_bank_A[1]], [0, 0])
                ffn_last_mm = last_dn_mm
                if ph == 15:
                    ffn_hid_last = hid_last_V

            if dbg and b == 0:
                wait("sync", "sV", idxi_done[NE - 1])
                inc("sync", "sC", lambda e: e.dma_start(out=dbg_m[:, 0:640], in_=scrf[:, 19456:20096]), 16)
                inc("sync", "sC", lambda e: e.dma_start(out=dbg_m[:, 640:1280], in_=iota_f), 16)
                inc("sync", "sC", lambda e: e.dma_start(out=dbg_idx, in_=idxf), 16)
                inc("sync", "sC", lambda e: e.dma_start(out=dbg_cg, in_=cg_s), 16)
                wait("sync", "sSC", cnt["sSC"])
                inc("sync", "sC", lambda e: e.dma_start(out=dbg_y12, in_=y12), 16)

            n_sc_all = cnt["sSC"]
            fin_st = [0, 0]
            rb_n = [0, 0]
            vadd = [0, 0]
            wait("sync", "sSC", n_sc_all)
            for i in range(NT):
                p = i % 2
                if vadd[p]:
                    wait("sync", "sV", vadd[p])
                rb_n[p] = inc("sync", "sR%d" % p,
                              lambda e, i_=i, pp=p: e.dma_start(
                                  out=stage[pp], in_=y12[i_ * 128:(i_ + 1) * 128, :]), 16)
                wait("vector", "sR%d" % p, rb_n[p])
                vadd[p] = inc("vector", "sV",
                              lambda e, i_=i, pp=p: e.tensor_add(
                                  y[:, i_ * 1024:(i_ + 1) * 1024],
                                  y[:, i_ * 1024:(i_ + 1) * 1024], stage[pp]))
                wait("gpsimd", "sV", vadd[p])
                fin_st[p] = inc("gpsimd", "sO%d" % p,
                                lambda e, d=out_hbm[B0 + i * 128:B0 + (i + 1) * 128, :],
                                s=y[:, i * 1024:(i + 1) * 1024]: e.dma_start(out=d, in_=s), 16)

            resc_last = resc_j[NT - 1]
            end_prev = {"sT": ffn_last_mm, "sA": cnt["sA"], "sV": cnt["sV"],
                        "sO0": cnt["sO0"], "sO1": cnt["sO1"], "hid": ffn_hid_last,
                        "sSC": cnt["sSC"]}

        wait("sync", "sO0", cnt["sO0"])
        wait("sync", "sO1", cnt["sO1"])

        with nc.Block() as block:
            @block.sync
            def _(e):
                for f in ops["sync"]:
                    f(e)

            @block.gpsimd
            def _(e):
                for f in ops["gpsimd"]:
                    f(e)

            @block.tensor
            def _(e):
                for f in ops["tensor"]:
                    f(e)

            @block.scalar
            def _(e):
                for f in ops["scalar"]:
                    f(e)

            @block.vector
            def _(e):
                for f in ops["vector"]:
                    f(e)

    return nc


def make_consts():
    s = np.arange(128)[:, None]
    t = np.arange(128)[None, :]
    tid = (np.arange(16)[None, :] * 128 + np.arange(128)[:, None]).astype(np.float32)
    return {
        "ident": np.eye(128, dtype=ml_dtypes.bfloat16),
        "ident32": np.eye(128, dtype=np.float32),
        "maskd": np.ascontiguousarray(np.where(s <= t, 0.0, NEG).astype(np.float32)),
        "lt": np.ascontiguousarray(np.tril(np.ones((128, 128), np.float32), -1).T
                                   .astype(ml_dtypes.bfloat16)),
        "ones_c": np.ones((128, 1), dtype=ml_dtypes.bfloat16),
        "ones_r": np.ones((1, 128), dtype=np.float32),
        "eps": np.full((128, 1), 1e-5, dtype=np.float32),
        "tidc": np.ascontiguousarray(tid),
        "zsrc": np.zeros((T, D), dtype=np.float32),
    }


def make_in_maps(inputs, consts):
    x = np.asarray(inputs["L0_attn_x"], dtype=np.float32)
    shared = {
        "wq": np.ascontiguousarray(np.asarray(inputs["L0_attn_Wq"], np.float32)),
        "wk": np.ascontiguousarray(np.asarray(inputs["L0_attn_Wk"], np.float32)),
        "wv": np.ascontiguousarray(np.asarray(inputs["L0_attn_Wv"], np.float32)),
        "bqT": np.ascontiguousarray(np.asarray(inputs["L0_attn_bq"], np.float32).reshape(DC, 128).T),
        "bkT": np.ascontiguousarray(np.asarray(inputs["L0_attn_bk"], np.float32).reshape(DC, 128).T),
        "bvb": np.ascontiguousarray(np.broadcast_to(np.asarray(inputs["L0_attn_bv"], np.float32), (128, D))),
        "rw": np.ascontiguousarray(np.asarray(inputs["L0_ffn_router_w"], np.float32)),
        "gw": np.ascontiguousarray(np.asarray(inputs["L0_ffn_gate_w"], np.float32)),
        "uw": np.ascontiguousarray(np.asarray(inputs["L0_ffn_up_w"], np.float32)),
        "dw": np.ascontiguousarray(np.asarray(inputs["L0_ffn_down_w"], np.float32)),
    }
    shared.update(consts)
    in_maps = []
    for i in range(NCORES):
        m = dict(shared)
        m["x"] = np.ascontiguousarray(x[2 * i:2 * i + 2].reshape(TB * T, D))
        in_maps.append(m)
    return in_maps


def kernel(**inputs):
    nc = build_program(dbg=False)
    in_maps = make_in_maps(inputs, make_consts())
    res = run_bass_kernel_spmd(nc, in_maps, core_ids=list(range(NCORES)))
    outs = [res.results[i]["out"].reshape(TB, T, D) for i in range(NCORES)]
    return np.concatenate(outs, axis=0).astype(np.float32)


# version 09
# speedup vs baseline: 1.4796x; 1.3043x over previous
import numpy as np
import ml_dtypes
import concourse.bass as bass
import concourse.mybir as mybir
from concourse.bass_utils import run_bass_kernel_spmd
from concourse.bass_core_rust import IndirectOffsetOnAxis
from contextlib import ExitStack

F32 = mybir.dt.float32
BF16 = mybir.dt.bfloat16
I32 = mybir.dt.int32
AF = mybir.ActivationFunctionType
ALU = mybir.AluOpType
AX = mybir.AxisListType

NCORES = 8
TB = 2
T = 2048
D = 1024
DC = 8
NT = 16
NE = 8
HH = 1024
NHT = 8
NEG = -3.0e38
ENGS = ("sync", "gpsimd", "tensor", "scalar", "vector")


def build_program(dbg=False):
    nc = bass.Bass("TRN2", target_bir_lowering=False)

    x_hbm = nc.dram_tensor("x", [TB * T, D], F32, kind="ExternalInput").ap()
    wq_h = nc.dram_tensor("wq", [D, D], F32, kind="ExternalInput").ap()
    wk_h = nc.dram_tensor("wk", [D, D], F32, kind="ExternalInput").ap()
    wv_h = nc.dram_tensor("wv", [D, D], F32, kind="ExternalInput").ap()
    bqT_h = nc.dram_tensor("bqT", [128, DC], F32, kind="ExternalInput").ap()
    bkT_h = nc.dram_tensor("bkT", [128, DC], F32, kind="ExternalInput").ap()
    bvb_h = nc.dram_tensor("bvb", [128, D], F32, kind="ExternalInput").ap()
    rw_h = nc.dram_tensor("rw", [D, NE], F32, kind="ExternalInput").ap()
    gw_h = nc.dram_tensor("gw", [NE, D, 2 * HH], F32, kind="ExternalInput").ap()
    uw_h = nc.dram_tensor("uw", [NE, D, 2 * HH], F32, kind="ExternalInput").ap()
    dw_h = nc.dram_tensor("dw", [NE, 2 * HH, D], F32, kind="ExternalInput").ap()
    ident_h = nc.dram_tensor("ident", [128, 128], BF16, kind="ExternalInput").ap()
    ident32_h = nc.dram_tensor("ident32", [128, 128], F32, kind="ExternalInput").ap()
    maskd_h = nc.dram_tensor("maskd", [128, 128], F32, kind="ExternalInput").ap()
    lt_h = nc.dram_tensor("lt", [128, 128], BF16, kind="ExternalInput").ap()
    ones_c_h = nc.dram_tensor("ones_c", [128, 1], BF16, kind="ExternalInput").ap()
    ones_r_h = nc.dram_tensor("ones_r", [1, 128], F32, kind="ExternalInput").ap()
    eps_h = nc.dram_tensor("eps", [128, 1], F32, kind="ExternalInput").ap()
    zsrc_h = nc.dram_tensor("zsrc", [T, D], F32, kind="ExternalInput").ap()
    tidc_h = nc.dram_tensor("tidc", [128, 16], F32, kind="ExternalInput").ap()
    out_hbm = nc.dram_tensor("out", [TB * T, D], F32, kind="ExternalOutput").ap()
    xnd = [nc.dram_tensor("xnd%d" % i, [T, D], BF16, kind="Internal").ap()
           for i in range(2)]
    y12 = nc.dram_tensor("y12d", [T, D], F32, kind="Internal").ap()
    if dbg:
        dbg_xnT = nc.dram_tensor("dbg_xnT", [128, DC * T], BF16, kind="ExternalOutput").ap()
        dbg_yat = nc.dram_tensor("dbg_yat", [128, NT * D], F32, kind="ExternalOutput").ap()
        dbg_c = nc.dram_tensor("dbg_c", [128, NT * NE], F32, kind="ExternalOutput").ap()
        dbg_s = nc.dram_tensor("dbg_s", [1, 8], F32, kind="ExternalOutput").ap()
        dbg_idx = nc.dram_tensor("dbg_idx", [128, 40], F32, kind="ExternalOutput").ap()
        dbg_cg = nc.dram_tensor("dbg_cg", [128, 40], F32, kind="ExternalOutput").ap()
        dbg_y12 = nc.dram_tensor("dbg_y12", [T, D], F32, kind="ExternalOutput").ap()
        dbg_m = nc.dram_tensor("dbg_m", [128, 1280], F32, kind="ExternalOutput").ap()

    stack = ExitStack()
    with stack:
        xnT = stack.enter_context(nc.sbuf_tensor("xnT", [128, DC * T], BF16)).ap()
        y = stack.enter_context(nc.sbuf_tensor("y", [128, NT * D], F32)).ap()
        scr_t = stack.enter_context(nc.sbuf_tensor("scr", [128, 53248], BF16))
        scr = scr_t.ap()
        scrf = scr_t.ap().bitcast(F32)
        xnb_t = stack.enter_context(nc.sbuf_tensor("xnb_t", [128, 2048], BF16)).ap()

        identity = stack.enter_context(nc.sbuf_tensor("identity_s", [128, 128], BF16)).ap()
        ident32 = stack.enter_context(nc.sbuf_tensor("ident32_s", [128, 128], F32)).ap()
        maskd = stack.enter_context(nc.sbuf_tensor("maskd_s", [128, 128], F32)).ap()
        ones_c = stack.enter_context(nc.sbuf_tensor("ones_c_s", [128, 1], BF16)).ap()
        ones_r = stack.enter_context(nc.sbuf_tensor("ones_r_s", [1, 128], F32)).ap()
        eps_t = stack.enter_context(nc.sbuf_tensor("eps_t", [128, 1], F32)).ap()
        bqT = stack.enter_context(nc.sbuf_tensor("bqT_s", [128, DC], F32)).ap()
        bkT = stack.enter_context(nc.sbuf_tensor("bkT_s", [128, DC], F32)).ap()
        rw_s = stack.enter_context(nc.sbuf_tensor("rw_s", [128, DC * NE], F32)).ap()
        c_all = stack.enter_context(nc.sbuf_tensor("c_all", [128, NT * NE], F32)).ap()
        lg_all = stack.enter_context(nc.sbuf_tensor("lg_all", [128, NT * NE], F32)).ap()
        l2_all = stack.enter_context(nc.sbuf_tensor("l2_all", [128, NT * NE], F32)).ap()
        rts = stack.enter_context(nc.sbuf_tensor("rts", [128, 48], F32)).ap()
        nacc = rts[0:1, 0:16]
        zacc = rts[0:1, 16:32]
        s1v = rts[0:1, 32:40]
        sca = rts[:, 40:42]

        stats = l2_all[:, 0:32]

        pb = [stack.enter_context(nc.psum_tensor("pb%d" % i, [128, 512], F32)).ap()
              for i in range(8)]
        pbh = [p.bitcast(BF16) for p in pb]

        kT = scr[:, 0:16384]
        v_s = scr[:, 16384:32768]
        w0 = scr[:, 32768:40960]
        w1 = scr[:, 40960:49152]
        qT = scr[:, 49152:53248]
        pt = [scr[:, 40960:41216], scr[:, 41216:41472]]
        sqb = [scr[:, 41472:41984], scr[:, 41984:42496]]
        xst = [scrf[:, 24576:25600], scrf[:, 25600:26624]]
        xr = [scrf[:, 24576:25600], scrf[:, 25600:26624],
              scrf[:, 21504:22528], scrf[:, 22528:23552], scrf[:, 23552:24576]]
        bvb_v = scrf[:, 24576:25600]
        xnb = [xnb_t[:, 0:1024], xnb_t[:, 1024:2048]]
        scri = scr_t.ap().bitcast(I32)
        CAP = 640
        NSL = 5
        gbuf = scr[:, 0:8192]
        ubuf = scr[:, 8192:16384]
        dbuf = scr[:, 16384:24576]
        hid = scr[:, 24576:29696]
        oh_s = scrf[:, 12288:13312]
        xeT = scr[:, 29696:34816]
        silu = [scrf[:, 17408:17920], scrf[:, 17920:18432]]
        xe_tm = [scr[:, 36864:37888], scr[:, 37888:38912]]
        idxf = scrf[:, 19456:19496]
        cg_s = scrf[:, 19496:19536]
        idxi = scri[:, 19536:19576]
        acc3 = scrf[:, 19576:19579]
        rhs48 = scrf[:, 19584:19632]
        pos_s = scrf[:, 19632:19648]
        m_s = scrf[:, 19648:19664]
        sc_s = scrf[:, 19664:19680]
        b_s = scrf[:, 19680:19681]
        rt_b = scr[:, 39424:39425]
        tidc = scrf[:, 19696:19712]
        stage = [scrf[:, 20480:21504], scrf[:, 21504:22528]]
        iota_f = scrf[:, 20480:21120]
        iota_i = scri[:, 21504:22144]
        lt_s = scr[:, 44288:44416]
        xn32 = scrf[:, 22528:23552]
        xnT32 = scrf[:, 23552:24576]

        ops = {e: [] for e in ENGS}
        sems = {}
        cnt = {}
        for nm in ("sC", "sX0", "sX1", "sW0", "sW1", "sGU", "sWD",
                   "sO0", "sO1", "sT", "sA", "sV", "sP",
                   "sR0", "sR1",
                   "sQ0", "sQ1", "sQ2", "sQ3",
                   "sXN0", "sXN1", "sZ", "sSC", "sGA0", "sGA1"):
            sems[nm] = stack.enter_context(nc.semaphore(nm))
            cnt[nm] = 0

        def add(e, fn):
            ops[e].append(fn)

        def wait(e, snm, val):
            if val > 0:
                add(e, lambda eng, s=sems[snm], v=val: eng.wait_ge(s, v))

        def inc(e, snm, fn, k=1):
            cnt[snm] += k
            add(e, lambda eng, f=fn, s=sems[snm], kk=k: f(eng).then_inc(s, kk))
            return cnt[snm]

        def emit(e, fn):
            add(e, fn)

        for dst, csrc in ((identity, ident_h), (ident32, ident32_h), (maskd, maskd_h),
                          (ones_c, ones_c_h), (ones_r, ones_r_h), (eps_t, eps_h),
                          (bqT, bqT_h), (bkT, bkT_h)):
            inc("sync", "sC", lambda e, d=dst, s=csrc: e.dma_start(out=d, in_=s), 16)
        inc("sync", "sC",
            lambda e: e.dma_start(out=rw_s.rearrange("p (c e) -> p c e", c=DC),
                                  in_=rw_h.rearrange("(c p) e -> p c e", p=128)), 16)
        n_const = cnt["sC"]
        wait("scalar", "sC", n_const)
        wait("vector", "sC", n_const)
        wait("tensor", "sC", n_const)

        end_prev = {"sT": 0, "sA": 0, "sV": 0, "sO0": 0, "sO1": 0, "hid": 0,
                    "sSC": 0}
        resc_last = 0

        for b in range(TB):
            B0 = b * T

            xcast = [0] * NT
            xn_st = [0] * NT
            xnT_cp = [0] * NT
            tr_done = [0] * NT
            f32tr_done = [0] * NT
            xn32cp = [0] * NT
            rt_mm = [0] * NT
            rt_ev = [0] * NT
            for i in range(NT):
                p = i % 2
                o = p * 16
                MEAN, VAR, STD, RSTD, NB = o + 12, o + 13, o + 14, o + 15, o + 0
                if i >= 2:
                    wait("sync", "sA", xcast[i - 2])
                elif b > 0:
                    wait("sync", "sV", resc_last)
                sxp = "sX%d" % p
                nx = inc("sync", sxp,
                         lambda e, d=xst[p], s=x_hbm[B0 + i * 128:B0 + (i + 1) * 128, :]:
                         e.dma_start(out=d, in_=s), 16)
                wait("vector", sxp, nx)
                emit("vector", lambda e, d=stats[:, o:o + 6], s=xst[p][:, 0:512]:
                     e.bn_stats(d, s))
                wbn = inc("vector", "sV",
                          lambda e, d=stats[:, o + 6:o + 12], s=xst[p][:, 512:1024]:
                          e.bn_stats(d, s))
                wait("vector", "sV", wbn)
                wag = inc("vector", "sV",
                          lambda e, d=stats[:, MEAN:MEAN + 2], s=stats[:, o:o + 12]:
                          e.bn_aggr(d, s))
                wait("scalar", "sV", wag)
                wstd = inc("scalar", "sA",
                           lambda e, d=stats[:, STD:STD + 1], s=stats[:, VAR:VAR + 1]:
                           e.activation(out=d, in_=s, func=AF.Sqrt, bias=eps_t))
                wait("vector", "sA", wstd)
                wrs = inc("vector", "sV",
                          lambda e, d=stats[:, RSTD:RSTD + 1], s=stats[:, STD:STD + 1]:
                          e.reciprocal(out=d, in_=s))
                wait("vector", "sV", wrs)
                wnb = inc("vector", "sV",
                          lambda e, d=stats[:, NB:NB + 1], s=stats[:, MEAN:MEAN + 1],
                          r=stats[:, RSTD:RSTD + 1]:
                          e.scalar_tensor_tensor(out=d, in0=s, scalar=-1.0, in1=r,
                                                 op0=ALU.mult, op1=ALU.mult))
                wait("scalar", "sV", wnb)
                if i >= 1:
                    wait("scalar", "sT", f32tr_done[i - 1])
                a32 = inc("scalar", "sA",
                          lambda e, d=xn32, s=xst[p], sc=stats[:, RSTD:RSTD + 1],
                          bi=stats[:, NB:NB + 1]:
                          e.activation(out=d, in_=s, func=AF.Identity, scale=sc, bias=bi))
                wait("scalar", "sA", a32)
                if i >= 2:
                    wait("scalar", "sT", tr_done[i - 2])
                    wait("scalar", "sXN%d" % p, xn_st[i - 2])
                xcast[i] = inc("scalar", "sA",
                               lambda e, d=xnb[p], s=xn32:
                               e.activation(out=d, in_=s, func=AF.Copy))
                wait("sync", "sA", xcast[i])
                xn_st[i] = inc("sync", "sXN%d" % p,
                               lambda e, d=xnd[b % 2][i * 128:(i + 1) * 128, :],
                               s=xnb[p]: e.dma_start(out=d, in_=s), 16)
                wait("tensor", "sA", xcast[i])
                if i < 2:
                    if b > 0:
                        wait("tensor", "sA", end_prev["sA"])
                else:
                    wait("tensor", "sV", xnT_cp[i - 2])
                for dc in range(DC):
                    fn = (lambda e, d=pbh[p][:, dc * 128:(dc + 1) * 128],
                          s=xnb[p][:, dc * 128:(dc + 1) * 128]:
                          e.transpose(d, s, identity))
                    if dc == DC - 1:
                        tr_done[i] = inc("tensor", "sT", fn)
                    else:
                        emit("tensor", fn)
                if i > 0:
                    wait("tensor", "sV", xn32cp[i - 1])
                elif b > 0:
                    wait("tensor", "sV", end_prev["hid"])
                for dc in range(DC):
                    fn = (lambda e,
                          d=pb[2 + dc // 4][:, (dc % 4) * 128:((dc % 4) + 1) * 128],
                          s=xn32[:, dc * 128:(dc + 1) * 128]:
                          e.transpose(d, s, ident32))
                    if dc == DC - 1:
                        f32tr_done[i] = inc("tensor", "sT", fn)
                    else:
                        emit("tensor", fn)
                wait("vector", "sT", tr_done[i])
                xnT_cp[i] = inc("vector", "sV",
                                lambda e,
                                d=xnT.rearrange("p (c t) -> p c t", c=DC)[:, :, i * 128:(i + 1) * 128],
                                s=pbh[p].rearrange("p (c t) -> p c t", c=DC):
                                e.tensor_copy(out=d, in_=s))
                wait("vector", "sT", f32tr_done[i])
                if i > 0:
                    wait("vector", "sT", rt_mm[i - 1])
                emit("vector",
                     lambda e: e.tensor_copy(out=xnT32[:, 0:512], in_=pb[2][:, 0:512]))
                xn32cp[i] = inc("vector", "sV",
                                lambda e: e.tensor_copy(out=xnT32[:, 512:1024],
                                                        in_=pb[3][:, 0:512]))
                lb = 4 + (i % 2)
                wait("tensor", "sV", xn32cp[i])
                if i > 1:
                    wait("tensor", "sA", rt_ev[i - 2])
                elif b > 0:
                    wait("tensor", "sV", end_prev["sV"])
                for kc in range(DC):
                    fn = (lambda e, o_=pb[lb][:, 0:8],
                          l=xnT32[:, kc * 128:(kc + 1) * 128],
                          r=rw_s[:, kc * 8:(kc + 1) * 8],
                          st=(kc == 0), sp=(kc == DC - 1):
                          e.matmul(o_, l, r, start=st, stop=sp))
                    if kc == DC - 1:
                        rt_mm[i] = inc("tensor", "sT", fn)
                    else:
                        emit("tensor", fn)
                if i > 0:
                    wait("scalar", "sT", rt_mm[i - 1])
                    rt_ev[i - 1] = inc("scalar", "sA",
                                       lambda e, d=lg_all[:, (i - 1) * 8:i * 8],
                                       s=pb[4 + ((i - 1) % 2)][:, 0:8]:
                                       e.activation(out=d, in_=s, func=AF.Copy))
                if i == NT - 1:
                    wait("scalar", "sT", rt_mm[i])
                    rt_ev[i] = inc("scalar", "sA",
                                   lambda e, d=lg_all[:, i * 8:(i + 1) * 8],
                                   s=pb[lb][:, 0:8]:
                                   e.activation(out=d, in_=s, func=AF.Copy))

            lg3 = lg_all.rearrange("p (t e) -> p t e", t=NT)
            l23 = l2_all.rearrange("p (t e) -> p t e", t=NT)
            c3 = c_all.rearrange("p (t e) -> p t e", t=NT)
            def rbc(col):
                return rts[:, col:col + 16].rearrange("p (t o) -> p t o", o=1) \
                          .broadcast_to([128, NT, NE])
            wait("vector", "sA", rt_ev[NT - 1])
            wv_ = inc("vector", "sV",
                      lambda e: e.tensor_reduce(rts[:, 0:16], lg3, axis=AX.X, op=ALU.max))
            wait("vector", "sV", wv_)
            wv_ = inc("vector", "sV",
                      lambda e: e.tensor_tensor(out=c3, in0=lg3, in1=rbc(0), op=ALU.is_equal))
            wait("vector", "sV", wv_)
            wv_ = inc("vector", "sV",
                      lambda e: e.scalar_tensor_tensor(out=l23, in0=c3, scalar=NEG,
                                                       in1=lg3, op0=ALU.mult, op1=ALU.add))
            wait("vector", "sV", wv_)
            wv_ = inc("vector", "sV",
                      lambda e: e.tensor_reduce(rts[:, 16:32], l23, axis=AX.X, op=ALU.max))
            wait("vector", "sV", wv_)
            wv_ = inc("vector", "sV",
                      lambda e: e.tensor_sub(rts[:, 32:48], rts[:, 16:32], rts[:, 0:16]))
            wait("scalar", "sV", wv_)
            wa_ = inc("scalar", "sA",
                      lambda e: e.activation(out=rts[:, 0:16], in_=rts[:, 32:48],
                                             func=AF.Sigmoid, scale=-1.0))
            wait("scalar", "sA", wa_)
            wa_ = inc("scalar", "sA",
                      lambda e: e.activation(out=rts[:, 32:48], in_=rts[:, 32:48],
                                             func=AF.Sigmoid))
            wait("vector", "sA", wa_)
            wv_ = inc("vector", "sV",
                      lambda e: e.tensor_tensor(out=c3, in0=c3, in1=rbc(0), op=ALU.mult))
            wait("vector", "sV", wv_)
            wv_ = inc("vector", "sV",
                      lambda e: e.tensor_tensor(out=l23, in0=l23, in1=rbc(16), op=ALU.is_equal))
            wait("vector", "sV", wv_)
            wv_ = inc("vector", "sV",
                      lambda e: e.tensor_tensor(out=l23, in0=l23, in1=rbc(32), op=ALU.mult))
            wait("vector", "sV", wv_)
            c_rdy = inc("vector", "sV",
                        lambda e: e.tensor_add(c_all, c_all, l2_all))

            if dbg and b == 0:
                wait("sync", "sV", xnT_cp[NT - 1])
                inc("sync", "sC", lambda e: e.dma_start(out=dbg_xnT, in_=xnT), 16)
                wait("sync", "sV", c_rdy)
                inc("sync", "sC", lambda e: e.dma_start(out=dbg_c, in_=c_all), 16)

            wait("sync", "sA", xcast[NT - 1])
            bvb_ld = inc("sync", "sC",
                         lambda e: e.dma_start(out=bvb_v, in_=bvb_h), 16)

            if b > 0:
                wait("gpsimd", "sT", end_prev["sT"])
                wait("gpsimd", "sSC", end_prev["sSC"])
            wk_ld = inc("gpsimd", "sW0",
                        lambda e: e.dma_start(
                            out=w0.rearrange("p (c d) -> p c d", c=DC),
                            in_=wk_h.rearrange("(c p) d -> p c d", p=128)), 16)
            wait("gpsimd", "sT", rt_mm[NT - 1])
            wait("gpsimd", "sA", xcast[NT - 1])
            wv_ld = inc("gpsimd", "sW1",
                        lambda e: e.dma_start(
                            out=w1.rearrange("p (c d) -> p c d", c=DC),
                            in_=wv_h.rearrange("(c p) d -> p c d", p=128)), 16)

            kT_ev = {}
            g_ev = [0, 0, 0, 0]
            gi = 0
            last_kT_mm = 0
            first_kt = True
            for dt in range(DC):
                for tc4 in range(4):
                    g = gi % 4
                    gi += 1
                    wait("tensor", "sW0", wk_ld)
                    wait("tensor", "sV", xnT_cp[tc4 * 4 + 3])
                    if first_kt:
                        wait("tensor", "sV", xn32cp[NT - 1])
                        first_kt = False
                    if g_ev[g]:
                        wait("tensor", "sA", g_ev[g])
                    for kc in range(DC):
                        fn = (lambda e, o_=pb[g][:, 0:512],
                              l=w0[:, kc * 1024 + dt * 128: kc * 1024 + (dt + 1) * 128],
                              r=xnT[:, kc * 2048 + tc4 * 512: kc * 2048 + (tc4 + 1) * 512],
                              st=(kc == 0), sp=(kc == DC - 1):
                              e.matmul(o_, l, r, start=st, stop=sp))
                        if kc == DC - 1:
                            last_kT_mm = inc("tensor", "sT", fn)
                        else:
                            emit("tensor", fn)
                    wait("scalar", "sT", last_kT_mm)
                    g_ev[g] = inc("scalar", "sA",
                                  lambda e,
                                  d=kT[:, dt * 2048 + tc4 * 512: dt * 2048 + (tc4 + 1) * 512],
                                  s=pb[g][:, 0:512], bi=bkT[:, dt:dt + 1]:
                                  e.activation(out=d, in_=s, func=AF.Identity, bias=bi))
                    kT_ev[(dt, tc4)] = g_ev[g]
            kT_done_A = cnt["sA"]

            wait("gpsimd", "sT", last_kT_mm)
            wq_ld = inc("gpsimd", "sW0",
                        lambda e: e.dma_start(
                            out=w0.rearrange("p (c d) -> p c d", c=DC),
                            in_=wq_h.rearrange("(c p) d -> p c d", p=128)), 16)

            g_evv = [0, 0, 0, 0]
            gi = 0
            last_v_mm = 0
            first_v = True
            for st_ in range(NT):
                for dh in range(2):
                    g = gi % 4
                    gi += 1
                    wait("tensor", "sW1", wv_ld)
                    wait("tensor", "sV", xnT_cp[st_])
                    if first_v:
                        wait("tensor", "sA", kT_done_A)
                    if g_evv[g]:
                        wait("tensor", "sV", g_evv[g])
                    for kc in range(DC):
                        fn = (lambda e, o_=pb[g][:, 0:512],
                              l=xnT[:, kc * 2048 + st_ * 128: kc * 2048 + (st_ + 1) * 128],
                              r=w1[:, kc * 1024 + dh * 512: kc * 1024 + (dh + 1) * 512],
                              st=(kc == 0), sp=(kc == DC - 1):
                              e.matmul(o_, l, r, start=st, stop=sp))
                        if kc == DC - 1:
                            last_v_mm = inc("tensor", "sT", fn)
                        else:
                            emit("tensor", fn)
                    wait("vector", "sT", last_v_mm)
                    if first_v:
                        wait("vector", "sC", bvb_ld)
                        first_v = False
                    g_evv[g] = inc("vector", "sV",
                                   lambda e,
                                   d=v_s[:, st_ * 1024 + dh * 512: st_ * 1024 + (dh + 1) * 512],
                                   s=pb[g][:, 0:512], bi=bvb_v[:, dh * 512:(dh + 1) * 512]:
                                   e.tensor_tensor(out=d, in0=s, in1=bi, op=ALU.add))
            v_done_V = cnt["sV"]

            sq_mmT = 0
            kn_ev = 0
            wait("scalar", "sT", last_v_mm)
            if b > 0:
                wait("scalar", "sSC", end_prev["sSC"])
            for j in range(4):
                for dt in range(DC):
                    jj = dt % 2
                    wait("scalar", "sA", kT_ev[(dt, j)])
                    if sq_mmT:
                        wait("scalar", "sT", sq_mmT)
                    a_k = inc("scalar", "sA",
                              lambda e, d=sqb[jj],
                              s=kT[:, dt * 2048 + j * 512: dt * 2048 + (j + 1) * 512]:
                              e.square(out=d, in_=s))
                    wait("tensor", "sA", a_k)
                    if j == 0 and dt == 0:
                        wait("tensor", "sA", rt_ev[NT - 1])
                    if dt == 0 and kn_ev:
                        wait("tensor", "sV", kn_ev)
                    sq_mmT = inc("tensor", "sT",
                                 lambda e, r=sqb[jj], st=(dt == 0), sp=(dt == DC - 1):
                                 e.matmul(pb[4][0:1, 0:512], ones_c, r, start=st, stop=sp))
                wait("vector", "sT", sq_mmT)
                kn_ev = inc("vector", "sV",
                            lambda e, d=nacc[0:1, 8 + j:9 + j]:
                            e.reduce_max(d, pb[4][0:1, 0:512], axis=AX.X))

            def qT_proj(cc, qb_wA, qb_wV, scalar_pre_V=0):
                evA = 0
                lastT = 0
                first = True
                for dt in range(DC):
                    g = 6 + dt % 2
                    wait("tensor", "sW0", wq_ld)
                    if qb_wA[dt % 2]:
                        wait("tensor", "sA", qb_wA[dt % 2])
                    if qb_wV[dt % 2]:
                        wait("tensor", "sV", qb_wV[dt % 2])
                        qb_wV[dt % 2] = 0
                    for kc in range(DC):
                        fn = (lambda e, o_=pb[g][:, 0:512],
                              l=w0[:, kc * 1024 + dt * 128: kc * 1024 + (dt + 1) * 128],
                              r=xnT[:, kc * 2048 + cc * 512: kc * 2048 + (cc + 1) * 512],
                              st=(kc == 0), sp=(kc == DC - 1):
                              e.matmul(o_, l, r, start=st, stop=sp))
                        if kc == DC - 1:
                            lastT = inc("tensor", "sT", fn)
                        else:
                            emit("tensor", fn)
                    wait("scalar", "sT", lastT)
                    if first and scalar_pre_V:
                        wait("scalar", "sV", scalar_pre_V)
                        first = False
                    evA = inc("scalar", "sA",
                              lambda e, d=qT[:, dt * 512:(dt + 1) * 512],
                              s=pb[g][:, 0:512], bi=bqT[:, dt:dt + 1]:
                              e.activation(out=d, in_=s, func=AF.Identity, bias=bi))
                    qb_wA[dt % 2] = evA
                return evA, lastT

            qb_wA = [0, 0]
            qn_ev = 0
            for cc in range(4):
                q_evA, _ = qT_proj(cc, qb_wA, [0, 0],
                                   scalar_pre_V=(v_done_V if cc == 0 else 0))
                for dt in range(DC):
                    jj = dt % 2
                    wait("scalar", "sA", q_evA)
                    if sq_mmT:
                        wait("scalar", "sT", sq_mmT)
                    a_q = inc("scalar", "sA",
                              lambda e, d=sqb[jj], s=qT[:, dt * 512:(dt + 1) * 512]:
                              e.square(out=d, in_=s))
                    wait("tensor", "sA", a_q)
                    if cc == 0 and dt == 0:
                        wait("tensor", "sV", kn_ev)
                    if dt == 0 and qn_ev:
                        wait("tensor", "sV", qn_ev)
                    sq_mmT = inc("tensor", "sT",
                                 lambda e, r=sqb[jj], st=(dt == 0), sp=(dt == DC - 1):
                                 e.matmul(pb[5][0:1, 0:512], ones_c, r, start=st, stop=sp))
                wait("vector", "sT", sq_mmT)
                qn_ev = inc("vector", "sV",
                            lambda e, d=nacc[0:1, cc:cc + 1]:
                            e.reduce_max(d, pb[5][0:1, 0:512], axis=AX.X))

            wait("vector", "sV", qn_ev)
            wv_ = inc("vector", "sV",
                      lambda e: e.reduce_max(s1v[0:1, 0:1], nacc[0:1, 0:4], axis=AX.X))
            wait("vector", "sV", wv_)
            wv_ = inc("vector", "sV",
                      lambda e: e.reduce_max(s1v[0:1, 1:2], nacc[0:1, 8:12], axis=AX.X))
            wait("scalar", "sV", wv_)
            wa_ = inc("scalar", "sA",
                      lambda e: e.activation(out=s1v[0:1, 2:3], in_=s1v[0:1, 0:1], func=AF.Sqrt))
            wait("scalar", "sA", wa_)
            wa_ = inc("scalar", "sA",
                      lambda e: e.activation(out=s1v[0:1, 3:4], in_=s1v[0:1, 1:2], func=AF.Sqrt))
            wait("vector", "sA", wa_)
            wv_ = inc("vector", "sV",
                      lambda e: e.scalar_tensor_tensor(out=s1v[0:1, 4:5], in0=s1v[0:1, 2:3],
                                                       scalar=-1.0 / 32.0, in1=s1v[0:1, 3:4],
                                                       op0=ALU.mult, op1=ALU.mult))
            wait("tensor", "sV", wv_)
            bc_mm = inc("tensor", "sT",
                        lambda e: e.matmul(pb[5][:, 0:1], ones_r, s1v[0:1, 4:5],
                                           start=True, stop=True))
            wait("vector", "sT", bc_mm)
            negC_rdy = inc("vector", "sV",
                           lambda e: e.tensor_copy(out=sca[:, 0:1], in_=pb[5][:, 0:1]))

            out_ev = [0, 0, 0, 0]
            sc_exA = [0, 0]
            pt_rdT = [0, 0]
            zr_evV = 0
            last_pv_mm = 0
            for cc in range(4):
                qb_wV2 = [zr_evV, zr_evV]
                q_evA, _ = qT_proj(cc, qb_wA, qb_wV2)
                for c2 in range(2):
                    c = 2 * cc + c2
                    toff = c2 * 256
                    nk = 2 * c + 2
                    for k in range(nk):
                        sb_ = 4 + (k % 2)
                        wait("tensor", "sA", q_evA)
                        if sc_exA[k % 2]:
                            wait("tensor", "sA", sc_exA[k % 2])
                        elif k % 2 == 0 and cc == 0 and c2 == 0:
                            wait("tensor", "sV", kn_ev)
                        elif k % 2 == 1 and cc == 0 and c2 == 0:
                            wait("tensor", "sV", negC_rdy)
                        if k == 0:
                            for gg in range(4):
                                if out_ev[gg]:
                                    wait("tensor", "sV", out_ev[gg])
                            if b > 0 and cc == 0 and c2 == 0:
                                wait("tensor", "sO0", end_prev["sO0"])
                                wait("tensor", "sO1", end_prev["sO1"])
                        lastS = 0
                        for dt in range(DC):
                            fn = (lambda e, o_=pb[sb_][:, 0:256],
                                  l=kT[:, dt * 2048 + k * 128: dt * 2048 + (k + 1) * 128],
                                  r=qT[:, dt * 512 + toff: dt * 512 + toff + 256],
                                  st=(dt == 0), sp=(dt == DC - 1):
                                  e.matmul(o_, l, r, start=st, stop=sp))
                            if dt == DC - 1:
                                lastS = inc("tensor", "sT", fn)
                            else:
                                emit("tensor", fn)
                        mk = 0
                        if k == 2 * c:
                            wait("vector", "sT", lastS)
                            mk = inc("vector", "sV",
                                     lambda e, o_=pb[sb_][:, 0:128]:
                                     e.tensor_tensor(out=o_, in0=o_, in1=maskd, op=ALU.add))
                        elif k == 2 * c + 1:
                            wait("vector", "sT", lastS)
                            emit("vector",
                                 lambda e, o_=pb[sb_][:, 0:128]: e.memset(o_, NEG))
                            mk = inc("vector", "sV",
                                     lambda e, o_=pb[sb_][:, 128:256]:
                                     e.tensor_tensor(out=o_, in0=o_, in1=maskd, op=ALU.add))
                        if mk:
                            wait("scalar", "sV", mk)
                        else:
                            wait("scalar", "sT", lastS)
                        if pt_rdT[k % 2]:
                            wait("scalar", "sT", pt_rdT[k % 2])
                        ex = inc("scalar", "sA",
                                 lambda e, d=pt[k % 2], s=pb[sb_][:, 0:256]:
                                 e.activation(out=d, in_=s, func=AF.Exp,
                                              scale=1.0 / 32.0, bias=sca[:, 0:1]))
                        sc_exA[k % 2] = ex
                        wait("tensor", "sA", ex)
                        if k == 0 and zr_evV:
                            wait("tensor", "sV", zr_evV)
                        emit("tensor",
                             lambda e, o_=pb[6][0:1, toff:toff + 256], r=pt[k % 2],
                             st=(k == 0), sp=(k == nk - 1):
                             e.matmul(o_, ones_c, r, start=st, stop=sp))
                        if cc == 0 and c2 == 0 and k == 0:
                            wait("tensor", "sV", v_done_V)
                        for sub in range(2):
                            gsub = 2 * c + sub
                            if k > gsub:
                                continue
                            for dh in range(2):
                                gg = sub * 2 + dh
                                fn = (lambda e, o_=pb[gg][:, 0:512],
                                      l=pt[k % 2][:, sub * 128:(sub + 1) * 128],
                                      r=v_s[:, k * 1024 + dh * 512: k * 1024 + (dh + 1) * 512],
                                      st=(k == 0), sp=(k == 2 * c + sub):
                                      e.matmul(o_, l, r, start=st, stop=sp))
                                if sub == 1 and dh == 1:
                                    pt_rdT[k % 2] = inc("tensor", "sT", fn)
                                else:
                                    emit("tensor", fn)
                        last_pv_mm = pt_rdT[k % 2]
                    for sub in range(2):
                        for dh in range(2):
                            gg = sub * 2 + dh
                            wait("vector", "sT", last_pv_mm)
                            out_ev[gg] = inc("vector", "sV",
                                             lambda e,
                                             d=y[:, (2 * c + sub) * 1024 + dh * 512:
                                                 (2 * c + sub) * 1024 + (dh + 1) * 512],
                                             s=pb[gg][:, 0:512]:
                                             e.tensor_copy(out=d, in_=s))
                    wait("vector", "sT", last_pv_mm)
                    zr_evV = inc("vector", "sV",
                                 lambda e, d=zacc[0:1, c:c + 1],
                                 s=pb[6][0:1, toff:toff + 256]:
                                 e.reduce_sum(d, s, axis=AX.X))

            wait("vector", "sV", zr_evV)
            wv_ = inc("vector", "sV",
                      lambda e: e.reduce_sum(s1v[0:1, 5:6], zacc[0:1, 0:8], axis=AX.X))
            wait("vector", "sV", wv_)
            wv_ = inc("vector", "sV",
                      lambda e: e.reciprocal(out=s1v[0:1, 6:7], in_=s1v[0:1, 5:6]))
            wait("tensor", "sV", wv_)
            wait("tensor", "sA", sc_exA[0])
            wait("tensor", "sA", sc_exA[1])
            bc2 = inc("tensor", "sT",
                      lambda e: e.matmul(pb[4][:, 0:1], ones_r, s1v[0:1, 6:7],
                                         start=True, stop=True))
            wait("vector", "sT", bc2)
            zinv_rdy = inc("vector", "sV",
                           lambda e: e.tensor_copy(out=sca[:, 1:2], in_=pb[4][:, 0:1]))

            def load_gu(e_, h2_, wait_T_gu):
                if wait_T_gu:
                    wait("gpsimd", "sT", wait_T_gu)
                inc("gpsimd", "sGU",
                    lambda e, ee=e_, hh=h2_: e.dma_start(
                        out=gbuf.rearrange("p (c h) -> p c h", c=DC),
                        in_=gw_h[ee].rearrange("(c p) h -> p c h", p=128)[:, :, hh * HH:(hh + 1) * HH]), 16)
                return inc("gpsimd", "sGU",
                           lambda e, ee=e_, hh=h2_: e.dma_start(
                               out=ubuf.rearrange("p (c h) -> p c h", c=DC),
                               in_=uw_h[ee].rearrange("(c p) h -> p c h", p=128)[:, :, hh * HH:(hh + 1) * HH]), 16)

            def load_d(e_, h2_, wait_T_d):
                if wait_T_d:
                    wait("gpsimd", "sT", wait_T_d)
                return inc("gpsimd", "sWD",
                           lambda e, ee=e_, hh=h2_: e.dma_start(
                               out=dbuf.rearrange("p (c d) -> p c d", c=DC),
                               in_=dw_h[ee][hh * HH:(hh + 1) * HH, :].rearrange("(c p) d -> p c d", p=128)), 16)

            wait("gpsimd", "sT", last_pv_mm)
            ph0_half = {}
            for snm, th, lo in (("sQ0", gw_h, 0), ("sQ2", uw_h, 0),
                                ("sQ1", gw_h, 4), ("sQ3", uw_h, 4)):
                dst = gbuf if th is gw_h else ubuf
                ph0_half[snm] = inc("gpsimd", snm,
                                    lambda e, d=dst, t=th, l_=lo: e.dma_start(
                                        out=d.rearrange("p (c h) -> p c h", c=DC)[:, l_:l_ + 4, :],
                                        in_=t[0].rearrange("(c p) h -> p c h", p=128)[:, l_:l_ + 4, 0:HH]), 16)
            gu_ld = 0
            d_ld = inc("gpsimd", "sWD",
                       lambda e: e.dma_start(
                           out=dbuf.rearrange("p (c d) -> p c d", c=DC),
                           in_=dw_h[0][0:HH, :].rearrange("(c p) d -> p c d", p=128)), 16)

            resc_j = [0] * NT
            resc_nx = [0] * NT

            def emit_resc_dma(j):
                p = j % 2
                srp = "sR%d" % p
                if j < 2:
                    wait("sync", "sT", last_pv_mm)
                else:
                    wait("sync", "sV", resc_j[j - 2])
                resc_nx[j] = inc("sync", srp,
                                 lambda e, d=xst[p], s=x_hbm[B0 + j * 128:B0 + (j + 1) * 128, :]:
                                 e.dma_start(out=d, in_=s), 16)

            def emit_resc_stt(j):
                p = j % 2
                wait("vector", "sR%d" % p, resc_nx[j])
                resc_j[j] = inc("vector", "sV",
                                lambda e, d=y[:, j * 1024:(j + 1) * 1024], x_=xst[p]:
                                e.scalar_tensor_tensor(out=d, in0=d, scalar=sca[:, 1:2],
                                                       in1=x_, op0=ALU.mult, op1=ALU.add))
                if j + 2 < NT:
                    emit_resc_dma(j + 2)

            for j in range(2):
                emit_resc_dma(j)

            if dbg and b == 0:
                wait("sync", "sV", zinv_rdy)
                dbg_sX = inc("sync", "sC", lambda e: e.dma_start(out=dbg_s, in_=s1v), 16)

            if dbg and b == 0:
                wait("gpsimd", "sC", dbg_sX)
                wait("tensor", "sC", dbg_sX)

            if b > 0:
                wait("sync", "sSC", end_prev["sSC"])
            zr = inc("sync", "sZ", lambda e: e.dma_start(out=y12, in_=zsrc_h), 16)

            wait("sync", "sT", last_pv_mm)
            lt_ld = inc("sync", "sC", lambda e: e.dma_start(out=lt_s, in_=lt_h), 16)
            tid_ld = inc("sync", "sC", lambda e: e.dma_start(out=tidc, in_=tidc_h), 16)
            wait("tensor", "sC", tid_ld)
            wait("vector", "sC", tid_ld)
            wait("gpsimd", "sT", last_pv_mm)
            g_io = inc("gpsimd", "sP",
                       lambda e: e.iota(iota_i, pattern=[[1, CAP]], base=0,
                                        channel_multiplier=0))
            wait("vector", "sP", g_io)
            wait("vector", "sT", last_pv_mm)
            wv_ = inc("vector", "sV", lambda e: e.tensor_copy(out=iota_f, in_=iota_i))
            wait("vector", "sV", wv_)
            wait("tensor", "sA", qb_wA[1] if qb_wA[1] else qb_wA[0])
            idxi_done = [0] * NE
            xeT_ev = [0] * NE
            bcop = 0
            evac_last = 0
            mm_prev = 0
            for e_ in range(NE):
                wv_ = inc("vector", "sV",
                          lambda e, ee=e_: e.tensor_single_scalar(
                              out=m_s, in_=c_all.rearrange("p (t e) -> p t e", e=NE)[:, :, ee],
                              scalar=0.0, op=ALU.is_gt))
                wait("vector", "sV", wv_)
                wv_ = inc("vector", "sV",
                          lambda e: e.tensor_tensor_scan(out=sc_s, data0=m_s, data1=m_s,
                                                         initial=0.0, op0=ALU.add,
                                                         op1=ALU.bypass))
                wait("vector", "sV", wv_)
                wv_ = inc("vector", "sV", lambda e: e.tensor_copy(out=rt_b, in_=sc_s[:, 15:16]))
                wait("tensor", "sV", wv_)
                if bcop:
                    wait("tensor", "sV", evac_last)
                bmm = inc("tensor", "sT",
                          lambda e: e.matmul(pb[7][:, 0:1], lt_s, rt_b, start=True, stop=True))
                wait("vector", "sT", bmm)
                bcop = inc("vector", "sV", lambda e: e.tensor_copy(out=b_s, in_=pb[7][:, 0:1]))
                wait("vector", "sV", bcop)
                wv_ = inc("vector", "sV",
                          lambda e: e.tensor_scalar(out=pos_s, in0=sc_s, scalar1=b_s,
                                                    scalar2=-10000.0, op0=ALU.add, op1=ALU.add))
                wait("vector", "sV", wv_)
                wv_ = inc("vector", "sV", lambda e: e.tensor_mul(pos_s, pos_s, m_s))
                wait("vector", "sV", wv_)
                wv_ = inc("vector", "sV", lambda e: e.tensor_scalar_add(pos_s, pos_s, 9999.0))
                wait("vector", "sV", wv_)
                r3 = rhs48.rearrange("p (t c) -> p t c", c=3)
                wv_ = inc("vector", "sV", lambda e: e.memset(r3[:, :, 1:2], 1.0))
                wait("vector", "sV", wv_)
                wv_ = inc("vector", "sV",
                          lambda e: e.tensor_copy(out=r3[:, :, 0:1],
                                                  in_=tidc.rearrange("p (t o) -> p t o", o=1)))
                wait("vector", "sV", wv_)
                wv_ = inc("vector", "sV",
                          lambda e, ee=e_: e.tensor_copy(
                              out=r3[:, :, 2:3],
                              in_=c_all.rearrange("p (t e) -> p t e", e=NE)[:, :, ee:ee + 1]))
                rhs_rdy = wv_
                for qc in range(NSL):
                    mm_half = 0
                    for half in range(2):
                        ohv = 0
                        if half == 1:
                            wait("vector", "sT", mm_half)
                        elif mm_prev:
                            wait("vector", "sT", mm_prev)
                        for tl8 in range(8):
                            tl = half * 8 + tl8
                            ohv = inc("vector", "sV",
                                      lambda e, t_=tl, t8=tl8, q_=qc: e.tensor_scalar(
                                          out=oh_s[:, t8 * 128:(t8 + 1) * 128],
                                          in0=iota_f[:, q_ * 128:(q_ + 1) * 128],
                                          scalar1=pos_s[:, t_:t_ + 1], scalar2=None,
                                          op0=ALU.is_equal))
                        wait("tensor", "sV", ohv)
                        if half == 0 and (qc > 0 or e_ > 0):
                            wait("tensor", "sV", evac_last)
                        for tl8 in range(8):
                            tl = half * 8 + tl8
                            fn = (lambda e, t_=tl, t8=tl8: e.matmul(
                                pb[7][:, 4:7], oh_s[:, t8 * 128:(t8 + 1) * 128],
                                rhs48[:, t_ * 3:t_ * 3 + 3],
                                start=(t_ == 0), stop=(t_ == 15)))
                            if tl8 == 7:
                                mm_half = inc("tensor", "sT", fn)
                            else:
                                emit("tensor", fn)
                    mm_prev = mm_half
                    wait("vector", "sT", mm_half)
                    wv_ = inc("vector", "sV", lambda e: e.tensor_copy(out=acc3, in_=pb[7][:, 4:7]))
                    wait("vector", "sV", wv_)
                    col = e_ * NSL + qc
                    wv_ = inc("vector", "sV",
                              lambda e, c_=col: e.scalar_tensor_tensor(
                                  out=idxf[:, c_:c_ + 1], in0=acc3[:, 1:2], scalar=-2047.0,
                                  in1=acc3[:, 0:1], op0=ALU.mult, op1=ALU.add))
                    wait("vector", "sV", wv_)
                    wv_ = inc("vector", "sV",
                              lambda e, c_=col: e.tensor_scalar_add(
                                  idxf[:, c_:c_ + 1], idxf[:, c_:c_ + 1], 2047.0))
                    wait("vector", "sV", wv_)
                    evac_last = inc("vector", "sV",
                                    lambda e, c_=col: e.tensor_copy(out=cg_s[:, c_:c_ + 1],
                                                                    in_=acc3[:, 2:3]))
                wait("vector", "sV", evac_last)
                idxi_done[e_] = inc("vector", "sV",
                                    lambda e, ee=e_: e.tensor_copy(
                                        out=idxi[:, ee * NSL:(ee + 1) * NSL],
                                        in_=idxf[:, ee * NSL:(ee + 1) * NSL]))

            ga_ring = [0, 0]
            tr_prev = [0, 0]


            ga_pend = {}

            def emit_gather(e_, s):
                ring = s % 2
                wait("gpsimd", "sV", idxi_done[e_])
                wait("gpsimd", "sXN0", xn_st[NT - 2])
                wait("gpsimd", "sXN1", xn_st[NT - 1])
                if ga_ring[ring]:
                    wait("gpsimd", "sT", ga_ring[ring])
                ga_pend[(e_, s)] = inc(
                    "gpsimd", "sGA%d" % ring,
                    lambda e, ee=e_, s_=s, xb=xnd[b % 2]: e.indirect_dma_start(
                        out=xe_tm[s_ % 2], out_offset=None, in_=xb,
                        in_offset=IndirectOffsetOnAxis(
                            ap=idxi[:, ee * NSL + s_:ee * NSL + s_ + 1], axis=0)), 16)

            def emit_transpose(e_, s, bank_waitA, bank_waitV):
                ring = s % 2
                wait("tensor", "sGA%d" % ring, ga_pend[(e_, s)])
                if bank_waitA[ring]:
                    wait("tensor", "sA", bank_waitA[ring])
                    bank_waitA[ring] = 0
                if bank_waitV[ring]:
                    wait("tensor", "sV", bank_waitV[ring])
                    bank_waitV[ring] = 0
                if tr_prev[ring]:
                    wait("tensor", "sV", tr_prev[ring])
                trd = 0
                for dc in range(DC):
                    fn = (lambda e, d=pbh[ring][:, dc * 128:(dc + 1) * 128],
                          m_=xe_tm[ring][:, dc * 128:(dc + 1) * 128]:
                          e.transpose(d, m_, identity))
                    if dc == DC - 1:
                        trd = inc("tensor", "sT", fn)
                    else:
                        emit("tensor", fn)
                ga_ring[ring] = trd
                wait("vector", "sT", trd)
                evs = inc("vector", "sV",
                          lambda e, s_=s:
                          e.tensor_copy(
                              out=xeT.rearrange("p (c t) -> p c t", c=DC)[:, :, s_ * 128:(s_ + 1) * 128],
                              in_=pbh[s_ % 2].rearrange("p (c t) -> p c t", c=DC)))
                tr_prev[ring] = evs
                xeT_ev[e_] = evs
                if s + 2 < NSL:
                    emit_gather(e_, s + 2)

            def emit_xeT(e_, bank_waitA, bank_waitV):
                emit_gather(e_, 0)
                emit_gather(e_, 1)
                for s in range(NSL):
                    emit_transpose(e_, s, bank_waitA, bank_waitV)

            emit_xeT(0, [0, 0], [zinv_rdy, zinv_rdy])

            gate_bank_A = [0, 0]
            up_bank_V = [0, 0]
            silu_war_V = [0, 0]
            dn_ev_V = [0, 0, 0, 0]
            ffn_last_mm = 0
            ffn_hid_last = 0
            first_ffn_mm = True
            sc_prev = 0
            sc_stage = [0, 0]
            first_sc = True
            TCS = ((0, 512), (512, 128))
            for ph in range(16):
                e_, h2_ = ph // 2, ph % 2
                hid_last_V = 0
                last_gu_mm = 0
                resc_todo = list(range(NT)) if ph == 0 else []
                first_in_ph = True
                for tof, tsz in TCS:
                    for ht in range(NHT):
                        pgi = ht % 2
                        first_grp = (ph == 0 and tof == 0 and ht == 0)
                        if first_in_ph:
                            wait("tensor", "sV", xeT_ev[e_])
                            first_in_ph = False
                        if ph == 0 and tof == 0 and ht == 1:
                            wait("tensor", "sQ1", ph0_half["sQ1"])
                            wait("tensor", "sQ3", ph0_half["sQ3"])
                        if ph > 0:
                            wait("tensor", "sGU", gu_ld)
                        if first_ffn_mm:
                            wait("tensor", "sV", zinv_rdy)
                            first_ffn_mm = False
                        if gate_bank_A[pgi]:
                            wait("tensor", "sA", gate_bank_A[pgi])
                        mg = 0
                        for kc in range(DC):
                            if first_grp and kc == 0:
                                wait("tensor", "sQ0", ph0_half["sQ0"])
                            if first_grp and kc == 4:
                                wait("tensor", "sQ1", ph0_half["sQ1"])
                            fn = (lambda e, o_=pb[pgi][:, 0:tsz],
                                  l=gbuf[:, kc * 1024 + ht * 128: kc * 1024 + (ht + 1) * 128],
                                  r=xeT[:, kc * CAP + tof: kc * CAP + tof + tsz],
                                  st=(kc == 0), sp=(kc == DC - 1):
                                  e.matmul(o_, l, r, start=st, stop=sp))
                            if kc == DC - 1:
                                mg = inc("tensor", "sT", fn)
                            else:
                                emit("tensor", fn)
                        if up_bank_V[pgi]:
                            wait("tensor", "sV", up_bank_V[pgi])
                        mu = 0
                        for kc in range(DC):
                            if first_grp and kc == 0:
                                wait("tensor", "sQ2", ph0_half["sQ2"])
                            if first_grp and kc == 4:
                                wait("tensor", "sQ3", ph0_half["sQ3"])
                            fn = (lambda e, o_=pb[2 + pgi][:, 0:tsz],
                                  l=ubuf[:, kc * 1024 + ht * 128: kc * 1024 + (ht + 1) * 128],
                                  r=xeT[:, kc * CAP + tof: kc * CAP + tof + tsz],
                                  st=(kc == 0), sp=(kc == DC - 1):
                                  e.matmul(o_, l, r, start=st, stop=sp))
                            if kc == DC - 1:
                                mu = inc("tensor", "sT", fn)
                            else:
                                emit("tensor", fn)
                        last_gu_mm = mu
                        wait("scalar", "sT", mg)
                        if silu_war_V[pgi]:
                            wait("scalar", "sV", silu_war_V[pgi])
                        asl = inc("scalar", "sA",
                                  lambda e, d=silu[pgi][:, 0:tsz], s=pb[pgi][:, 0:tsz]:
                                  e.activation(out=d, in_=s, func=AF.Silu))
                        gate_bank_A[pgi] = asl
                        wait("vector", "sA", asl)
                        wait("vector", "sT", mu)
                        hv = inc("vector", "sV",
                                 lambda e,
                                 d=hid[:, ht * CAP + tof: ht * CAP + tof + tsz],
                                 s=silu[pgi][:, 0:tsz], u=pb[2 + pgi][:, 0:tsz]:
                                 e.tensor_tensor(out=d, in0=s, in1=u, op=ALU.mult))
                        up_bank_V[pgi] = hv
                        silu_war_V[pgi] = hv
                        hid_last_V = hv
                        if resc_todo:
                            emit_resc_stt(resc_todo.pop(0))
                if ph < 15:
                    gu_ld_next = load_gu((ph + 1) // 2, (ph + 1) % 2, last_gu_mm)
                if h2_ == 1 and e_ < NE - 1:
                    for s_g in range(2):
                        emit_gather(e_ + 1, s_g)
                last_dn_mm = 0
                tr_todo = list(range(NSL)) if (h2_ == 1 and e_ < NE - 1) else []
                for tile in range(NSL):
                    ring = tile % 2
                    evd = [0, 0]
                    for dh in range(2):
                        gg = 4 + (tile % 2) * 2 + dh
                        wait("tensor", "sWD", d_ld)
                        wait("tensor", "sV", hid_last_V)
                        if dn_ev_V[(tile % 2) * 2 + dh]:
                            wait("tensor", "sV", dn_ev_V[(tile % 2) * 2 + dh])
                        md = 0
                        for ht in range(NHT):
                            fn = (lambda e, o_=pb[gg][:, 0:512],
                                  l=hid[:, ht * CAP + tile * 128: ht * CAP + (tile + 1) * 128],
                                  r=dbuf[:, ht * 1024 + dh * 512: ht * 1024 + (dh + 1) * 512],
                                  st=(ht == 0), sp=(ht == NHT - 1):
                                  e.matmul(o_, l, r, start=st, stop=sp))
                            if ht == NHT - 1:
                                md = inc("tensor", "sT", fn)
                            else:
                                emit("tensor", fn)
                        last_dn_mm = md
                        wait("vector", "sT", md)
                        if sc_stage[ring]:
                            wait("vector", "sSC", sc_stage[ring])
                        evd[dh] = inc("vector", "sV",
                                      lambda e, rg=ring, d_=dh,
                                      s=pb[gg][:, 0:512],
                                      cs=cg_s[:, e_ * NSL + tile:e_ * NSL + tile + 1]:
                                      e.tensor_scalar_mul(stage[rg][:, d_ * 512:(d_ + 1) * 512],
                                                          s, cs))
                        dn_ev_V[(tile % 2) * 2 + dh] = evd[dh]
                    wait("gpsimd", "sV", evd[1])
                    if first_sc:
                        wait("gpsimd", "sZ", zr)
                        first_sc = False
                    if sc_prev:
                        wait("gpsimd", "sSC", sc_prev)
                    sc_stage[ring] = cnt["sSC"] + 16
                    sc_prev = inc("gpsimd", "sSC",
                                  lambda e, ee=e_, t_=tile, rg=ring: e.indirect_dma_start(
                                      out=y12,
                                      out_offset=IndirectOffsetOnAxis(
                                          ap=idxi[:, ee * NSL + t_:ee * NSL + t_ + 1], axis=0),
                                      in_=stage[rg], in_offset=None,
                                      compute_op=ALU.add), 16)
                    if tr_todo and tile >= 1:
                        emit_transpose(e_ + 1, tr_todo.pop(0),
                                       [gate_bank_A[0], gate_bank_A[1]], [0, 0])
                while tr_todo:
                    emit_transpose(e_ + 1, tr_todo.pop(0),
                                   [gate_bank_A[0], gate_bank_A[1]], [0, 0])
                if ph < 15:
                    gu_ld = gu_ld_next
                    d_ld = load_d((ph + 1) // 2, (ph + 1) % 2, last_dn_mm)
                ffn_last_mm = last_dn_mm
                if ph == 15:
                    ffn_hid_last = hid_last_V

            if dbg and b == 0:
                wait("sync", "sV", idxi_done[NE - 1])
                inc("sync", "sC", lambda e: e.dma_start(out=dbg_m[:, 0:640], in_=scrf[:, 19456:20096]), 16)
                inc("sync", "sC", lambda e: e.dma_start(out=dbg_m[:, 640:1280], in_=iota_f), 16)
                inc("sync", "sC", lambda e: e.dma_start(out=dbg_idx, in_=idxf), 16)
                inc("sync", "sC", lambda e: e.dma_start(out=dbg_cg, in_=cg_s), 16)
                wait("sync", "sSC", cnt["sSC"])
                inc("sync", "sC", lambda e: e.dma_start(out=dbg_y12, in_=y12), 16)

            n_sc_all = cnt["sSC"]
            fin_st = [0, 0]
            rb_n = [0, 0]
            vadd = [0, 0]
            wait("sync", "sSC", n_sc_all)
            for i in range(NT):
                p = i % 2
                if vadd[p]:
                    wait("sync", "sV", vadd[p])
                rb_n[p] = inc("sync", "sR%d" % p,
                              lambda e, i_=i, pp=p: e.dma_start(
                                  out=stage[pp], in_=y12[i_ * 128:(i_ + 1) * 128, :]), 16)
                wait("vector", "sR%d" % p, rb_n[p])
                vadd[p] = inc("vector", "sV",
                              lambda e, i_=i, pp=p: e.tensor_add(
                                  y[:, i_ * 1024:(i_ + 1) * 1024],
                                  y[:, i_ * 1024:(i_ + 1) * 1024], stage[pp]))
                wait("gpsimd", "sV", vadd[p])
                fin_st[p] = inc("gpsimd", "sO%d" % p,
                                lambda e, d=out_hbm[B0 + i * 128:B0 + (i + 1) * 128, :],
                                s=y[:, i * 1024:(i + 1) * 1024]: e.dma_start(out=d, in_=s), 16)

            resc_last = resc_j[NT - 1]
            end_prev = {"sT": ffn_last_mm, "sA": cnt["sA"], "sV": cnt["sV"],
                        "sO0": cnt["sO0"], "sO1": cnt["sO1"], "hid": ffn_hid_last,
                        "sSC": cnt["sSC"]}

        wait("sync", "sO0", cnt["sO0"])
        wait("sync", "sO1", cnt["sO1"])

        with nc.Block() as block:
            @block.sync
            def _(e):
                for f in ops["sync"]:
                    f(e)

            @block.gpsimd
            def _(e):
                for f in ops["gpsimd"]:
                    f(e)

            @block.tensor
            def _(e):
                for f in ops["tensor"]:
                    f(e)

            @block.scalar
            def _(e):
                for f in ops["scalar"]:
                    f(e)

            @block.vector
            def _(e):
                for f in ops["vector"]:
                    f(e)

    return nc


def make_consts():
    s = np.arange(128)[:, None]
    t = np.arange(128)[None, :]
    tid = (np.arange(16)[None, :] * 128 + np.arange(128)[:, None]).astype(np.float32)
    return {
        "ident": np.eye(128, dtype=ml_dtypes.bfloat16),
        "ident32": np.eye(128, dtype=np.float32),
        "maskd": np.ascontiguousarray(np.where(s <= t, 0.0, NEG).astype(np.float32)),
        "lt": np.ascontiguousarray(np.tril(np.ones((128, 128), np.float32), -1).T
                                   .astype(ml_dtypes.bfloat16)),
        "ones_c": np.ones((128, 1), dtype=ml_dtypes.bfloat16),
        "ones_r": np.ones((1, 128), dtype=np.float32),
        "eps": np.full((128, 1), 1e-5, dtype=np.float32),
        "tidc": np.ascontiguousarray(tid),
        "zsrc": np.zeros((T, D), dtype=np.float32),
    }


def make_in_maps(inputs, consts):
    x = np.asarray(inputs["L0_attn_x"], dtype=np.float32)
    shared = {
        "wq": np.ascontiguousarray(np.asarray(inputs["L0_attn_Wq"], np.float32)),
        "wk": np.ascontiguousarray(np.asarray(inputs["L0_attn_Wk"], np.float32)),
        "wv": np.ascontiguousarray(np.asarray(inputs["L0_attn_Wv"], np.float32)),
        "bqT": np.ascontiguousarray(np.asarray(inputs["L0_attn_bq"], np.float32).reshape(DC, 128).T),
        "bkT": np.ascontiguousarray(np.asarray(inputs["L0_attn_bk"], np.float32).reshape(DC, 128).T),
        "bvb": np.ascontiguousarray(np.broadcast_to(np.asarray(inputs["L0_attn_bv"], np.float32), (128, D))),
        "rw": np.ascontiguousarray(np.asarray(inputs["L0_ffn_router_w"], np.float32)),
        "gw": np.ascontiguousarray(np.asarray(inputs["L0_ffn_gate_w"], np.float32)),
        "uw": np.ascontiguousarray(np.asarray(inputs["L0_ffn_up_w"], np.float32)),
        "dw": np.ascontiguousarray(np.asarray(inputs["L0_ffn_down_w"], np.float32)),
    }
    shared.update(consts)
    in_maps = []
    for i in range(NCORES):
        m = dict(shared)
        m["x"] = np.ascontiguousarray(x[2 * i:2 * i + 2].reshape(TB * T, D))
        in_maps.append(m)
    return in_maps


def kernel(**inputs):
    nc = build_program(dbg=False)
    in_maps = make_in_maps(inputs, make_consts())
    res = run_bass_kernel_spmd(nc, in_maps, core_ids=list(range(NCORES)))
    outs = [res.results[i]["out"].reshape(TB, T, D) for i in range(NCORES)]
    return np.concatenate(outs, axis=0).astype(np.float32)


# version 10
# speedup vs baseline: 1.4818x; 1.0015x over previous
import numpy as np
import ml_dtypes
import concourse.bass as bass
import concourse.mybir as mybir
from concourse.bass_utils import run_bass_kernel_spmd
from concourse.bass_core_rust import IndirectOffsetOnAxis
from contextlib import ExitStack

F32 = mybir.dt.float32
BF16 = mybir.dt.bfloat16
I32 = mybir.dt.int32
AF = mybir.ActivationFunctionType
ALU = mybir.AluOpType
AX = mybir.AxisListType

NCORES = 8
TB = 2
T = 2048
D = 1024
DC = 8
NT = 16
NE = 8
HH = 1024
NHT = 8
NEG = -3.0e38
ENGS = ("sync", "gpsimd", "tensor", "scalar", "vector")


def build_program(dbg=False):
    nc = bass.Bass("TRN2", target_bir_lowering=False)

    x_hbm = nc.dram_tensor("x", [TB * T, D], F32, kind="ExternalInput").ap()
    wq_h = nc.dram_tensor("wq", [D, D], F32, kind="ExternalInput").ap()
    wk_h = nc.dram_tensor("wk", [D, D], F32, kind="ExternalInput").ap()
    wv_h = nc.dram_tensor("wv", [D, D], F32, kind="ExternalInput").ap()
    bqT_h = nc.dram_tensor("bqT", [128, DC], F32, kind="ExternalInput").ap()
    bkT_h = nc.dram_tensor("bkT", [128, DC], F32, kind="ExternalInput").ap()
    bvb_h = nc.dram_tensor("bvb", [128, D], F32, kind="ExternalInput").ap()
    rw_h = nc.dram_tensor("rw", [D, NE], F32, kind="ExternalInput").ap()
    gw_h = nc.dram_tensor("gw", [NE, D, 2 * HH], F32, kind="ExternalInput").ap()
    uw_h = nc.dram_tensor("uw", [NE, D, 2 * HH], F32, kind="ExternalInput").ap()
    dw_h = nc.dram_tensor("dw", [NE, 2 * HH, D], F32, kind="ExternalInput").ap()
    ident_h = nc.dram_tensor("ident", [128, 128], BF16, kind="ExternalInput").ap()
    ident32_h = nc.dram_tensor("ident32", [128, 128], F32, kind="ExternalInput").ap()
    maskd_h = nc.dram_tensor("maskd", [128, 128], F32, kind="ExternalInput").ap()
    lt_h = nc.dram_tensor("lt", [128, 128], BF16, kind="ExternalInput").ap()
    ones_c_h = nc.dram_tensor("ones_c", [128, 1], BF16, kind="ExternalInput").ap()
    ones_r_h = nc.dram_tensor("ones_r", [1, 128], F32, kind="ExternalInput").ap()
    eps_h = nc.dram_tensor("eps", [128, 1], F32, kind="ExternalInput").ap()
    zsrc_h = nc.dram_tensor("zsrc", [T, D], F32, kind="ExternalInput").ap()
    tidc_h = nc.dram_tensor("tidc", [128, 16], F32, kind="ExternalInput").ap()
    out_hbm = nc.dram_tensor("out", [TB * T, D], F32, kind="ExternalOutput").ap()
    xnd = [nc.dram_tensor("xnd%d" % i, [T, D], BF16, kind="Internal").ap()
           for i in range(2)]
    y12 = nc.dram_tensor("y12d", [T, D], F32, kind="Internal").ap()
    if dbg:
        dbg_xnT = nc.dram_tensor("dbg_xnT", [128, DC * T], BF16, kind="ExternalOutput").ap()
        dbg_yat = nc.dram_tensor("dbg_yat", [128, NT * D], F32, kind="ExternalOutput").ap()
        dbg_c = nc.dram_tensor("dbg_c", [128, NT * NE], F32, kind="ExternalOutput").ap()
        dbg_s = nc.dram_tensor("dbg_s", [1, 8], F32, kind="ExternalOutput").ap()
        dbg_idx = nc.dram_tensor("dbg_idx", [128, 40], F32, kind="ExternalOutput").ap()
        dbg_cg = nc.dram_tensor("dbg_cg", [128, 40], F32, kind="ExternalOutput").ap()
        dbg_y12 = nc.dram_tensor("dbg_y12", [T, D], F32, kind="ExternalOutput").ap()
        dbg_m = nc.dram_tensor("dbg_m", [128, 1280], F32, kind="ExternalOutput").ap()

    stack = ExitStack()
    with stack:
        xnT = stack.enter_context(nc.sbuf_tensor("xnT", [128, DC * T], BF16)).ap()
        y = stack.enter_context(nc.sbuf_tensor("y", [128, NT * D], F32)).ap()
        scr_t = stack.enter_context(nc.sbuf_tensor("scr", [128, 53248], BF16))
        scr = scr_t.ap()
        scrf = scr_t.ap().bitcast(F32)
        xnb_t = stack.enter_context(nc.sbuf_tensor("xnb_t", [128, 2048], BF16)).ap()

        identity = stack.enter_context(nc.sbuf_tensor("identity_s", [128, 128], BF16)).ap()
        ident32 = stack.enter_context(nc.sbuf_tensor("ident32_s", [128, 128], F32)).ap()
        maskd = stack.enter_context(nc.sbuf_tensor("maskd_s", [128, 128], F32)).ap()
        ones_c = stack.enter_context(nc.sbuf_tensor("ones_c_s", [128, 1], BF16)).ap()
        ones_r = stack.enter_context(nc.sbuf_tensor("ones_r_s", [1, 128], F32)).ap()
        eps_t = stack.enter_context(nc.sbuf_tensor("eps_t", [128, 1], F32)).ap()
        bqT = stack.enter_context(nc.sbuf_tensor("bqT_s", [128, DC], F32)).ap()
        bkT = stack.enter_context(nc.sbuf_tensor("bkT_s", [128, DC], F32)).ap()
        rw_s = stack.enter_context(nc.sbuf_tensor("rw_s", [128, DC * NE], F32)).ap()
        c_all = stack.enter_context(nc.sbuf_tensor("c_all", [128, NT * NE], F32)).ap()
        lg_all = stack.enter_context(nc.sbuf_tensor("lg_all", [128, NT * NE], F32)).ap()
        l2_all = stack.enter_context(nc.sbuf_tensor("l2_all", [128, NT * NE], F32)).ap()
        rts = stack.enter_context(nc.sbuf_tensor("rts", [128, 48], F32)).ap()
        nacc = rts[0:1, 0:16]
        zacc = rts[0:1, 16:32]
        s1v = rts[0:1, 32:40]
        sca = rts[:, 40:42]

        stats = l2_all[:, 0:32]

        pb = [stack.enter_context(nc.psum_tensor("pb%d" % i, [128, 512], F32)).ap()
              for i in range(8)]
        pbh = [p.bitcast(BF16) for p in pb]

        kT = scr[:, 0:16384]
        v_s = scr[:, 16384:32768]
        w0 = scr[:, 32768:40960]
        w1 = scr[:, 40960:49152]
        qT = scr[:, 49152:53248]
        pt = [scr[:, 40960:41216], scr[:, 41216:41472]]
        sqb = [scr[:, 41472:41984], scr[:, 41984:42496]]
        xst = [scrf[:, 24576:25600], scrf[:, 25600:26624]]
        xr = [scrf[:, 24576:25600], scrf[:, 25600:26624],
              scrf[:, 21504:22528], scrf[:, 22528:23552], scrf[:, 23552:24576]]
        bvb_v = scrf[:, 24576:25600]
        xnb = [xnb_t[:, 0:1024], xnb_t[:, 1024:2048]]
        scri = scr_t.ap().bitcast(I32)
        CAP = 640
        NSL = 5
        gbuf = scr[:, 0:8192]
        ubuf = scr[:, 8192:16384]
        dbuf = scr[:, 16384:24576]
        hid = scr[:, 24576:29696]
        oh_s = scrf[:, 12288:13312]
        xeT = scr[:, 29696:34816]
        silu = [scrf[:, 17408:17920], scrf[:, 17920:18432]]
        xe_tm = [scr[:, 36864:37888], scr[:, 37888:38912]]
        idxf = scrf[:, 19456:19496]
        cg_s = scrf[:, 19496:19536]
        idxi = scri[:, 19536:19576]
        acc3 = scrf[:, 19576:19579]
        rhs48 = scrf[:, 19584:19632]
        pos_s = scrf[:, 19632:19648]
        m_s = scrf[:, 19648:19664]
        sc_s = scrf[:, 19664:19680]
        b_s = scrf[:, 19680:19681]
        rt_b = scr[:, 39424:39425]
        tidc = scrf[:, 19696:19712]
        stage = [scrf[:, 20480:21504], scrf[:, 21504:22528]]
        iota_f = scrf[:, 20480:21120]
        iota_i = scri[:, 21504:22144]
        lt_s = scr[:, 44288:44416]
        xn32 = scrf[:, 22528:23552]
        xnT32 = scrf[:, 23552:24576]

        ops = {e: [] for e in ENGS}
        sems = {}
        cnt = {}
        for nm in ("sC", "sX0", "sX1", "sW0", "sW1", "sGU", "sWD",
                   "sO0", "sO1", "sT", "sA", "sV", "sP",
                   "sR0", "sR1",
                   "sQ0", "sQ1", "sQ2", "sQ3",
                   "sXN0", "sXN1", "sZ", "sZA0", "sZA1", "sSC", "sGA0", "sGA1"):
            sems[nm] = stack.enter_context(nc.semaphore(nm))
            cnt[nm] = 0

        def add(e, fn):
            ops[e].append(fn)

        def wait(e, snm, val):
            if val > 0:
                add(e, lambda eng, s=sems[snm], v=val: eng.wait_ge(s, v))

        def inc(e, snm, fn, k=1):
            cnt[snm] += k
            add(e, lambda eng, f=fn, s=sems[snm], kk=k: f(eng).then_inc(s, kk))
            return cnt[snm]

        def emit(e, fn):
            add(e, fn)

        for dst, csrc in ((identity, ident_h), (ident32, ident32_h), (maskd, maskd_h),
                          (ones_c, ones_c_h), (ones_r, ones_r_h), (eps_t, eps_h),
                          (bqT, bqT_h), (bkT, bkT_h)):
            inc("sync", "sC", lambda e, d=dst, s=csrc: e.dma_start(out=d, in_=s), 16)
        inc("sync", "sC",
            lambda e: e.dma_start(out=rw_s.rearrange("p (c e) -> p c e", c=DC),
                                  in_=rw_h.rearrange("(c p) e -> p c e", p=128)), 16)
        n_const = cnt["sC"]
        wait("scalar", "sC", n_const)
        wait("vector", "sC", n_const)
        wait("tensor", "sC", n_const)

        end_prev = {"sT": 0, "sA": 0, "sV": 0, "sO0": 0, "sO1": 0, "hid": 0,
                    "sSC": 0, "sZA0": 0, "sZA1": 0}
        resc_last = 0

        for b in range(TB):
            B0 = b * T

            xcast = [0] * NT
            xn_st = [0] * NT
            xnT_cp = [0] * NT
            tr_done = [0] * NT
            f32tr_done = [0] * NT
            xn32cp = [0] * NT
            rt_mm = [0] * NT
            rt_ev = [0] * NT
            for i in range(NT):
                p = i % 2
                o = p * 16
                MEAN, VAR, STD, RSTD, NB = o + 12, o + 13, o + 14, o + 15, o + 0
                if i >= 2:
                    wait("sync", "sA", xcast[i - 2])
                elif b > 0:
                    wait("sync", "sV", resc_last)
                sxp = "sX%d" % p
                nx = inc("sync", sxp,
                         lambda e, d=xst[p], s=x_hbm[B0 + i * 128:B0 + (i + 1) * 128, :]:
                         e.dma_start(out=d, in_=s), 16)
                wait("vector", sxp, nx)
                emit("vector", lambda e, d=stats[:, o:o + 6], s=xst[p][:, 0:512]:
                     e.bn_stats(d, s))
                wbn = inc("vector", "sV",
                          lambda e, d=stats[:, o + 6:o + 12], s=xst[p][:, 512:1024]:
                          e.bn_stats(d, s))
                wait("vector", "sV", wbn)
                wag = inc("vector", "sV",
                          lambda e, d=stats[:, MEAN:MEAN + 2], s=stats[:, o:o + 12]:
                          e.bn_aggr(d, s))
                wait("scalar", "sV", wag)
                wstd = inc("scalar", "sA",
                           lambda e, d=stats[:, STD:STD + 1], s=stats[:, VAR:VAR + 1]:
                           e.activation(out=d, in_=s, func=AF.Sqrt, bias=eps_t))
                wait("vector", "sA", wstd)
                wrs = inc("vector", "sV",
                          lambda e, d=stats[:, RSTD:RSTD + 1], s=stats[:, STD:STD + 1]:
                          e.reciprocal(out=d, in_=s))
                wait("vector", "sV", wrs)
                wnb = inc("vector", "sV",
                          lambda e, d=stats[:, NB:NB + 1], s=stats[:, MEAN:MEAN + 1],
                          r=stats[:, RSTD:RSTD + 1]:
                          e.scalar_tensor_tensor(out=d, in0=s, scalar=-1.0, in1=r,
                                                 op0=ALU.mult, op1=ALU.mult))
                wait("scalar", "sV", wnb)
                if i >= 1:
                    wait("scalar", "sT", f32tr_done[i - 1])
                a32 = inc("scalar", "sA",
                          lambda e, d=xn32, s=xst[p], sc=stats[:, RSTD:RSTD + 1],
                          bi=stats[:, NB:NB + 1]:
                          e.activation(out=d, in_=s, func=AF.Identity, scale=sc, bias=bi))
                wait("scalar", "sA", a32)
                if i >= 2:
                    wait("scalar", "sT", tr_done[i - 2])
                    wait("scalar", "sXN%d" % p, xn_st[i - 2])
                xcast[i] = inc("scalar", "sA",
                               lambda e, d=xnb[p], s=xn32:
                               e.activation(out=d, in_=s, func=AF.Copy))
                wait("sync", "sA", xcast[i])
                xn_st[i] = inc("sync", "sXN%d" % p,
                               lambda e, d=xnd[b % 2][i * 128:(i + 1) * 128, :],
                               s=xnb[p]: e.dma_start(out=d, in_=s), 16)
                wait("tensor", "sA", xcast[i])
                if i < 2:
                    if b > 0:
                        wait("tensor", "sA", end_prev["sA"])
                else:
                    wait("tensor", "sV", xnT_cp[i - 2])
                for dc in range(DC):
                    fn = (lambda e, d=pbh[p][:, dc * 128:(dc + 1) * 128],
                          s=xnb[p][:, dc * 128:(dc + 1) * 128]:
                          e.transpose(d, s, identity))
                    if dc == DC - 1:
                        tr_done[i] = inc("tensor", "sT", fn)
                    else:
                        emit("tensor", fn)
                if i > 0:
                    wait("tensor", "sV", xn32cp[i - 1])
                elif b > 0:
                    wait("tensor", "sV", end_prev["hid"])
                for dc in range(DC):
                    fn = (lambda e,
                          d=pb[2 + dc // 4][:, (dc % 4) * 128:((dc % 4) + 1) * 128],
                          s=xn32[:, dc * 128:(dc + 1) * 128]:
                          e.transpose(d, s, ident32))
                    if dc == DC - 1:
                        f32tr_done[i] = inc("tensor", "sT", fn)
                    else:
                        emit("tensor", fn)
                wait("vector", "sT", tr_done[i])
                xnT_cp[i] = inc("vector", "sV",
                                lambda e,
                                d=xnT.rearrange("p (c t) -> p c t", c=DC)[:, :, i * 128:(i + 1) * 128],
                                s=pbh[p].rearrange("p (c t) -> p c t", c=DC):
                                e.tensor_copy(out=d, in_=s))
                wait("vector", "sT", f32tr_done[i])
                if i > 0:
                    wait("vector", "sT", rt_mm[i - 1])
                emit("vector",
                     lambda e: e.tensor_copy(out=xnT32[:, 0:512], in_=pb[2][:, 0:512]))
                xn32cp[i] = inc("vector", "sV",
                                lambda e: e.tensor_copy(out=xnT32[:, 512:1024],
                                                        in_=pb[3][:, 0:512]))
                lb = 4 + (i % 2)
                wait("tensor", "sV", xn32cp[i])
                if i > 1:
                    wait("tensor", "sA", rt_ev[i - 2])
                elif b > 0:
                    wait("tensor", "sV", end_prev["sV"])
                for kc in range(DC):
                    fn = (lambda e, o_=pb[lb][:, 0:8],
                          l=xnT32[:, kc * 128:(kc + 1) * 128],
                          r=rw_s[:, kc * 8:(kc + 1) * 8],
                          st=(kc == 0), sp=(kc == DC - 1):
                          e.matmul(o_, l, r, start=st, stop=sp))
                    if kc == DC - 1:
                        rt_mm[i] = inc("tensor", "sT", fn)
                    else:
                        emit("tensor", fn)
                if i > 0:
                    wait("scalar", "sT", rt_mm[i - 1])
                    rt_ev[i - 1] = inc("scalar", "sA",
                                       lambda e, d=lg_all[:, (i - 1) * 8:i * 8],
                                       s=pb[4 + ((i - 1) % 2)][:, 0:8]:
                                       e.activation(out=d, in_=s, func=AF.Copy))
                if i == NT - 1:
                    wait("scalar", "sT", rt_mm[i])
                    rt_ev[i] = inc("scalar", "sA",
                                   lambda e, d=lg_all[:, i * 8:(i + 1) * 8],
                                   s=pb[lb][:, 0:8]:
                                   e.activation(out=d, in_=s, func=AF.Copy))

            lg3 = lg_all.rearrange("p (t e) -> p t e", t=NT)
            l23 = l2_all.rearrange("p (t e) -> p t e", t=NT)
            c3 = c_all.rearrange("p (t e) -> p t e", t=NT)
            def rbc(col):
                return rts[:, col:col + 16].rearrange("p (t o) -> p t o", o=1) \
                          .broadcast_to([128, NT, NE])
            wait("vector", "sA", rt_ev[NT - 1])
            wv_ = inc("vector", "sV",
                      lambda e: e.tensor_reduce(rts[:, 0:16], lg3, axis=AX.X, op=ALU.max))
            wait("vector", "sV", wv_)
            wv_ = inc("vector", "sV",
                      lambda e: e.tensor_tensor(out=c3, in0=lg3, in1=rbc(0), op=ALU.is_equal))
            wait("vector", "sV", wv_)
            wv_ = inc("vector", "sV",
                      lambda e: e.scalar_tensor_tensor(out=l23, in0=c3, scalar=NEG,
                                                       in1=lg3, op0=ALU.mult, op1=ALU.add))
            wait("vector", "sV", wv_)
            wv_ = inc("vector", "sV",
                      lambda e: e.tensor_reduce(rts[:, 16:32], l23, axis=AX.X, op=ALU.max))
            wait("vector", "sV", wv_)
            wv_ = inc("vector", "sV",
                      lambda e: e.tensor_sub(rts[:, 32:48], rts[:, 16:32], rts[:, 0:16]))
            wait("scalar", "sV", wv_)
            wa_ = inc("scalar", "sA",
                      lambda e: e.activation(out=rts[:, 0:16], in_=rts[:, 32:48],
                                             func=AF.Sigmoid, scale=-1.0))
            wait("scalar", "sA", wa_)
            wa_ = inc("scalar", "sA",
                      lambda e: e.activation(out=rts[:, 32:48], in_=rts[:, 32:48],
                                             func=AF.Sigmoid))
            wait("vector", "sA", wa_)
            wv_ = inc("vector", "sV",
                      lambda e: e.tensor_tensor(out=c3, in0=c3, in1=rbc(0), op=ALU.mult))
            wait("vector", "sV", wv_)
            wv_ = inc("vector", "sV",
                      lambda e: e.tensor_tensor(out=l23, in0=l23, in1=rbc(16), op=ALU.is_equal))
            wait("vector", "sV", wv_)
            wv_ = inc("vector", "sV",
                      lambda e: e.tensor_tensor(out=l23, in0=l23, in1=rbc(32), op=ALU.mult))
            wait("vector", "sV", wv_)
            c_rdy = inc("vector", "sV",
                        lambda e: e.tensor_add(c_all, c_all, l2_all))

            if dbg and b == 0:
                wait("sync", "sV", xnT_cp[NT - 1])
                inc("sync", "sC", lambda e: e.dma_start(out=dbg_xnT, in_=xnT), 16)
                wait("sync", "sV", c_rdy)
                inc("sync", "sC", lambda e: e.dma_start(out=dbg_c, in_=c_all), 16)

            wait("sync", "sA", xcast[NT - 1])
            bvb_ld = inc("sync", "sC",
                         lambda e: e.dma_start(out=bvb_v, in_=bvb_h), 16)

            if b > 0:
                wait("gpsimd", "sT", end_prev["sT"])
                wait("gpsimd", "sSC", end_prev["sSC"])
            wk_ld = inc("gpsimd", "sW0",
                        lambda e: e.dma_start(
                            out=w0.rearrange("p (c d) -> p c d", c=DC),
                            in_=wk_h.rearrange("(c p) d -> p c d", p=128)), 16)
            wait("gpsimd", "sT", rt_mm[NT - 1])
            wait("gpsimd", "sA", xcast[NT - 1])
            if b > 0:
                wait("gpsimd", "sZA0", end_prev["sZA0"])
                wait("gpsimd", "sZA1", end_prev["sZA1"])
            wv_ld = inc("gpsimd", "sW1",
                        lambda e: e.dma_start(
                            out=w1.rearrange("p (c d) -> p c d", c=DC),
                            in_=wv_h.rearrange("(c p) d -> p c d", p=128)), 16)

            kT_ev = {}
            g_ev = [0, 0, 0, 0]
            gi = 0
            last_kT_mm = 0
            first_kt = True
            for dt in range(DC):
                for tc4 in range(4):
                    g = gi % 4
                    gi += 1
                    wait("tensor", "sW0", wk_ld)
                    wait("tensor", "sV", xnT_cp[tc4 * 4 + 3])
                    if first_kt:
                        wait("tensor", "sV", xn32cp[NT - 1])
                        first_kt = False
                    if g_ev[g]:
                        wait("tensor", "sA", g_ev[g])
                    for kc in range(DC):
                        fn = (lambda e, o_=pb[g][:, 0:512],
                              l=w0[:, kc * 1024 + dt * 128: kc * 1024 + (dt + 1) * 128],
                              r=xnT[:, kc * 2048 + tc4 * 512: kc * 2048 + (tc4 + 1) * 512],
                              st=(kc == 0), sp=(kc == DC - 1):
                              e.matmul(o_, l, r, start=st, stop=sp))
                        if kc == DC - 1:
                            last_kT_mm = inc("tensor", "sT", fn)
                        else:
                            emit("tensor", fn)
                    wait("scalar", "sT", last_kT_mm)
                    g_ev[g] = inc("scalar", "sA",
                                  lambda e,
                                  d=kT[:, dt * 2048 + tc4 * 512: dt * 2048 + (tc4 + 1) * 512],
                                  s=pb[g][:, 0:512], bi=bkT[:, dt:dt + 1]:
                                  e.activation(out=d, in_=s, func=AF.Identity, bias=bi))
                    kT_ev[(dt, tc4)] = g_ev[g]
            kT_done_A = cnt["sA"]

            wait("gpsimd", "sT", last_kT_mm)
            wq_ld = inc("gpsimd", "sW0",
                        lambda e: e.dma_start(
                            out=w0.rearrange("p (c d) -> p c d", c=DC),
                            in_=wq_h.rearrange("(c p) d -> p c d", p=128)), 16)

            g_evv = [0, 0, 0, 0]
            gi = 0
            last_v_mm = 0
            first_v = True
            for st_ in range(NT):
                for dh in range(2):
                    g = gi % 4
                    gi += 1
                    wait("tensor", "sW1", wv_ld)
                    wait("tensor", "sV", xnT_cp[st_])
                    if first_v:
                        wait("tensor", "sA", kT_done_A)
                    if g_evv[g]:
                        wait("tensor", "sV", g_evv[g])
                    for kc in range(DC):
                        fn = (lambda e, o_=pb[g][:, 0:512],
                              l=xnT[:, kc * 2048 + st_ * 128: kc * 2048 + (st_ + 1) * 128],
                              r=w1[:, kc * 1024 + dh * 512: kc * 1024 + (dh + 1) * 512],
                              st=(kc == 0), sp=(kc == DC - 1):
                              e.matmul(o_, l, r, start=st, stop=sp))
                        if kc == DC - 1:
                            last_v_mm = inc("tensor", "sT", fn)
                        else:
                            emit("tensor", fn)
                    wait("vector", "sT", last_v_mm)
                    if first_v:
                        wait("vector", "sC", bvb_ld)
                        first_v = False
                    g_evv[g] = inc("vector", "sV",
                                   lambda e,
                                   d=v_s[:, st_ * 1024 + dh * 512: st_ * 1024 + (dh + 1) * 512],
                                   s=pb[g][:, 0:512], bi=bvb_v[:, dh * 512:(dh + 1) * 512]:
                                   e.tensor_tensor(out=d, in0=s, in1=bi, op=ALU.add))
            v_done_V = cnt["sV"]

            sq_mmT = 0
            kn_ev = 0
            wait("scalar", "sT", last_v_mm)
            if b > 0:
                wait("scalar", "sSC", end_prev["sSC"])
                wait("scalar", "sZA0", end_prev["sZA0"])
                wait("scalar", "sZA1", end_prev["sZA1"])
            for j in range(4):
                for dt in range(DC):
                    jj = dt % 2
                    wait("scalar", "sA", kT_ev[(dt, j)])
                    if sq_mmT:
                        wait("scalar", "sT", sq_mmT)
                    a_k = inc("scalar", "sA",
                              lambda e, d=sqb[jj],
                              s=kT[:, dt * 2048 + j * 512: dt * 2048 + (j + 1) * 512]:
                              e.square(out=d, in_=s))
                    wait("tensor", "sA", a_k)
                    if j == 0 and dt == 0:
                        wait("tensor", "sA", rt_ev[NT - 1])
                    if dt == 0 and kn_ev:
                        wait("tensor", "sV", kn_ev)
                    sq_mmT = inc("tensor", "sT",
                                 lambda e, r=sqb[jj], st=(dt == 0), sp=(dt == DC - 1):
                                 e.matmul(pb[4][0:1, 0:512], ones_c, r, start=st, stop=sp))
                wait("vector", "sT", sq_mmT)
                kn_ev = inc("vector", "sV",
                            lambda e, d=nacc[0:1, 8 + j:9 + j]:
                            e.reduce_max(d, pb[4][0:1, 0:512], axis=AX.X))

            def qT_proj(cc, qb_wA, qb_wV, scalar_pre_V=0):
                evA = 0
                lastT = 0
                first = True
                for dt in range(DC):
                    g = 6 + dt % 2
                    wait("tensor", "sW0", wq_ld)
                    if qb_wA[dt % 2]:
                        wait("tensor", "sA", qb_wA[dt % 2])
                    if qb_wV[dt % 2]:
                        wait("tensor", "sV", qb_wV[dt % 2])
                        qb_wV[dt % 2] = 0
                    for kc in range(DC):
                        fn = (lambda e, o_=pb[g][:, 0:512],
                              l=w0[:, kc * 1024 + dt * 128: kc * 1024 + (dt + 1) * 128],
                              r=xnT[:, kc * 2048 + cc * 512: kc * 2048 + (cc + 1) * 512],
                              st=(kc == 0), sp=(kc == DC - 1):
                              e.matmul(o_, l, r, start=st, stop=sp))
                        if kc == DC - 1:
                            lastT = inc("tensor", "sT", fn)
                        else:
                            emit("tensor", fn)
                    wait("scalar", "sT", lastT)
                    if first and scalar_pre_V:
                        wait("scalar", "sV", scalar_pre_V)
                        first = False
                    evA = inc("scalar", "sA",
                              lambda e, d=qT[:, dt * 512:(dt + 1) * 512],
                              s=pb[g][:, 0:512], bi=bqT[:, dt:dt + 1]:
                              e.activation(out=d, in_=s, func=AF.Identity, bias=bi))
                    qb_wA[dt % 2] = evA
                return evA, lastT

            qb_wA = [0, 0]
            qn_ev = 0
            for cc in range(4):
                q_evA, _ = qT_proj(cc, qb_wA, [0, 0],
                                   scalar_pre_V=(v_done_V if cc == 0 else 0))
                for dt in range(DC):
                    jj = dt % 2
                    wait("scalar", "sA", q_evA)
                    if sq_mmT:
                        wait("scalar", "sT", sq_mmT)
                    a_q = inc("scalar", "sA",
                              lambda e, d=sqb[jj], s=qT[:, dt * 512:(dt + 1) * 512]:
                              e.square(out=d, in_=s))
                    wait("tensor", "sA", a_q)
                    if cc == 0 and dt == 0:
                        wait("tensor", "sV", kn_ev)
                    if dt == 0 and qn_ev:
                        wait("tensor", "sV", qn_ev)
                    sq_mmT = inc("tensor", "sT",
                                 lambda e, r=sqb[jj], st=(dt == 0), sp=(dt == DC - 1):
                                 e.matmul(pb[5][0:1, 0:512], ones_c, r, start=st, stop=sp))
                wait("vector", "sT", sq_mmT)
                qn_ev = inc("vector", "sV",
                            lambda e, d=nacc[0:1, cc:cc + 1]:
                            e.reduce_max(d, pb[5][0:1, 0:512], axis=AX.X))

            wait("vector", "sV", qn_ev)
            wv_ = inc("vector", "sV",
                      lambda e: e.reduce_max(s1v[0:1, 0:1], nacc[0:1, 0:4], axis=AX.X))
            wait("vector", "sV", wv_)
            wv_ = inc("vector", "sV",
                      lambda e: e.reduce_max(s1v[0:1, 1:2], nacc[0:1, 8:12], axis=AX.X))
            wait("scalar", "sV", wv_)
            wa_ = inc("scalar", "sA",
                      lambda e: e.activation(out=s1v[0:1, 2:3], in_=s1v[0:1, 0:1], func=AF.Sqrt))
            wait("scalar", "sA", wa_)
            wa_ = inc("scalar", "sA",
                      lambda e: e.activation(out=s1v[0:1, 3:4], in_=s1v[0:1, 1:2], func=AF.Sqrt))
            wait("vector", "sA", wa_)
            wv_ = inc("vector", "sV",
                      lambda e: e.scalar_tensor_tensor(out=s1v[0:1, 4:5], in0=s1v[0:1, 2:3],
                                                       scalar=-1.0 / 32.0, in1=s1v[0:1, 3:4],
                                                       op0=ALU.mult, op1=ALU.mult))
            wait("tensor", "sV", wv_)
            bc_mm = inc("tensor", "sT",
                        lambda e: e.matmul(pb[5][:, 0:1], ones_r, s1v[0:1, 4:5],
                                           start=True, stop=True))
            wait("vector", "sT", bc_mm)
            negC_rdy = inc("vector", "sV",
                           lambda e: e.tensor_copy(out=sca[:, 0:1], in_=pb[5][:, 0:1]))

            out_ev = [0, 0, 0, 0]
            sc_exA = [0, 0]
            pt_rdT = [0, 0]
            zr_evV = 0
            last_pv_mm = 0
            for cc in range(4):
                qb_wV2 = [zr_evV, zr_evV]
                q_evA, _ = qT_proj(cc, qb_wA, qb_wV2)
                for c2 in range(2):
                    c = 2 * cc + c2
                    toff = c2 * 256
                    nk = 2 * c + 2
                    for k in range(nk):
                        sb_ = 4 + (k % 2)
                        wait("tensor", "sA", q_evA)
                        if sc_exA[k % 2]:
                            wait("tensor", "sA", sc_exA[k % 2])
                        elif k % 2 == 0 and cc == 0 and c2 == 0:
                            wait("tensor", "sV", kn_ev)
                        elif k % 2 == 1 and cc == 0 and c2 == 0:
                            wait("tensor", "sV", negC_rdy)
                        if k == 0:
                            for gg in range(4):
                                if out_ev[gg]:
                                    wait("tensor", "sV", out_ev[gg])
                            if b > 0 and cc == 0 and c2 == 0:
                                wait("tensor", "sO0", end_prev["sO0"])
                                wait("tensor", "sO1", end_prev["sO1"])
                        lastS = 0
                        for dt in range(DC):
                            fn = (lambda e, o_=pb[sb_][:, 0:256],
                                  l=kT[:, dt * 2048 + k * 128: dt * 2048 + (k + 1) * 128],
                                  r=qT[:, dt * 512 + toff: dt * 512 + toff + 256],
                                  st=(dt == 0), sp=(dt == DC - 1):
                                  e.matmul(o_, l, r, start=st, stop=sp))
                            if dt == DC - 1:
                                lastS = inc("tensor", "sT", fn)
                            else:
                                emit("tensor", fn)
                        mk = 0
                        if k == 2 * c:
                            wait("vector", "sT", lastS)
                            mk = inc("vector", "sV",
                                     lambda e, o_=pb[sb_][:, 0:128]:
                                     e.tensor_tensor(out=o_, in0=o_, in1=maskd, op=ALU.add))
                        elif k == 2 * c + 1:
                            wait("vector", "sT", lastS)
                            emit("vector",
                                 lambda e, o_=pb[sb_][:, 0:128]: e.memset(o_, NEG))
                            mk = inc("vector", "sV",
                                     lambda e, o_=pb[sb_][:, 128:256]:
                                     e.tensor_tensor(out=o_, in0=o_, in1=maskd, op=ALU.add))
                        if mk:
                            wait("scalar", "sV", mk)
                        else:
                            wait("scalar", "sT", lastS)
                        if pt_rdT[k % 2]:
                            wait("scalar", "sT", pt_rdT[k % 2])
                        ex = inc("scalar", "sA",
                                 lambda e, d=pt[k % 2], s=pb[sb_][:, 0:256]:
                                 e.activation(out=d, in_=s, func=AF.Exp,
                                              scale=1.0 / 32.0, bias=sca[:, 0:1]))
                        sc_exA[k % 2] = ex
                        wait("tensor", "sA", ex)
                        if k == 0 and zr_evV:
                            wait("tensor", "sV", zr_evV)
                        emit("tensor",
                             lambda e, o_=pb[6][0:1, toff:toff + 256], r=pt[k % 2],
                             st=(k == 0), sp=(k == nk - 1):
                             e.matmul(o_, ones_c, r, start=st, stop=sp))
                        if cc == 0 and c2 == 0 and k == 0:
                            wait("tensor", "sV", v_done_V)
                        for sub in range(2):
                            gsub = 2 * c + sub
                            if k > gsub:
                                continue
                            for dh in range(2):
                                gg = sub * 2 + dh
                                fn = (lambda e, o_=pb[gg][:, 0:512],
                                      l=pt[k % 2][:, sub * 128:(sub + 1) * 128],
                                      r=v_s[:, k * 1024 + dh * 512: k * 1024 + (dh + 1) * 512],
                                      st=(k == 0), sp=(k == 2 * c + sub):
                                      e.matmul(o_, l, r, start=st, stop=sp))
                                if sub == 1 and dh == 1:
                                    pt_rdT[k % 2] = inc("tensor", "sT", fn)
                                else:
                                    emit("tensor", fn)
                        last_pv_mm = pt_rdT[k % 2]
                    for sub in range(2):
                        for dh in range(2):
                            gg = sub * 2 + dh
                            wait("vector", "sT", last_pv_mm)
                            out_ev[gg] = inc("vector", "sV",
                                             lambda e,
                                             d=y[:, (2 * c + sub) * 1024 + dh * 512:
                                                 (2 * c + sub) * 1024 + (dh + 1) * 512],
                                             s=pb[gg][:, 0:512]:
                                             e.tensor_copy(out=d, in_=s))
                    wait("vector", "sT", last_pv_mm)
                    zr_evV = inc("vector", "sV",
                                 lambda e, d=zacc[0:1, c:c + 1],
                                 s=pb[6][0:1, toff:toff + 256]:
                                 e.reduce_sum(d, s, axis=AX.X))

            wait("vector", "sV", zr_evV)
            wv_ = inc("vector", "sV",
                      lambda e: e.reduce_sum(s1v[0:1, 5:6], zacc[0:1, 0:8], axis=AX.X))
            wait("vector", "sV", wv_)
            wv_ = inc("vector", "sV",
                      lambda e: e.reciprocal(out=s1v[0:1, 6:7], in_=s1v[0:1, 5:6]))
            wait("tensor", "sV", wv_)
            wait("tensor", "sA", sc_exA[0])
            wait("tensor", "sA", sc_exA[1])
            bc2 = inc("tensor", "sT",
                      lambda e: e.matmul(pb[4][:, 0:1], ones_r, s1v[0:1, 6:7],
                                         start=True, stop=True))
            wait("vector", "sT", bc2)
            zinv_rdy = inc("vector", "sV",
                           lambda e: e.tensor_copy(out=sca[:, 1:2], in_=pb[4][:, 0:1]))

            def load_gu(e_, h2_, wait_T_gu):
                if wait_T_gu:
                    wait("gpsimd", "sT", wait_T_gu)
                inc("gpsimd", "sGU",
                    lambda e, ee=e_, hh=h2_: e.dma_start(
                        out=gbuf.rearrange("p (c h) -> p c h", c=DC),
                        in_=gw_h[ee].rearrange("(c p) h -> p c h", p=128)[:, :, hh * HH:(hh + 1) * HH]), 16)
                return inc("gpsimd", "sGU",
                           lambda e, ee=e_, hh=h2_: e.dma_start(
                               out=ubuf.rearrange("p (c h) -> p c h", c=DC),
                               in_=uw_h[ee].rearrange("(c p) h -> p c h", p=128)[:, :, hh * HH:(hh + 1) * HH]), 16)

            def load_d(e_, h2_, wait_T_d):
                if wait_T_d:
                    wait("gpsimd", "sT", wait_T_d)
                return inc("gpsimd", "sWD",
                           lambda e, ee=e_, hh=h2_: e.dma_start(
                               out=dbuf.rearrange("p (c d) -> p c d", c=DC),
                               in_=dw_h[ee][hh * HH:(hh + 1) * HH, :].rearrange("(c p) d -> p c d", p=128)), 16)

            wait("gpsimd", "sT", last_pv_mm)
            ph0_half = {}
            for snm, th, lo in (("sQ0", gw_h, 0), ("sQ2", uw_h, 0),
                                ("sQ1", gw_h, 4), ("sQ3", uw_h, 4)):
                dst = gbuf if th is gw_h else ubuf
                ph0_half[snm] = inc("gpsimd", snm,
                                    lambda e, d=dst, t=th, l_=lo: e.dma_start(
                                        out=d.rearrange("p (c h) -> p c h", c=DC)[:, l_:l_ + 4, :],
                                        in_=t[0].rearrange("(c p) h -> p c h", p=128)[:, l_:l_ + 4, 0:HH]), 16)
            gu_ld = 0
            d_ld = inc("gpsimd", "sWD",
                       lambda e: e.dma_start(
                           out=dbuf.rearrange("p (c d) -> p c d", c=DC),
                           in_=dw_h[0][0:HH, :].rearrange("(c p) d -> p c d", p=128)), 16)

            resc_j = [0] * NT
            resc_nx = [0] * NT
            fin_st = [0, 0]

            def emit_resc_dma(j):
                p = j % 2
                srp = "sR%d" % p
                if j < 2:
                    wait("sync", "sT", last_pv_mm)
                else:
                    wait("sync", "sV", resc_j[j - 2])
                resc_nx[j] = inc("sync", srp,
                                 lambda e, d=xst[p], s=x_hbm[B0 + j * 128:B0 + (j + 1) * 128, :]:
                                 e.dma_start(out=d, in_=s), 16)

            def emit_resc_stt(j):
                p = j % 2
                wait("vector", "sR%d" % p, resc_nx[j])
                resc_j[j] = inc("vector", "sV",
                                lambda e, d=y[:, j * 1024:(j + 1) * 1024], x_=xst[p]:
                                e.scalar_tensor_tensor(out=d, in0=d, scalar=sca[:, 1:2],
                                                       in1=x_, op0=ALU.mult, op1=ALU.add))
                if j + 2 < NT:
                    emit_resc_dma(j + 2)
                wait("sync", "sV", resc_j[j])
                fin_st[j % 2] = inc("sync", "sO%d" % (j % 2),
                                    lambda e, d=out_hbm[B0 + j * 128:B0 + (j + 1) * 128, :],
                                    s=y[:, j * 1024:(j + 1) * 1024]:
                                    e.dma_start(out=d, in_=s), 16)

            for j in range(2):
                emit_resc_dma(j)

            if dbg and b == 0:
                wait("sync", "sV", zinv_rdy)
                dbg_sX = inc("sync", "sC", lambda e: e.dma_start(out=dbg_s, in_=s1v), 16)

            if dbg and b == 0:
                wait("gpsimd", "sC", dbg_sX)
                wait("tensor", "sC", dbg_sX)

            if b > 0:
                wait("sync", "sZA0", end_prev["sZA0"])
                wait("sync", "sZA1", end_prev["sZA1"])
            zr = inc("sync", "sZ", lambda e: e.dma_start(out=y12, in_=zsrc_h), 16)

            wait("sync", "sT", last_pv_mm)
            lt_ld = inc("sync", "sC", lambda e: e.dma_start(out=lt_s, in_=lt_h), 16)
            tid_ld = inc("sync", "sC", lambda e: e.dma_start(out=tidc, in_=tidc_h), 16)
            wait("tensor", "sC", tid_ld)
            wait("vector", "sC", tid_ld)
            wait("gpsimd", "sT", last_pv_mm)
            g_io = inc("gpsimd", "sP",
                       lambda e: e.iota(iota_i, pattern=[[1, CAP]], base=0,
                                        channel_multiplier=0))
            wait("vector", "sP", g_io)
            wait("vector", "sT", last_pv_mm)
            wv_ = inc("vector", "sV", lambda e: e.tensor_copy(out=iota_f, in_=iota_i))
            wait("vector", "sV", wv_)
            wait("tensor", "sA", qb_wA[1] if qb_wA[1] else qb_wA[0])
            idxi_done = [0] * NE
            xeT_ev = [0] * NE
            bcop = 0
            evac_last = 0
            mm_prev = 0
            for e_ in range(NE):
                wv_ = inc("vector", "sV",
                          lambda e, ee=e_: e.tensor_single_scalar(
                              out=m_s, in_=c_all.rearrange("p (t e) -> p t e", e=NE)[:, :, ee],
                              scalar=0.0, op=ALU.is_gt))
                wait("vector", "sV", wv_)
                wv_ = inc("vector", "sV",
                          lambda e: e.tensor_tensor_scan(out=sc_s, data0=m_s, data1=m_s,
                                                         initial=0.0, op0=ALU.add,
                                                         op1=ALU.bypass))
                wait("vector", "sV", wv_)
                wv_ = inc("vector", "sV", lambda e: e.tensor_copy(out=rt_b, in_=sc_s[:, 15:16]))
                wait("tensor", "sV", wv_)
                if bcop:
                    wait("tensor", "sV", evac_last)
                bmm = inc("tensor", "sT",
                          lambda e: e.matmul(pb[7][:, 0:1], lt_s, rt_b, start=True, stop=True))
                wait("vector", "sT", bmm)
                bcop = inc("vector", "sV", lambda e: e.tensor_copy(out=b_s, in_=pb[7][:, 0:1]))
                wait("vector", "sV", bcop)
                wv_ = inc("vector", "sV",
                          lambda e: e.tensor_scalar(out=pos_s, in0=sc_s, scalar1=b_s,
                                                    scalar2=-10000.0, op0=ALU.add, op1=ALU.add))
                wait("vector", "sV", wv_)
                wv_ = inc("vector", "sV", lambda e: e.tensor_mul(pos_s, pos_s, m_s))
                wait("vector", "sV", wv_)
                wv_ = inc("vector", "sV", lambda e: e.tensor_scalar_add(pos_s, pos_s, 9999.0))
                wait("vector", "sV", wv_)
                r3 = rhs48.rearrange("p (t c) -> p t c", c=3)
                wv_ = inc("vector", "sV", lambda e: e.memset(r3[:, :, 1:2], 1.0))
                wait("vector", "sV", wv_)
                wv_ = inc("vector", "sV",
                          lambda e: e.tensor_copy(out=r3[:, :, 0:1],
                                                  in_=tidc.rearrange("p (t o) -> p t o", o=1)))
                wait("vector", "sV", wv_)
                wv_ = inc("vector", "sV",
                          lambda e, ee=e_: e.tensor_copy(
                              out=r3[:, :, 2:3],
                              in_=c_all.rearrange("p (t e) -> p t e", e=NE)[:, :, ee:ee + 1]))
                rhs_rdy = wv_
                for qc in range(NSL):
                    mm_half = 0
                    for half in range(2):
                        ohv = 0
                        if half == 1:
                            wait("vector", "sT", mm_half)
                        elif mm_prev:
                            wait("vector", "sT", mm_prev)
                        for tl8 in range(8):
                            tl = half * 8 + tl8
                            ohv = inc("vector", "sV",
                                      lambda e, t_=tl, t8=tl8, q_=qc: e.tensor_scalar(
                                          out=oh_s[:, t8 * 128:(t8 + 1) * 128],
                                          in0=iota_f[:, q_ * 128:(q_ + 1) * 128],
                                          scalar1=pos_s[:, t_:t_ + 1], scalar2=None,
                                          op0=ALU.is_equal))
                        wait("tensor", "sV", ohv)
                        if half == 0 and (qc > 0 or e_ > 0):
                            wait("tensor", "sV", evac_last)
                        for tl8 in range(8):
                            tl = half * 8 + tl8
                            fn = (lambda e, t_=tl, t8=tl8: e.matmul(
                                pb[7][:, 4:7], oh_s[:, t8 * 128:(t8 + 1) * 128],
                                rhs48[:, t_ * 3:t_ * 3 + 3],
                                start=(t_ == 0), stop=(t_ == 15)))
                            if tl8 == 7:
                                mm_half = inc("tensor", "sT", fn)
                            else:
                                emit("tensor", fn)
                    mm_prev = mm_half
                    wait("vector", "sT", mm_half)
                    wv_ = inc("vector", "sV", lambda e: e.tensor_copy(out=acc3, in_=pb[7][:, 4:7]))
                    wait("vector", "sV", wv_)
                    col = e_ * NSL + qc
                    wv_ = inc("vector", "sV",
                              lambda e, c_=col: e.scalar_tensor_tensor(
                                  out=idxf[:, c_:c_ + 1], in0=acc3[:, 1:2], scalar=-2047.0,
                                  in1=acc3[:, 0:1], op0=ALU.mult, op1=ALU.add))
                    wait("vector", "sV", wv_)
                    wv_ = inc("vector", "sV",
                              lambda e, c_=col: e.tensor_scalar_add(
                                  idxf[:, c_:c_ + 1], idxf[:, c_:c_ + 1], 2047.0))
                    wait("vector", "sV", wv_)
                    evac_last = inc("vector", "sV",
                                    lambda e, c_=col: e.tensor_copy(out=cg_s[:, c_:c_ + 1],
                                                                    in_=acc3[:, 2:3]))
                wait("vector", "sV", evac_last)
                idxi_done[e_] = inc("vector", "sV",
                                    lambda e, ee=e_: e.tensor_copy(
                                        out=idxi[:, ee * NSL:(ee + 1) * NSL],
                                        in_=idxf[:, ee * NSL:(ee + 1) * NSL]))

            ga_ring = [0, 0]
            tr_prev = [0, 0]


            ga_pend = {}

            def emit_gather(e_, s):
                ring = s % 2
                wait("gpsimd", "sV", idxi_done[e_])
                wait("gpsimd", "sXN0", xn_st[NT - 2])
                wait("gpsimd", "sXN1", xn_st[NT - 1])
                if ga_ring[ring]:
                    wait("gpsimd", "sT", ga_ring[ring])
                ga_pend[(e_, s)] = inc(
                    "gpsimd", "sGA%d" % ring,
                    lambda e, ee=e_, s_=s, xb=xnd[b % 2]: e.indirect_dma_start(
                        out=xe_tm[s_ % 2], out_offset=None, in_=xb,
                        in_offset=IndirectOffsetOnAxis(
                            ap=idxi[:, ee * NSL + s_:ee * NSL + s_ + 1], axis=0)), 16)

            def emit_transpose(e_, s, bank_waitA, bank_waitV):
                ring = s % 2
                wait("tensor", "sGA%d" % ring, ga_pend[(e_, s)])
                if bank_waitA[ring]:
                    wait("tensor", "sA", bank_waitA[ring])
                    bank_waitA[ring] = 0
                if bank_waitV[ring]:
                    wait("tensor", "sV", bank_waitV[ring])
                    bank_waitV[ring] = 0
                if tr_prev[ring]:
                    wait("tensor", "sV", tr_prev[ring])
                trd = 0
                for dc in range(DC):
                    fn = (lambda e, d=pbh[ring][:, dc * 128:(dc + 1) * 128],
                          m_=xe_tm[ring][:, dc * 128:(dc + 1) * 128]:
                          e.transpose(d, m_, identity))
                    if dc == DC - 1:
                        trd = inc("tensor", "sT", fn)
                    else:
                        emit("tensor", fn)
                ga_ring[ring] = trd
                wait("vector", "sT", trd)
                evs = inc("vector", "sV",
                          lambda e, s_=s:
                          e.tensor_copy(
                              out=xeT.rearrange("p (c t) -> p c t", c=DC)[:, :, s_ * 128:(s_ + 1) * 128],
                              in_=pbh[s_ % 2].rearrange("p (c t) -> p c t", c=DC)))
                tr_prev[ring] = evs
                xeT_ev[e_] = evs
                if s + 2 < NSL:
                    emit_gather(e_, s + 2)

            def emit_xeT(e_, bank_waitA, bank_waitV):
                emit_gather(e_, 0)
                emit_gather(e_, 1)
                for s in range(NSL):
                    emit_transpose(e_, s, bank_waitA, bank_waitV)

            emit_xeT(0, [0, 0], [zinv_rdy, zinv_rdy])

            gate_bank_A = [0, 0]
            up_bank_V = [0, 0]
            silu_war_V = [0, 0]
            dn_ev_V = [0, 0, 0, 0]
            ffn_last_mm = 0
            ffn_hid_last = 0
            first_ffn_mm = True
            sc_prev = 0
            sc_stage = [0, 0]
            first_sc = True
            TCS = ((0, 512), (512, 128))
            for ph in range(16):
                e_, h2_ = ph // 2, ph % 2
                hid_last_V = 0
                last_gu_mm = 0
                resc_todo = list(range(NT)) if ph == 0 else []
                first_in_ph = True
                for tof, tsz in TCS:
                    for ht in range(NHT):
                        pgi = ht % 2
                        first_grp = (ph == 0 and tof == 0 and ht == 0)
                        if first_in_ph:
                            wait("tensor", "sV", xeT_ev[e_])
                            first_in_ph = False
                        if ph == 0 and tof == 0 and ht == 1:
                            wait("tensor", "sQ1", ph0_half["sQ1"])
                            wait("tensor", "sQ3", ph0_half["sQ3"])
                        if ph > 0:
                            wait("tensor", "sGU", gu_ld)
                        if first_ffn_mm:
                            wait("tensor", "sV", zinv_rdy)
                            first_ffn_mm = False
                        if gate_bank_A[pgi]:
                            wait("tensor", "sA", gate_bank_A[pgi])
                        mg = 0
                        for kc in range(DC):
                            if first_grp and kc == 0:
                                wait("tensor", "sQ0", ph0_half["sQ0"])
                            if first_grp and kc == 4:
                                wait("tensor", "sQ1", ph0_half["sQ1"])
                            fn = (lambda e, o_=pb[pgi][:, 0:tsz],
                                  l=gbuf[:, kc * 1024 + ht * 128: kc * 1024 + (ht + 1) * 128],
                                  r=xeT[:, kc * CAP + tof: kc * CAP + tof + tsz],
                                  st=(kc == 0), sp=(kc == DC - 1):
                                  e.matmul(o_, l, r, start=st, stop=sp))
                            if kc == DC - 1:
                                mg = inc("tensor", "sT", fn)
                            else:
                                emit("tensor", fn)
                        if up_bank_V[pgi]:
                            wait("tensor", "sV", up_bank_V[pgi])
                        mu = 0
                        for kc in range(DC):
                            if first_grp and kc == 0:
                                wait("tensor", "sQ2", ph0_half["sQ2"])
                            if first_grp and kc == 4:
                                wait("tensor", "sQ3", ph0_half["sQ3"])
                            fn = (lambda e, o_=pb[2 + pgi][:, 0:tsz],
                                  l=ubuf[:, kc * 1024 + ht * 128: kc * 1024 + (ht + 1) * 128],
                                  r=xeT[:, kc * CAP + tof: kc * CAP + tof + tsz],
                                  st=(kc == 0), sp=(kc == DC - 1):
                                  e.matmul(o_, l, r, start=st, stop=sp))
                            if kc == DC - 1:
                                mu = inc("tensor", "sT", fn)
                            else:
                                emit("tensor", fn)
                        last_gu_mm = mu
                        wait("scalar", "sT", mg)
                        if silu_war_V[pgi]:
                            wait("scalar", "sV", silu_war_V[pgi])
                        asl = inc("scalar", "sA",
                                  lambda e, d=silu[pgi][:, 0:tsz], s=pb[pgi][:, 0:tsz]:
                                  e.activation(out=d, in_=s, func=AF.Silu))
                        gate_bank_A[pgi] = asl
                        wait("vector", "sA", asl)
                        wait("vector", "sT", mu)
                        hv = inc("vector", "sV",
                                 lambda e,
                                 d=hid[:, ht * CAP + tof: ht * CAP + tof + tsz],
                                 s=silu[pgi][:, 0:tsz], u=pb[2 + pgi][:, 0:tsz]:
                                 e.tensor_tensor(out=d, in0=s, in1=u, op=ALU.mult))
                        up_bank_V[pgi] = hv
                        silu_war_V[pgi] = hv
                        hid_last_V = hv
                        if resc_todo:
                            emit_resc_stt(resc_todo.pop(0))
                if ph < 15:
                    gu_ld_next = load_gu((ph + 1) // 2, (ph + 1) % 2, last_gu_mm)
                if h2_ == 1 and e_ < NE - 1:
                    for s_g in range(2):
                        emit_gather(e_ + 1, s_g)
                last_dn_mm = 0
                tr_todo = list(range(NSL)) if (h2_ == 1 and e_ < NE - 1) else []
                for tile in range(NSL):
                    ring = tile % 2
                    evd = [0, 0]
                    for dh in range(2):
                        gg = 4 + (tile % 2) * 2 + dh
                        wait("tensor", "sWD", d_ld)
                        wait("tensor", "sV", hid_last_V)
                        if dn_ev_V[(tile % 2) * 2 + dh]:
                            wait("tensor", "sV", dn_ev_V[(tile % 2) * 2 + dh])
                        md = 0
                        for ht in range(NHT):
                            fn = (lambda e, o_=pb[gg][:, 0:512],
                                  l=hid[:, ht * CAP + tile * 128: ht * CAP + (tile + 1) * 128],
                                  r=dbuf[:, ht * 1024 + dh * 512: ht * 1024 + (dh + 1) * 512],
                                  st=(ht == 0), sp=(ht == NHT - 1):
                                  e.matmul(o_, l, r, start=st, stop=sp))
                            if ht == NHT - 1:
                                md = inc("tensor", "sT", fn)
                            else:
                                emit("tensor", fn)
                        last_dn_mm = md
                        wait("vector", "sT", md)
                        if sc_stage[ring]:
                            wait("vector", "sSC", sc_stage[ring])
                        evd[dh] = inc("vector", "sV",
                                      lambda e, rg=ring, d_=dh,
                                      s=pb[gg][:, 0:512],
                                      cs=cg_s[:, e_ * NSL + tile:e_ * NSL + tile + 1]:
                                      e.tensor_scalar_mul(stage[rg][:, d_ * 512:(d_ + 1) * 512],
                                                          s, cs))
                        dn_ev_V[(tile % 2) * 2 + dh] = evd[dh]
                    wait("gpsimd", "sV", evd[1])
                    if first_sc:
                        wait("gpsimd", "sZ", zr)
                        first_sc = False
                    if sc_prev:
                        wait("gpsimd", "sSC", sc_prev)
                    sc_stage[ring] = cnt["sSC"] + 16
                    sc_prev = inc("gpsimd", "sSC",
                                  lambda e, ee=e_, t_=tile, rg=ring: e.indirect_dma_start(
                                      out=y12,
                                      out_offset=IndirectOffsetOnAxis(
                                          ap=idxi[:, ee * NSL + t_:ee * NSL + t_ + 1], axis=0),
                                      in_=stage[rg], in_offset=None,
                                      compute_op=ALU.add), 16)
                    if tr_todo and tile >= 1:
                        emit_transpose(e_ + 1, tr_todo.pop(0),
                                       [gate_bank_A[0], gate_bank_A[1]], [0, 0])
                while tr_todo:
                    emit_transpose(e_ + 1, tr_todo.pop(0),
                                   [gate_bank_A[0], gate_bank_A[1]], [0, 0])
                if ph < 15:
                    gu_ld = gu_ld_next
                    d_ld = load_d((ph + 1) // 2, (ph + 1) % 2, last_dn_mm)
                ffn_last_mm = last_dn_mm
                if ph == 15:
                    ffn_hid_last = hid_last_V

            if dbg and b == 0:
                wait("sync", "sV", idxi_done[NE - 1])
                inc("sync", "sC", lambda e: e.dma_start(out=dbg_m[:, 0:640], in_=scrf[:, 19456:20096]), 16)
                inc("sync", "sC", lambda e: e.dma_start(out=dbg_m[:, 640:1280], in_=iota_f), 16)
                inc("sync", "sC", lambda e: e.dma_start(out=dbg_idx, in_=idxf), 16)
                inc("sync", "sC", lambda e: e.dma_start(out=dbg_cg, in_=cg_s), 16)
                wait("sync", "sSC", cnt["sSC"])
                inc("sync", "sC", lambda e: e.dma_start(out=dbg_y12, in_=y12), 16)

            wait("sync", "sSC", cnt["sSC"])
            wait("gpsimd", "sO0", cnt["sO0"])
            wait("gpsimd", "sO1", cnt["sO1"])
            rb_n = [0, 0]
            za_n = [0, 0]
            for i in range(NT):
                p = i % 2
                if za_n[p]:
                    wait("sync", "sZA%d" % p, za_n[p])
                rb_n[p] = inc("sync", "sR%d" % p,
                              lambda e, i_=i, pp=p: e.dma_start(
                                  out=stage[pp], in_=y12[i_ * 128:(i_ + 1) * 128, :]), 16)
                wait("gpsimd", "sR%d" % p, rb_n[p])
                za_n[p] = inc("gpsimd", "sZA%d" % p,
                              lambda e, d=out_hbm[B0 + i * 128:B0 + (i + 1) * 128, :],
                              pp=p: e.dma_start(out=d, in_=stage[pp],
                                                accum_op=ALU.add), 16)

            resc_last = resc_j[NT - 1]
            end_prev = {"sT": ffn_last_mm, "sA": cnt["sA"], "sV": cnt["sV"],
                        "sO0": cnt["sO0"], "sO1": cnt["sO1"], "hid": ffn_hid_last,
                        "sSC": cnt["sSC"],
                        "sZA0": cnt["sZA0"], "sZA1": cnt["sZA1"]}

        wait("sync", "sO0", cnt["sO0"])
        wait("sync", "sO1", cnt["sO1"])
        wait("sync", "sZA0", cnt["sZA0"])
        wait("sync", "sZA1", cnt["sZA1"])

        with nc.Block() as block:
            @block.sync
            def _(e):
                for f in ops["sync"]:
                    f(e)

            @block.gpsimd
            def _(e):
                for f in ops["gpsimd"]:
                    f(e)

            @block.tensor
            def _(e):
                for f in ops["tensor"]:
                    f(e)

            @block.scalar
            def _(e):
                for f in ops["scalar"]:
                    f(e)

            @block.vector
            def _(e):
                for f in ops["vector"]:
                    f(e)

    return nc


def make_consts():
    s = np.arange(128)[:, None]
    t = np.arange(128)[None, :]
    tid = (np.arange(16)[None, :] * 128 + np.arange(128)[:, None]).astype(np.float32)
    return {
        "ident": np.eye(128, dtype=ml_dtypes.bfloat16),
        "ident32": np.eye(128, dtype=np.float32),
        "maskd": np.ascontiguousarray(np.where(s <= t, 0.0, NEG).astype(np.float32)),
        "lt": np.ascontiguousarray(np.tril(np.ones((128, 128), np.float32), -1).T
                                   .astype(ml_dtypes.bfloat16)),
        "ones_c": np.ones((128, 1), dtype=ml_dtypes.bfloat16),
        "ones_r": np.ones((1, 128), dtype=np.float32),
        "eps": np.full((128, 1), 1e-5, dtype=np.float32),
        "tidc": np.ascontiguousarray(tid),
        "zsrc": np.zeros((T, D), dtype=np.float32),
    }


def make_in_maps(inputs, consts):
    x = np.asarray(inputs["L0_attn_x"], dtype=np.float32)
    shared = {
        "wq": np.ascontiguousarray(np.asarray(inputs["L0_attn_Wq"], np.float32)),
        "wk": np.ascontiguousarray(np.asarray(inputs["L0_attn_Wk"], np.float32)),
        "wv": np.ascontiguousarray(np.asarray(inputs["L0_attn_Wv"], np.float32)),
        "bqT": np.ascontiguousarray(np.asarray(inputs["L0_attn_bq"], np.float32).reshape(DC, 128).T),
        "bkT": np.ascontiguousarray(np.asarray(inputs["L0_attn_bk"], np.float32).reshape(DC, 128).T),
        "bvb": np.ascontiguousarray(np.broadcast_to(np.asarray(inputs["L0_attn_bv"], np.float32), (128, D))),
        "rw": np.ascontiguousarray(np.asarray(inputs["L0_ffn_router_w"], np.float32)),
        "gw": np.ascontiguousarray(np.asarray(inputs["L0_ffn_gate_w"], np.float32)),
        "uw": np.ascontiguousarray(np.asarray(inputs["L0_ffn_up_w"], np.float32)),
        "dw": np.ascontiguousarray(np.asarray(inputs["L0_ffn_down_w"], np.float32)),
    }
    shared.update(consts)
    in_maps = []
    for i in range(NCORES):
        m = dict(shared)
        m["x"] = np.ascontiguousarray(x[2 * i:2 * i + 2].reshape(TB * T, D))
        in_maps.append(m)
    return in_maps


def kernel(**inputs):
    nc = build_program(dbg=False)
    in_maps = make_in_maps(inputs, make_consts())
    res = run_bass_kernel_spmd(nc, in_maps, core_ids=list(range(NCORES)))
    outs = [res.results[i]["out"].reshape(TB, T, D) for i in range(NCORES)]
    return np.concatenate(outs, axis=0).astype(np.float32)


# version 12
# speedup vs baseline: 1.5464x; 1.0436x over previous
import numpy as np
import ml_dtypes
import concourse.bass as bass
import concourse.mybir as mybir
from concourse.bass_utils import run_bass_kernel_spmd
from concourse.bass_core_rust import IndirectOffsetOnAxis
from contextlib import ExitStack

F32 = mybir.dt.float32
BF16 = mybir.dt.bfloat16
I32 = mybir.dt.int32
AF = mybir.ActivationFunctionType
ALU = mybir.AluOpType
AX = mybir.AxisListType

NCORES = 8
TB = 2
T = 2048
D = 1024
DC = 8
NT = 16
NE = 8
HH = 1024
NHT = 8
NEG = -3.0e38
ENGS = ("sync", "gpsimd", "tensor", "scalar", "vector")


def build_program(dbg=False):
    nc = bass.Bass("TRN2", target_bir_lowering=False)

    x_hbm = nc.dram_tensor("x", [TB * T, D], F32, kind="ExternalInput").ap()
    wq_h = nc.dram_tensor("wq", [D, D], F32, kind="ExternalInput").ap()
    wk_h = nc.dram_tensor("wk", [D, D], F32, kind="ExternalInput").ap()
    wv_h = nc.dram_tensor("wv", [D, D], F32, kind="ExternalInput").ap()
    bqT_h = nc.dram_tensor("bqT", [128, DC], F32, kind="ExternalInput").ap()
    bkT_h = nc.dram_tensor("bkT", [128, DC], F32, kind="ExternalInput").ap()
    bvb_h = nc.dram_tensor("bvb", [128, D], F32, kind="ExternalInput").ap()
    rw_h = nc.dram_tensor("rw", [D, NE], F32, kind="ExternalInput").ap()
    gw_h = nc.dram_tensor("gw", [NE, D, 2 * HH], F32, kind="ExternalInput").ap()
    uw_h = nc.dram_tensor("uw", [NE, D, 2 * HH], F32, kind="ExternalInput").ap()
    dw_h = nc.dram_tensor("dw", [NE, 2 * HH, D], F32, kind="ExternalInput").ap()
    ident_h = nc.dram_tensor("ident", [128, 128], BF16, kind="ExternalInput").ap()
    ident32_h = nc.dram_tensor("ident32", [128, 128], F32, kind="ExternalInput").ap()
    maskd_h = nc.dram_tensor("maskd", [128, 128], F32, kind="ExternalInput").ap()
    lt_h = nc.dram_tensor("lt", [128, 128], BF16, kind="ExternalInput").ap()
    ones_c_h = nc.dram_tensor("ones_c", [128, 1], BF16, kind="ExternalInput").ap()
    ones_r_h = nc.dram_tensor("ones_r", [1, 128], F32, kind="ExternalInput").ap()
    eps_h = nc.dram_tensor("eps", [128, 1], F32, kind="ExternalInput").ap()
    zsrc_h = nc.dram_tensor("zsrc", [T, D], F32, kind="ExternalInput").ap()
    tidc_h = nc.dram_tensor("tidc", [128, 16], F32, kind="ExternalInput").ap()
    out_hbm = nc.dram_tensor("out", [TB * T, D], F32, kind="ExternalOutput").ap()
    xnd = [nc.dram_tensor("xnd%d" % i, [T, D], BF16, kind="Internal").ap()
           for i in range(2)]
    y12 = nc.dram_tensor("y12d", [T, D], F32, kind="Internal").ap()
    if dbg:
        dbg_xnT = nc.dram_tensor("dbg_xnT", [128, DC * T], BF16, kind="ExternalOutput").ap()
        dbg_yat = nc.dram_tensor("dbg_yat", [128, NT * D], F32, kind="ExternalOutput").ap()
        dbg_c = nc.dram_tensor("dbg_c", [128, NT * NE], F32, kind="ExternalOutput").ap()
        dbg_s = nc.dram_tensor("dbg_s", [1, 8], F32, kind="ExternalOutput").ap()
        dbg_idx = nc.dram_tensor("dbg_idx", [128, 40], F32, kind="ExternalOutput").ap()
        dbg_cg = nc.dram_tensor("dbg_cg", [128, 40], F32, kind="ExternalOutput").ap()
        dbg_y12 = nc.dram_tensor("dbg_y12", [T, D], F32, kind="ExternalOutput").ap()
        dbg_m = nc.dram_tensor("dbg_m", [128, 1280], F32, kind="ExternalOutput").ap()

    stack = ExitStack()
    with stack:
        xnT = stack.enter_context(nc.sbuf_tensor("xnT", [128, DC * T], BF16)).ap()
        y_t = stack.enter_context(nc.sbuf_tensor("y", [128, NT * D], F32))
        y = y_t.ap()
        yb = y_t.ap().bitcast(BF16)
        scr_t = stack.enter_context(nc.sbuf_tensor("scr", [128, 53248], BF16))
        scr = scr_t.ap()
        scrf = scr_t.ap().bitcast(F32)
        xnb_t = stack.enter_context(nc.sbuf_tensor("xnb_t", [128, 2048], BF16)).ap()

        identity = stack.enter_context(nc.sbuf_tensor("identity_s", [128, 128], BF16)).ap()
        ident32 = stack.enter_context(nc.sbuf_tensor("ident32_s", [128, 128], F32)).ap()
        maskd = stack.enter_context(nc.sbuf_tensor("maskd_s", [128, 128], F32)).ap()
        ones_c = stack.enter_context(nc.sbuf_tensor("ones_c_s", [128, 1], BF16)).ap()
        ones_r = stack.enter_context(nc.sbuf_tensor("ones_r_s", [1, 128], F32)).ap()
        eps_t = stack.enter_context(nc.sbuf_tensor("eps_t", [128, 1], F32)).ap()
        bqT = stack.enter_context(nc.sbuf_tensor("bqT_s", [128, DC], F32)).ap()
        bkT = stack.enter_context(nc.sbuf_tensor("bkT_s", [128, DC], F32)).ap()
        rw_s = stack.enter_context(nc.sbuf_tensor("rw_s", [128, DC * NE], F32)).ap()
        c_all = stack.enter_context(nc.sbuf_tensor("c_all", [128, NT * NE], F32)).ap()
        lg_all = stack.enter_context(nc.sbuf_tensor("lg_all", [128, NT * NE], F32)).ap()
        l2_all = stack.enter_context(nc.sbuf_tensor("l2_all", [128, NT * NE], F32)).ap()
        rts = stack.enter_context(nc.sbuf_tensor("rts", [128, 48], F32)).ap()
        nacc = rts[0:1, 0:16]
        zacc = rts[0:1, 16:32]
        s1v = rts[0:1, 32:40]
        sca = rts[:, 40:42]

        stats = l2_all[:, 0:32]

        pb = [stack.enter_context(nc.psum_tensor("pb%d" % i, [128, 512], F32)).ap()
              for i in range(8)]
        pbh = [p.bitcast(BF16) for p in pb]

        kT = scr[:, 0:16384]
        v_s = scr[:, 16384:32768]
        w0 = scr[:, 32768:40960]
        w1 = scr[:, 40960:49152]
        qT = scr[:, 49152:53248]
        pt = [scr[:, 40960:41216], scr[:, 41216:41472]]
        sqb = [scr[:, 41472:41984], scr[:, 41984:42496]]
        xst = [scrf[:, 24576:25600], scrf[:, 25600:26624]]
        xr = [scrf[:, 24576:25600], scrf[:, 25600:26624],
              scrf[:, 21504:22528], scrf[:, 22528:23552], scrf[:, 23552:24576]]
        bvb_v = scrf[:, 24576:25600]
        xnb = [xnb_t[:, 0:1024], xnb_t[:, 1024:2048]]
        scri = scr_t.ap().bitcast(I32)
        CAP = 640
        NSL = 5
        gbuf = scr[:, 0:8192]
        ubuf = scr[:, 8192:16384]
        dbuf = scr[:, 16384:24576]
        gbufs = [gbuf, yb[:, 0:8192]]
        ubufs = [ubuf, yb[:, 8192:16384]]
        dbufs = [dbuf, yb[:, 16384:24576]]
        hid = scr[:, 24576:29696]
        oh_s = scrf[:, 12288:13312]
        xeT = scr[:, 29696:34816]
        silu = [scrf[:, 17408:17920], scrf[:, 17920:18432]]
        xe_tm = [scr[:, 36864:37888], scr[:, 37888:38912]]
        idxf = scrf[:, 19456:19496]
        cg_s = scrf[:, 19496:19536]
        idxi = scri[:, 19536:19576]
        acc3 = scrf[:, 19576:19579]
        rhs48 = scrf[:, 19584:19632]
        pos_s = scrf[:, 19632:19648]
        m_s = scrf[:, 19648:19664]
        sc_s = scrf[:, 19664:19680]
        b_s = scrf[:, 19680:19681]
        rt_b = scr[:, 39424:39425]
        tidc = scrf[:, 19696:19712]
        stage = [scrf[:, 20480:21504], scrf[:, 21504:22528]]
        iota_f = scrf[:, 20480:21120]
        iota_i = scri[:, 21504:22144]
        lt_s = scr[:, 44288:44416]
        xn32 = scrf[:, 22528:23552]
        xnT32 = scrf[:, 23552:24576]

        ops = {e: [] for e in ENGS}
        sems = {}
        cnt = {}
        for nm in ("sC", "sX0", "sX1", "sW0", "sW1",
                   "sO0", "sO1", "sY0", "sY1", "sU0", "sU1",
                   "sGU0", "sGU1", "sWD0", "sWD1",
                   "sT", "sA", "sV", "sP",
                   "sR0", "sR1",
                   "sQ0", "sQ1", "sQ2", "sQ3",
                   "sXN0", "sXN1", "sZ", "sZA0", "sZA1", "sSC", "sGA0", "sGA1"):
            sems[nm] = stack.enter_context(nc.semaphore(nm))
            cnt[nm] = 0

        def add(e, fn):
            ops[e].append(fn)

        def wait(e, snm, val):
            if val > 0:
                add(e, lambda eng, s=sems[snm], v=val: eng.wait_ge(s, v))

        def inc(e, snm, fn, k=1):
            cnt[snm] += k
            add(e, lambda eng, f=fn, s=sems[snm], kk=k: f(eng).then_inc(s, kk))
            return cnt[snm]

        def emit(e, fn):
            add(e, fn)

        for dst, csrc in ((identity, ident_h), (ident32, ident32_h), (maskd, maskd_h),
                          (ones_c, ones_c_h), (ones_r, ones_r_h), (eps_t, eps_h),
                          (bqT, bqT_h), (bkT, bkT_h)):
            inc("sync", "sC", lambda e, d=dst, s=csrc: e.dma_start(out=d, in_=s), 16)
        inc("sync", "sC",
            lambda e: e.dma_start(out=rw_s.rearrange("p (c e) -> p c e", c=DC),
                                  in_=rw_h.rearrange("(c p) e -> p c e", p=128)), 16)
        n_const = cnt["sC"]
        wait("scalar", "sC", n_const)
        wait("vector", "sC", n_const)
        wait("tensor", "sC", n_const)

        end_prev = {"sT": 0, "sA": 0, "sV": 0, "sO0": 0, "sO1": 0, "hid": 0,
                    "sY0": 0, "sY1": 0, "sU0": 0, "sU1": 0,
                    "sSC": 0, "sZA0": 0, "sZA1": 0}
        prev_gu_mm = [0, 0]
        prev_d_mm = [0, 0]
        pending_final = None
        fin_za_gate = (0, 0)
        resc_last = 0

        for b in range(TB):
            B0 = b * T

            xcast = [0] * NT
            xn_st = [0] * NT
            xnT_cp = [0] * NT
            tr_done = [0] * NT
            f32tr_done = [0] * NT
            xn32cp = [0] * NT
            rt_mm = [0] * NT
            rt_ev = [0] * NT
            for i in range(NT):
                p = i % 2
                o = p * 16
                MEAN, VAR, STD, RSTD, NB = o + 12, o + 13, o + 14, o + 15, o + 0
                if i >= 2:
                    wait("sync", "sA", xcast[i - 2])
                elif b > 0:
                    wait("sync", "sV", resc_last)
                sxp = "sX%d" % p
                nx = inc("sync", sxp,
                         lambda e, d=xst[p], s=x_hbm[B0 + i * 128:B0 + (i + 1) * 128, :]:
                         e.dma_start(out=d, in_=s), 16)
                wait("vector", sxp, nx)
                emit("vector", lambda e, d=stats[:, o:o + 6], s=xst[p][:, 0:512]:
                     e.bn_stats(d, s))
                wbn = inc("vector", "sV",
                          lambda e, d=stats[:, o + 6:o + 12], s=xst[p][:, 512:1024]:
                          e.bn_stats(d, s))
                wait("vector", "sV", wbn)
                wag = inc("vector", "sV",
                          lambda e, d=stats[:, MEAN:MEAN + 2], s=stats[:, o:o + 12]:
                          e.bn_aggr(d, s))
                wait("scalar", "sV", wag)
                wstd = inc("scalar", "sA",
                           lambda e, d=stats[:, STD:STD + 1], s=stats[:, VAR:VAR + 1]:
                           e.activation(out=d, in_=s, func=AF.Sqrt, bias=eps_t))
                wait("vector", "sA", wstd)
                wrs = inc("vector", "sV",
                          lambda e, d=stats[:, RSTD:RSTD + 1], s=stats[:, STD:STD + 1]:
                          e.reciprocal(out=d, in_=s))
                wait("vector", "sV", wrs)
                wnb = inc("vector", "sV",
                          lambda e, d=stats[:, NB:NB + 1], s=stats[:, MEAN:MEAN + 1],
                          r=stats[:, RSTD:RSTD + 1]:
                          e.scalar_tensor_tensor(out=d, in0=s, scalar=-1.0, in1=r,
                                                 op0=ALU.mult, op1=ALU.mult))
                wait("scalar", "sV", wnb)
                if i >= 1:
                    wait("scalar", "sT", f32tr_done[i - 1])
                a32 = inc("scalar", "sA",
                          lambda e, d=xn32, s=xst[p], sc=stats[:, RSTD:RSTD + 1],
                          bi=stats[:, NB:NB + 1]:
                          e.activation(out=d, in_=s, func=AF.Identity, scale=sc, bias=bi))
                wait("scalar", "sA", a32)
                if i >= 2:
                    wait("scalar", "sT", tr_done[i - 2])
                    wait("scalar", "sXN%d" % p, xn_st[i - 2])
                xcast[i] = inc("scalar", "sA",
                               lambda e, d=xnb[p], s=xn32:
                               e.activation(out=d, in_=s, func=AF.Copy))
                wait("sync", "sA", xcast[i])
                xn_st[i] = inc("sync", "sXN%d" % p,
                               lambda e, d=xnd[b % 2][i * 128:(i + 1) * 128, :],
                               s=xnb[p]: e.dma_start(out=d, in_=s), 16)
                wait("tensor", "sA", xcast[i])
                if i < 2:
                    if b > 0:
                        wait("tensor", "sA", end_prev["sA"])
                else:
                    wait("tensor", "sV", xnT_cp[i - 2])
                for dc in range(DC):
                    fn = (lambda e, d=pbh[p][:, dc * 128:(dc + 1) * 128],
                          s=xnb[p][:, dc * 128:(dc + 1) * 128]:
                          e.transpose(d, s, identity))
                    if dc == DC - 1:
                        tr_done[i] = inc("tensor", "sT", fn)
                    else:
                        emit("tensor", fn)
                if i > 0:
                    wait("tensor", "sV", xn32cp[i - 1])
                elif b > 0:
                    wait("tensor", "sV", end_prev["hid"])
                for dc in range(DC):
                    fn = (lambda e,
                          d=pb[2 + dc // 4][:, (dc % 4) * 128:((dc % 4) + 1) * 128],
                          s=xn32[:, dc * 128:(dc + 1) * 128]:
                          e.transpose(d, s, ident32))
                    if dc == DC - 1:
                        f32tr_done[i] = inc("tensor", "sT", fn)
                    else:
                        emit("tensor", fn)
                wait("vector", "sT", tr_done[i])
                xnT_cp[i] = inc("vector", "sV",
                                lambda e,
                                d=xnT.rearrange("p (c t) -> p c t", c=DC)[:, :, i * 128:(i + 1) * 128],
                                s=pbh[p].rearrange("p (c t) -> p c t", c=DC):
                                e.tensor_copy(out=d, in_=s))
                wait("vector", "sT", f32tr_done[i])
                if i > 0:
                    wait("vector", "sT", rt_mm[i - 1])
                emit("vector",
                     lambda e: e.tensor_copy(out=xnT32[:, 0:512], in_=pb[2][:, 0:512]))
                xn32cp[i] = inc("vector", "sV",
                                lambda e: e.tensor_copy(out=xnT32[:, 512:1024],
                                                        in_=pb[3][:, 0:512]))
                lb = 4 + (i % 2)
                wait("tensor", "sV", xn32cp[i])
                if i > 1:
                    wait("tensor", "sA", rt_ev[i - 2])
                elif b > 0:
                    wait("tensor", "sV", end_prev["sV"])
                for kc in range(DC):
                    fn = (lambda e, o_=pb[lb][:, 0:8],
                          l=xnT32[:, kc * 128:(kc + 1) * 128],
                          r=rw_s[:, kc * 8:(kc + 1) * 8],
                          st=(kc == 0), sp=(kc == DC - 1):
                          e.matmul(o_, l, r, start=st, stop=sp))
                    if kc == DC - 1:
                        rt_mm[i] = inc("tensor", "sT", fn)
                    else:
                        emit("tensor", fn)
                if i > 0:
                    wait("scalar", "sT", rt_mm[i - 1])
                    rt_ev[i - 1] = inc("scalar", "sA",
                                       lambda e, d=lg_all[:, (i - 1) * 8:i * 8],
                                       s=pb[4 + ((i - 1) % 2)][:, 0:8]:
                                       e.activation(out=d, in_=s, func=AF.Copy))
                if i == NT - 1:
                    wait("scalar", "sT", rt_mm[i])
                    rt_ev[i] = inc("scalar", "sA",
                                   lambda e, d=lg_all[:, i * 8:(i + 1) * 8],
                                   s=pb[lb][:, 0:8]:
                                   e.activation(out=d, in_=s, func=AF.Copy))

            lg3 = lg_all.rearrange("p (t e) -> p t e", t=NT)
            l23 = l2_all.rearrange("p (t e) -> p t e", t=NT)
            c3 = c_all.rearrange("p (t e) -> p t e", t=NT)
            def rbc(col):
                return rts[:, col:col + 16].rearrange("p (t o) -> p t o", o=1) \
                          .broadcast_to([128, NT, NE])
            wait("vector", "sA", rt_ev[NT - 1])
            wv_ = inc("vector", "sV",
                      lambda e: e.tensor_reduce(rts[:, 0:16], lg3, axis=AX.X, op=ALU.max))
            wait("vector", "sV", wv_)
            wv_ = inc("vector", "sV",
                      lambda e: e.tensor_tensor(out=c3, in0=lg3, in1=rbc(0), op=ALU.is_equal))
            wait("vector", "sV", wv_)
            wv_ = inc("vector", "sV",
                      lambda e: e.scalar_tensor_tensor(out=l23, in0=c3, scalar=NEG,
                                                       in1=lg3, op0=ALU.mult, op1=ALU.add))
            wait("vector", "sV", wv_)
            wv_ = inc("vector", "sV",
                      lambda e: e.tensor_reduce(rts[:, 16:32], l23, axis=AX.X, op=ALU.max))
            wait("vector", "sV", wv_)
            wv_ = inc("vector", "sV",
                      lambda e: e.tensor_sub(rts[:, 32:48], rts[:, 16:32], rts[:, 0:16]))
            wait("scalar", "sV", wv_)
            wa_ = inc("scalar", "sA",
                      lambda e: e.activation(out=rts[:, 0:16], in_=rts[:, 32:48],
                                             func=AF.Sigmoid, scale=-1.0))
            wait("scalar", "sA", wa_)
            wa_ = inc("scalar", "sA",
                      lambda e: e.activation(out=rts[:, 32:48], in_=rts[:, 32:48],
                                             func=AF.Sigmoid))
            wait("vector", "sA", wa_)
            wv_ = inc("vector", "sV",
                      lambda e: e.tensor_tensor(out=c3, in0=c3, in1=rbc(0), op=ALU.mult))
            wait("vector", "sV", wv_)
            wv_ = inc("vector", "sV",
                      lambda e: e.tensor_tensor(out=l23, in0=l23, in1=rbc(16), op=ALU.is_equal))
            wait("vector", "sV", wv_)
            wv_ = inc("vector", "sV",
                      lambda e: e.tensor_tensor(out=l23, in0=l23, in1=rbc(32), op=ALU.mult))
            wait("vector", "sV", wv_)
            c_rdy = inc("vector", "sV",
                        lambda e: e.tensor_add(c_all, c_all, l2_all))

            if dbg and b == 0:
                wait("sync", "sV", xnT_cp[NT - 1])
                inc("sync", "sC", lambda e: e.dma_start(out=dbg_xnT, in_=xnT), 16)
                wait("sync", "sV", c_rdy)
                inc("sync", "sC", lambda e: e.dma_start(out=dbg_c, in_=c_all), 16)

            if pending_final is not None:
                pending_final()
                pending_final = None
            fin_za_gate = (cnt["sZA0"], cnt["sZA1"])

            wait("sync", "sA", xcast[NT - 1])
            bvb_ld = inc("sync", "sC",
                         lambda e: e.dma_start(out=bvb_v, in_=bvb_h), 16)

            if b > 0:
                wait("gpsimd", "sT", end_prev["sT"])
                wait("gpsimd", "sSC", end_prev["sSC"])
            wk_ld = inc("gpsimd", "sW0",
                        lambda e: e.dma_start(
                            out=w0.rearrange("p (c d) -> p c d", c=DC),
                            in_=wk_h.rearrange("(c p) d -> p c d", p=128)), 16)
            wait("gpsimd", "sT", rt_mm[NT - 1])
            wait("gpsimd", "sA", xcast[NT - 1])
            if b > 0:
                wait("gpsimd", "sZA0", cnt["sZA0"])
                wait("gpsimd", "sZA1", cnt["sZA1"])
            wv_ld = inc("gpsimd", "sW1",
                        lambda e: e.dma_start(
                            out=w1.rearrange("p (c d) -> p c d", c=DC),
                            in_=wv_h.rearrange("(c p) d -> p c d", p=128)), 16)

            kT_ev = {}
            g_ev = [0, 0, 0, 0]
            gi = 0
            last_kT_mm = 0
            first_kt = True
            for dt in range(DC):
                for tc4 in range(4):
                    g = gi % 4
                    gi += 1
                    wait("tensor", "sW0", wk_ld)
                    wait("tensor", "sV", xnT_cp[tc4 * 4 + 3])
                    if first_kt:
                        wait("tensor", "sV", xn32cp[NT - 1])
                        first_kt = False
                    if g_ev[g]:
                        wait("tensor", "sA", g_ev[g])
                    for kc in range(DC):
                        fn = (lambda e, o_=pb[g][:, 0:512],
                              l=w0[:, kc * 1024 + dt * 128: kc * 1024 + (dt + 1) * 128],
                              r=xnT[:, kc * 2048 + tc4 * 512: kc * 2048 + (tc4 + 1) * 512],
                              st=(kc == 0), sp=(kc == DC - 1):
                              e.matmul(o_, l, r, start=st, stop=sp))
                        if kc == DC - 1:
                            last_kT_mm = inc("tensor", "sT", fn)
                        else:
                            emit("tensor", fn)
                    wait("scalar", "sT", last_kT_mm)
                    g_ev[g] = inc("scalar", "sA",
                                  lambda e,
                                  d=kT[:, dt * 2048 + tc4 * 512: dt * 2048 + (tc4 + 1) * 512],
                                  s=pb[g][:, 0:512], bi=bkT[:, dt:dt + 1]:
                                  e.activation(out=d, in_=s, func=AF.Identity, bias=bi))
                    kT_ev[(dt, tc4)] = g_ev[g]
            kT_done_A = cnt["sA"]

            wait("gpsimd", "sT", last_kT_mm)
            wq_ld = inc("gpsimd", "sW0",
                        lambda e: e.dma_start(
                            out=w0.rearrange("p (c d) -> p c d", c=DC),
                            in_=wq_h.rearrange("(c p) d -> p c d", p=128)), 16)

            g_evv = [0, 0, 0, 0]
            gi = 0
            last_v_mm = 0
            first_v = True
            for st_ in range(NT):
                for dh in range(2):
                    g = gi % 4
                    gi += 1
                    wait("tensor", "sW1", wv_ld)
                    wait("tensor", "sV", xnT_cp[st_])
                    if first_v:
                        wait("tensor", "sA", kT_done_A)
                    if g_evv[g]:
                        wait("tensor", "sV", g_evv[g])
                    for kc in range(DC):
                        fn = (lambda e, o_=pb[g][:, 0:512],
                              l=xnT[:, kc * 2048 + st_ * 128: kc * 2048 + (st_ + 1) * 128],
                              r=w1[:, kc * 1024 + dh * 512: kc * 1024 + (dh + 1) * 512],
                              st=(kc == 0), sp=(kc == DC - 1):
                              e.matmul(o_, l, r, start=st, stop=sp))
                        if kc == DC - 1:
                            last_v_mm = inc("tensor", "sT", fn)
                        else:
                            emit("tensor", fn)
                    wait("vector", "sT", last_v_mm)
                    if first_v:
                        wait("vector", "sC", bvb_ld)
                        first_v = False
                    g_evv[g] = inc("vector", "sV",
                                   lambda e,
                                   d=v_s[:, st_ * 1024 + dh * 512: st_ * 1024 + (dh + 1) * 512],
                                   s=pb[g][:, 0:512], bi=bvb_v[:, dh * 512:(dh + 1) * 512]:
                                   e.tensor_tensor(out=d, in0=s, in1=bi, op=ALU.add))
            v_done_V = cnt["sV"]

            sq_mmT = 0
            kn_ev = 0
            wait("scalar", "sT", last_v_mm)
            if b > 0:
                wait("scalar", "sSC", end_prev["sSC"])
                wait("scalar", "sZA0", fin_za_gate[0])
                wait("scalar", "sZA1", fin_za_gate[1])
            for j in range(4):
                for dt in range(DC):
                    jj = dt % 2
                    wait("scalar", "sA", kT_ev[(dt, j)])
                    if sq_mmT:
                        wait("scalar", "sT", sq_mmT)
                    a_k = inc("scalar", "sA",
                              lambda e, d=sqb[jj],
                              s=kT[:, dt * 2048 + j * 512: dt * 2048 + (j + 1) * 512]:
                              e.square(out=d, in_=s))
                    wait("tensor", "sA", a_k)
                    if j == 0 and dt == 0:
                        wait("tensor", "sA", rt_ev[NT - 1])
                    if dt == 0 and kn_ev:
                        wait("tensor", "sV", kn_ev)
                    sq_mmT = inc("tensor", "sT",
                                 lambda e, r=sqb[jj], st=(dt == 0), sp=(dt == DC - 1):
                                 e.matmul(pb[4][0:1, 0:512], ones_c, r, start=st, stop=sp))
                wait("vector", "sT", sq_mmT)
                kn_ev = inc("vector", "sV",
                            lambda e, d=nacc[0:1, 8 + j:9 + j]:
                            e.reduce_max(d, pb[4][0:1, 0:512], axis=AX.X))

            def qT_proj(cc, qb_wA, qb_wV, scalar_pre_V=0):
                evA = 0
                lastT = 0
                first = True
                for dt in range(DC):
                    g = 6 + dt % 2
                    wait("tensor", "sW0", wq_ld)
                    if qb_wA[dt % 2]:
                        wait("tensor", "sA", qb_wA[dt % 2])
                    if qb_wV[dt % 2]:
                        wait("tensor", "sV", qb_wV[dt % 2])
                        qb_wV[dt % 2] = 0
                    for kc in range(DC):
                        fn = (lambda e, o_=pb[g][:, 0:512],
                              l=w0[:, kc * 1024 + dt * 128: kc * 1024 + (dt + 1) * 128],
                              r=xnT[:, kc * 2048 + cc * 512: kc * 2048 + (cc + 1) * 512],
                              st=(kc == 0), sp=(kc == DC - 1):
                              e.matmul(o_, l, r, start=st, stop=sp))
                        if kc == DC - 1:
                            lastT = inc("tensor", "sT", fn)
                        else:
                            emit("tensor", fn)
                    wait("scalar", "sT", lastT)
                    if first and scalar_pre_V:
                        wait("scalar", "sV", scalar_pre_V)
                        first = False
                    evA = inc("scalar", "sA",
                              lambda e, d=qT[:, dt * 512:(dt + 1) * 512],
                              s=pb[g][:, 0:512], bi=bqT[:, dt:dt + 1]:
                              e.activation(out=d, in_=s, func=AF.Identity, bias=bi))
                    qb_wA[dt % 2] = evA
                return evA, lastT

            qb_wA = [0, 0]
            qn_ev = 0
            for cc in range(4):
                q_evA, _ = qT_proj(cc, qb_wA, [0, 0],
                                   scalar_pre_V=(v_done_V if cc == 0 else 0))
                for dt in range(DC):
                    jj = dt % 2
                    wait("scalar", "sA", q_evA)
                    if sq_mmT:
                        wait("scalar", "sT", sq_mmT)
                    a_q = inc("scalar", "sA",
                              lambda e, d=sqb[jj], s=qT[:, dt * 512:(dt + 1) * 512]:
                              e.square(out=d, in_=s))
                    wait("tensor", "sA", a_q)
                    if cc == 0 and dt == 0:
                        wait("tensor", "sV", kn_ev)
                    if dt == 0 and qn_ev:
                        wait("tensor", "sV", qn_ev)
                    sq_mmT = inc("tensor", "sT",
                                 lambda e, r=sqb[jj], st=(dt == 0), sp=(dt == DC - 1):
                                 e.matmul(pb[5][0:1, 0:512], ones_c, r, start=st, stop=sp))
                wait("vector", "sT", sq_mmT)
                qn_ev = inc("vector", "sV",
                            lambda e, d=nacc[0:1, cc:cc + 1]:
                            e.reduce_max(d, pb[5][0:1, 0:512], axis=AX.X))

            wait("vector", "sV", qn_ev)
            wv_ = inc("vector", "sV",
                      lambda e: e.reduce_max(s1v[0:1, 0:1], nacc[0:1, 0:4], axis=AX.X))
            wait("vector", "sV", wv_)
            wv_ = inc("vector", "sV",
                      lambda e: e.reduce_max(s1v[0:1, 1:2], nacc[0:1, 8:12], axis=AX.X))
            wait("scalar", "sV", wv_)
            wa_ = inc("scalar", "sA",
                      lambda e: e.activation(out=s1v[0:1, 2:3], in_=s1v[0:1, 0:1], func=AF.Sqrt))
            wait("scalar", "sA", wa_)
            wa_ = inc("scalar", "sA",
                      lambda e: e.activation(out=s1v[0:1, 3:4], in_=s1v[0:1, 1:2], func=AF.Sqrt))
            wait("vector", "sA", wa_)
            wv_ = inc("vector", "sV",
                      lambda e: e.scalar_tensor_tensor(out=s1v[0:1, 4:5], in0=s1v[0:1, 2:3],
                                                       scalar=-1.0 / 32.0, in1=s1v[0:1, 3:4],
                                                       op0=ALU.mult, op1=ALU.mult))
            wait("tensor", "sV", wv_)
            bc_mm = inc("tensor", "sT",
                        lambda e: e.matmul(pb[5][:, 0:1], ones_r, s1v[0:1, 4:5],
                                           start=True, stop=True))
            wait("vector", "sT", bc_mm)
            negC_rdy = inc("vector", "sV",
                           lambda e: e.tensor_copy(out=sca[:, 0:1], in_=pb[5][:, 0:1]))

            out_ev = [0, 0, 0, 0]
            sc_exA = [0, 0]
            pt_rdT = [0, 0]
            zr_evV = 0
            last_pv_mm = 0
            for cc in range(4):
                qb_wV2 = [zr_evV, zr_evV]
                q_evA, _ = qT_proj(cc, qb_wA, qb_wV2)
                for c2 in range(2):
                    c = 2 * cc + c2
                    toff = c2 * 256
                    nk = 2 * c + 2
                    for k in range(nk):
                        sb_ = 4 + (k % 2)
                        wait("tensor", "sA", q_evA)
                        if sc_exA[k % 2]:
                            wait("tensor", "sA", sc_exA[k % 2])
                        elif k % 2 == 0 and cc == 0 and c2 == 0:
                            wait("tensor", "sV", kn_ev)
                        elif k % 2 == 1 and cc == 0 and c2 == 0:
                            wait("tensor", "sV", negC_rdy)
                        if k == 0:
                            for gg in range(4):
                                if out_ev[gg]:
                                    wait("tensor", "sV", out_ev[gg])
                            if b > 0 and cc == 0 and c2 == 0:
                                for s_ in ("sO0", "sO1", "sY0", "sY1",
                                           "sU0", "sU1"):
                                    wait("tensor", s_, end_prev[s_])
                        lastS = 0
                        for dt in range(DC):
                            fn = (lambda e, o_=pb[sb_][:, 0:256],
                                  l=kT[:, dt * 2048 + k * 128: dt * 2048 + (k + 1) * 128],
                                  r=qT[:, dt * 512 + toff: dt * 512 + toff + 256],
                                  st=(dt == 0), sp=(dt == DC - 1):
                                  e.matmul(o_, l, r, start=st, stop=sp))
                            if dt == DC - 1:
                                lastS = inc("tensor", "sT", fn)
                            else:
                                emit("tensor", fn)
                        mk = 0
                        if k == 2 * c:
                            wait("vector", "sT", lastS)
                            mk = inc("vector", "sV",
                                     lambda e, o_=pb[sb_][:, 0:128]:
                                     e.tensor_tensor(out=o_, in0=o_, in1=maskd, op=ALU.add))
                        elif k == 2 * c + 1:
                            wait("vector", "sT", lastS)
                            emit("vector",
                                 lambda e, o_=pb[sb_][:, 0:128]: e.memset(o_, NEG))
                            mk = inc("vector", "sV",
                                     lambda e, o_=pb[sb_][:, 128:256]:
                                     e.tensor_tensor(out=o_, in0=o_, in1=maskd, op=ALU.add))
                        if mk:
                            wait("scalar", "sV", mk)
                        else:
                            wait("scalar", "sT", lastS)
                        if pt_rdT[k % 2]:
                            wait("scalar", "sT", pt_rdT[k % 2])
                        ex = inc("scalar", "sA",
                                 lambda e, d=pt[k % 2], s=pb[sb_][:, 0:256]:
                                 e.activation(out=d, in_=s, func=AF.Exp,
                                              scale=1.0 / 32.0, bias=sca[:, 0:1]))
                        sc_exA[k % 2] = ex
                        wait("tensor", "sA", ex)
                        if k == 0 and zr_evV:
                            wait("tensor", "sV", zr_evV)
                        emit("tensor",
                             lambda e, o_=pb[6][0:1, toff:toff + 256], r=pt[k % 2],
                             st=(k == 0), sp=(k == nk - 1):
                             e.matmul(o_, ones_c, r, start=st, stop=sp))
                        if cc == 0 and c2 == 0 and k == 0:
                            wait("tensor", "sV", v_done_V)
                        for sub in range(2):
                            gsub = 2 * c + sub
                            if k > gsub:
                                continue
                            for dh in range(2):
                                gg = sub * 2 + dh
                                fn = (lambda e, o_=pb[gg][:, 0:512],
                                      l=pt[k % 2][:, sub * 128:(sub + 1) * 128],
                                      r=v_s[:, k * 1024 + dh * 512: k * 1024 + (dh + 1) * 512],
                                      st=(k == 0), sp=(k == 2 * c + sub):
                                      e.matmul(o_, l, r, start=st, stop=sp))
                                if sub == 1 and dh == 1:
                                    pt_rdT[k % 2] = inc("tensor", "sT", fn)
                                else:
                                    emit("tensor", fn)
                        last_pv_mm = pt_rdT[k % 2]
                    for sub in range(2):
                        for dh in range(2):
                            gg = sub * 2 + dh
                            wait("vector", "sT", last_pv_mm)
                            out_ev[gg] = inc("vector", "sV",
                                             lambda e,
                                             d=y[:, (2 * c + sub) * 1024 + dh * 512:
                                                 (2 * c + sub) * 1024 + (dh + 1) * 512],
                                             s=pb[gg][:, 0:512]:
                                             e.tensor_copy(out=d, in_=s))
                    wait("vector", "sT", last_pv_mm)
                    zr_evV = inc("vector", "sV",
                                 lambda e, d=zacc[0:1, c:c + 1],
                                 s=pb[6][0:1, toff:toff + 256]:
                                 e.reduce_sum(d, s, axis=AX.X))

            wait("vector", "sV", zr_evV)
            wv_ = inc("vector", "sV",
                      lambda e: e.reduce_sum(s1v[0:1, 5:6], zacc[0:1, 0:8], axis=AX.X))
            wait("vector", "sV", wv_)
            wv_ = inc("vector", "sV",
                      lambda e: e.reciprocal(out=s1v[0:1, 6:7], in_=s1v[0:1, 5:6]))
            wait("tensor", "sV", wv_)
            wait("tensor", "sA", sc_exA[0])
            wait("tensor", "sA", sc_exA[1])
            bc2 = inc("tensor", "sT",
                      lambda e: e.matmul(pb[4][:, 0:1], ones_r, s1v[0:1, 6:7],
                                         start=True, stop=True))
            wait("vector", "sT", bc2)
            zinv_rdy = inc("vector", "sV",
                           lambda e: e.tensor_copy(out=sca[:, 1:2], in_=pb[4][:, 0:1]))

            def load_gu(e_, h2_, wait_T_gu, pi_):
                if wait_T_gu:
                    wait("gpsimd", "sT", wait_T_gu)
                inc("gpsimd", "sGU%d" % pi_,
                    lambda e, ee=e_, hh=h2_, d=gbufs[pi_]: e.dma_start(
                        out=d.rearrange("p (c h) -> p c h", c=DC),
                        in_=gw_h[ee].rearrange("(c p) h -> p c h", p=128)[:, :, hh * HH:(hh + 1) * HH]), 16)
                return inc("gpsimd", "sGU%d" % pi_,
                           lambda e, ee=e_, hh=h2_, d=ubufs[pi_]: e.dma_start(
                               out=d.rearrange("p (c h) -> p c h", c=DC),
                               in_=uw_h[ee].rearrange("(c p) h -> p c h", p=128)[:, :, hh * HH:(hh + 1) * HH]), 16)

            def load_d(e_, h2_, wait_T_d, pi_):
                if wait_T_d:
                    wait("gpsimd", "sT", wait_T_d)
                return inc("gpsimd", "sWD%d" % pi_,
                           lambda e, ee=e_, hh=h2_, d=dbufs[pi_]: e.dma_start(
                               out=d.rearrange("p (c d) -> p c d", c=DC),
                               in_=dw_h[ee][hh * HH:(hh + 1) * HH, :].rearrange("(c p) d -> p c d", p=128)), 16)

            wait("gpsimd", "sT", last_pv_mm)
            ph0_half = {}
            for snm, th, lo in (("sQ0", gw_h, 0), ("sQ2", uw_h, 0),
                                ("sQ1", gw_h, 4), ("sQ3", uw_h, 4)):
                dst = gbuf if th is gw_h else ubuf
                ph0_half[snm] = inc("gpsimd", snm,
                                    lambda e, d=dst, t=th, l_=lo: e.dma_start(
                                        out=d.rearrange("p (c h) -> p c h", c=DC)[:, l_:l_ + 4, :],
                                        in_=t[0].rearrange("(c p) h -> p c h", p=128)[:, l_:l_ + 4, 0:HH]), 16)
            d_ld = inc("gpsimd", "sWD0",
                       lambda e: e.dma_start(
                           out=dbuf.rearrange("p (c d) -> p c d", c=DC),
                           in_=dw_h[0][0:HH, :].rearrange("(c p) d -> p c d", p=128)), 16)

            resc_j = [0] * NT
            resc_nx = [0] * NT
            fin_st = [0, 0]

            def emit_resc_dma(j):
                p = j % 2
                srp = "sR%d" % p
                if j < 2:
                    wait("sync", "sT", last_pv_mm)
                else:
                    wait("sync", "sV", resc_j[j - 2])
                resc_nx[j] = inc("sync", srp,
                                 lambda e, d=xst[p], s=x_hbm[B0 + j * 128:B0 + (j + 1) * 128, :]:
                                 e.dma_start(out=d, in_=s), 16)

            def emit_resc_stt(j):
                p = j % 2
                wait("vector", "sR%d" % p, resc_nx[j])
                resc_j[j] = inc("vector", "sV",
                                lambda e, d=y[:, j * 1024:(j + 1) * 1024], x_=xst[p]:
                                e.scalar_tensor_tensor(out=d, in0=d, scalar=sca[:, 1:2],
                                                       in1=x_, op0=ALU.mult, op1=ALU.add))
                if j + 2 < NT:
                    emit_resc_dma(j + 2)
                grp = "sO" if j < 8 else ("sY" if j < 12 else "sU")
                wait("sync", "sV", resc_j[j])
                fin_st[j % 2] = inc("sync", "%s%d" % (grp, j % 2),
                                    lambda e, d=out_hbm[B0 + j * 128:B0 + (j + 1) * 128, :],
                                    s=y[:, j * 1024:(j + 1) * 1024]:
                                    e.dma_start(out=d, in_=s), 16)

            for j in range(2):
                emit_resc_dma(j)

            if dbg and b == 0:
                wait("sync", "sV", zinv_rdy)
                dbg_sX = inc("sync", "sC", lambda e: e.dma_start(out=dbg_s, in_=s1v), 16)

            if dbg and b == 0:
                wait("gpsimd", "sC", dbg_sX)
                wait("tensor", "sC", dbg_sX)


            zr = inc("sync", "sZ", lambda e: e.dma_start(out=y12, in_=zsrc_h), 16)
            wait("sync", "sT", last_pv_mm)
            lt_ld = inc("sync", "sC", lambda e: e.dma_start(out=lt_s, in_=lt_h), 16)
            tid_ld = inc("sync", "sC", lambda e: e.dma_start(out=tidc, in_=tidc_h), 16)
            wait("tensor", "sC", tid_ld)
            wait("vector", "sC", tid_ld)
            wait("gpsimd", "sT", last_pv_mm)
            g_io = inc("gpsimd", "sP",
                       lambda e: e.iota(iota_i, pattern=[[1, CAP]], base=0,
                                        channel_multiplier=0))
            wait("vector", "sP", g_io)
            wait("vector", "sT", last_pv_mm)
            wv_ = inc("vector", "sV", lambda e: e.tensor_copy(out=iota_f, in_=iota_i))
            wait("vector", "sV", wv_)
            wait("tensor", "sA", qb_wA[1] if qb_wA[1] else qb_wA[0])
            idxi_done = [0] * NE
            xeT_ev = [0] * NE
            bcop = 0
            evac_last = 0
            mm_prev = 0
            for e_ in range(NE):
                wv_ = inc("vector", "sV",
                          lambda e, ee=e_: e.tensor_single_scalar(
                              out=m_s, in_=c_all.rearrange("p (t e) -> p t e", e=NE)[:, :, ee],
                              scalar=0.0, op=ALU.is_gt))
                wait("vector", "sV", wv_)
                wv_ = inc("vector", "sV",
                          lambda e: e.tensor_tensor_scan(out=sc_s, data0=m_s, data1=m_s,
                                                         initial=0.0, op0=ALU.add,
                                                         op1=ALU.bypass))
                wait("vector", "sV", wv_)
                wv_ = inc("vector", "sV", lambda e: e.tensor_copy(out=rt_b, in_=sc_s[:, 15:16]))
                wait("tensor", "sV", wv_)
                if bcop:
                    wait("tensor", "sV", evac_last)
                bmm = inc("tensor", "sT",
                          lambda e: e.matmul(pb[7][:, 0:1], lt_s, rt_b, start=True, stop=True))
                wait("vector", "sT", bmm)
                bcop = inc("vector", "sV", lambda e: e.tensor_copy(out=b_s, in_=pb[7][:, 0:1]))
                wait("vector", "sV", bcop)
                wv_ = inc("vector", "sV",
                          lambda e: e.tensor_scalar(out=pos_s, in0=sc_s, scalar1=b_s,
                                                    scalar2=-10000.0, op0=ALU.add, op1=ALU.add))
                wait("vector", "sV", wv_)
                wv_ = inc("vector", "sV", lambda e: e.tensor_mul(pos_s, pos_s, m_s))
                wait("vector", "sV", wv_)
                wv_ = inc("vector", "sV", lambda e: e.tensor_scalar_add(pos_s, pos_s, 9999.0))
                wait("vector", "sV", wv_)
                r3 = rhs48.rearrange("p (t c) -> p t c", c=3)
                wv_ = inc("vector", "sV", lambda e: e.memset(r3[:, :, 1:2], 1.0))
                wait("vector", "sV", wv_)
                wv_ = inc("vector", "sV",
                          lambda e: e.tensor_copy(out=r3[:, :, 0:1],
                                                  in_=tidc.rearrange("p (t o) -> p t o", o=1)))
                wait("vector", "sV", wv_)
                wv_ = inc("vector", "sV",
                          lambda e, ee=e_: e.tensor_copy(
                              out=r3[:, :, 2:3],
                              in_=c_all.rearrange("p (t e) -> p t e", e=NE)[:, :, ee:ee + 1]))
                rhs_rdy = wv_
                for qc in range(NSL):
                    mm_half = 0
                    for half in range(2):
                        ohv = 0
                        if half == 1:
                            wait("vector", "sT", mm_half)
                        elif mm_prev:
                            wait("vector", "sT", mm_prev)
                        for tl8 in range(8):
                            tl = half * 8 + tl8
                            ohv = inc("vector", "sV",
                                      lambda e, t_=tl, t8=tl8, q_=qc: e.tensor_scalar(
                                          out=oh_s[:, t8 * 128:(t8 + 1) * 128],
                                          in0=iota_f[:, q_ * 128:(q_ + 1) * 128],
                                          scalar1=pos_s[:, t_:t_ + 1], scalar2=None,
                                          op0=ALU.is_equal))
                        wait("tensor", "sV", ohv)
                        if half == 0 and (qc > 0 or e_ > 0):
                            wait("tensor", "sV", evac_last)
                        for tl8 in range(8):
                            tl = half * 8 + tl8
                            fn = (lambda e, t_=tl, t8=tl8: e.matmul(
                                pb[7][:, 4:7], oh_s[:, t8 * 128:(t8 + 1) * 128],
                                rhs48[:, t_ * 3:t_ * 3 + 3],
                                start=(t_ == 0), stop=(t_ == 15)))
                            if tl8 == 7:
                                mm_half = inc("tensor", "sT", fn)
                            else:
                                emit("tensor", fn)
                    mm_prev = mm_half
                    wait("vector", "sT", mm_half)
                    wv_ = inc("vector", "sV", lambda e: e.tensor_copy(out=acc3, in_=pb[7][:, 4:7]))
                    wait("vector", "sV", wv_)
                    col = e_ * NSL + qc
                    wv_ = inc("vector", "sV",
                              lambda e, c_=col: e.scalar_tensor_tensor(
                                  out=idxf[:, c_:c_ + 1], in0=acc3[:, 1:2], scalar=-2047.0,
                                  in1=acc3[:, 0:1], op0=ALU.mult, op1=ALU.add))
                    wait("vector", "sV", wv_)
                    wv_ = inc("vector", "sV",
                              lambda e, c_=col: e.tensor_scalar_add(
                                  idxf[:, c_:c_ + 1], idxf[:, c_:c_ + 1], 2047.0))
                    wait("vector", "sV", wv_)
                    evac_last = inc("vector", "sV",
                                    lambda e, c_=col: e.tensor_copy(out=cg_s[:, c_:c_ + 1],
                                                                    in_=acc3[:, 2:3]))
                wait("vector", "sV", evac_last)
                idxi_done[e_] = inc("vector", "sV",
                                    lambda e, ee=e_: e.tensor_copy(
                                        out=idxi[:, ee * NSL:(ee + 1) * NSL],
                                        in_=idxf[:, ee * NSL:(ee + 1) * NSL]))

            ga_ring = [0, 0]
            tr_prev = [0, 0]


            ga_pend = {}

            def emit_gather(e_, s):
                ring = s % 2
                wait("gpsimd", "sV", idxi_done[e_])
                wait("gpsimd", "sXN0", xn_st[NT - 2])
                wait("gpsimd", "sXN1", xn_st[NT - 1])
                if ga_ring[ring]:
                    wait("gpsimd", "sT", ga_ring[ring])
                ga_pend[(e_, s)] = inc(
                    "gpsimd", "sGA%d" % ring,
                    lambda e, ee=e_, s_=s, xb=xnd[b % 2]: e.indirect_dma_start(
                        out=xe_tm[s_ % 2], out_offset=None, in_=xb,
                        in_offset=IndirectOffsetOnAxis(
                            ap=idxi[:, ee * NSL + s_:ee * NSL + s_ + 1], axis=0)), 16)

            def emit_transpose(e_, s, bank_waitA, bank_waitV):
                ring = s % 2
                wait("tensor", "sGA%d" % ring, ga_pend[(e_, s)])
                if bank_waitA[ring]:
                    wait("tensor", "sA", bank_waitA[ring])
                    bank_waitA[ring] = 0
                if bank_waitV[ring]:
                    wait("tensor", "sV", bank_waitV[ring])
                    bank_waitV[ring] = 0
                if tr_prev[ring]:
                    wait("tensor", "sV", tr_prev[ring])
                trd = 0
                for dc in range(DC):
                    fn = (lambda e, d=pbh[ring][:, dc * 128:(dc + 1) * 128],
                          m_=xe_tm[ring][:, dc * 128:(dc + 1) * 128]:
                          e.transpose(d, m_, identity))
                    if dc == DC - 1:
                        trd = inc("tensor", "sT", fn)
                    else:
                        emit("tensor", fn)
                ga_ring[ring] = trd
                wait("vector", "sT", trd)
                evs = inc("vector", "sV",
                          lambda e, s_=s:
                          e.tensor_copy(
                              out=xeT.rearrange("p (c t) -> p c t", c=DC)[:, :, s_ * 128:(s_ + 1) * 128],
                              in_=pbh[s_ % 2].rearrange("p (c t) -> p c t", c=DC)))
                tr_prev[ring] = evs
                xeT_ev[e_] = evs
                if s + 2 < NSL:
                    emit_gather(e_, s + 2)

            def emit_xeT(e_, bank_waitA, bank_waitV):
                emit_gather(e_, 0)
                emit_gather(e_, 1)
                for s in range(NSL):
                    emit_transpose(e_, s, bank_waitA, bank_waitV)

            emit_xeT(0, [0, 0], [zinv_rdy, zinv_rdy])

            gate_bank_A = [0, 0]
            up_bank_V = [0, 0]
            silu_war_V = [0, 0]
            dn_ev_V = [0, 0, 0, 0]
            ffn_last_mm = 0
            ffn_hid_last = 0
            first_ffn_mm = True
            sc_prev = 0
            sc_stage = [0, 0]
            fin_za_seed = (cnt["sZA0"], cnt["sZA1"])
            first_sc = True
            TCS = ((0, 512), (512, 128))
            gu_tok = [0] * 17
            d_tok = [0] * 17
            d_tok[0] = d_ld
            for ph in range(16):
                e_, h2_ = ph // 2, ph % 2
                pi = ph % 2
                gb_c, ub_c, db_c = gbufs[pi], ubufs[pi], dbufs[pi]
                if 1 <= ph < 15:
                    npi = (ph + 1) % 2
                    gu_tok[ph + 1] = load_gu((ph + 1) // 2, (ph + 1) % 2,
                                             prev_gu_mm[npi], npi)
                    d_tok[ph + 1] = load_d((ph + 1) // 2, (ph + 1) % 2,
                                           prev_d_mm[npi], npi)
                hid_last_V = 0
                last_gu_mm = 0
                resc_todo = list(range(NT)) if ph == 0 else []
                first_in_ph = True
                for tof, tsz in TCS:
                    for ht in range(NHT):
                        pgi = ht % 2
                        first_grp = (ph == 0 and tof == 0 and ht == 0)
                        if first_in_ph:
                            wait("tensor", "sV", xeT_ev[e_])
                            if ph > 0:
                                wait("tensor", "sGU%d" % pi, gu_tok[ph])
                            first_in_ph = False
                        if ph == 0 and tof == 0 and ht == 1:
                            wait("tensor", "sQ1", ph0_half["sQ1"])
                            wait("tensor", "sQ3", ph0_half["sQ3"])
                        if first_ffn_mm:
                            wait("tensor", "sV", zinv_rdy)
                            first_ffn_mm = False
                        if gate_bank_A[pgi]:
                            wait("tensor", "sA", gate_bank_A[pgi])
                        mg = 0
                        for kc in range(DC):
                            if first_grp and kc == 0:
                                wait("tensor", "sQ0", ph0_half["sQ0"])
                            if first_grp and kc == 4:
                                wait("tensor", "sQ1", ph0_half["sQ1"])
                            fn = (lambda e, o_=pb[pgi][:, 0:tsz],
                                  l=gb_c[:, kc * 1024 + ht * 128: kc * 1024 + (ht + 1) * 128],
                                  r=xeT[:, kc * CAP + tof: kc * CAP + tof + tsz],
                                  st=(kc == 0), sp=(kc == DC - 1):
                                  e.matmul(o_, l, r, start=st, stop=sp))
                            if kc == DC - 1:
                                mg = inc("tensor", "sT", fn)
                            else:
                                emit("tensor", fn)
                        if up_bank_V[pgi]:
                            wait("tensor", "sV", up_bank_V[pgi])
                        mu = 0
                        for kc in range(DC):
                            if first_grp and kc == 0:
                                wait("tensor", "sQ2", ph0_half["sQ2"])
                            if first_grp and kc == 4:
                                wait("tensor", "sQ3", ph0_half["sQ3"])
                            fn = (lambda e, o_=pb[2 + pgi][:, 0:tsz],
                                  l=ub_c[:, kc * 1024 + ht * 128: kc * 1024 + (ht + 1) * 128],
                                  r=xeT[:, kc * CAP + tof: kc * CAP + tof + tsz],
                                  st=(kc == 0), sp=(kc == DC - 1):
                                  e.matmul(o_, l, r, start=st, stop=sp))
                            if kc == DC - 1:
                                mu = inc("tensor", "sT", fn)
                            else:
                                emit("tensor", fn)
                        last_gu_mm = mu
                        wait("scalar", "sT", mg)
                        if silu_war_V[pgi]:
                            wait("scalar", "sV", silu_war_V[pgi])
                        asl = inc("scalar", "sA",
                                  lambda e, d=silu[pgi][:, 0:tsz], s=pb[pgi][:, 0:tsz]:
                                  e.activation(out=d, in_=s, func=AF.Silu))
                        gate_bank_A[pgi] = asl
                        wait("vector", "sA", asl)
                        wait("vector", "sT", mu)
                        hv = inc("vector", "sV",
                                 lambda e,
                                 d=hid[:, ht * CAP + tof: ht * CAP + tof + tsz],
                                 s=silu[pgi][:, 0:tsz], u=pb[2 + pgi][:, 0:tsz]:
                                 e.tensor_tensor(out=d, in0=s, in1=u, op=ALU.mult))
                        up_bank_V[pgi] = hv
                        silu_war_V[pgi] = hv
                        hid_last_V = hv
                        if resc_todo:
                            jj = resc_todo.pop(0)
                            emit_resc_stt(jj)
                            if ph == 0 and jj == 7:
                                wait("gpsimd", "sO0", cnt["sO0"])
                                wait("gpsimd", "sO1", cnt["sO1"])
                                gu_tok[1] = load_gu(0, 1, prev_gu_mm[1], 1)
                            elif ph == 0 and jj == 11:
                                wait("gpsimd", "sY0", cnt["sY0"])
                                wait("gpsimd", "sY1", cnt["sY1"])
                                d_tok[1] = load_d(0, 1, prev_d_mm[1], 1)
                if h2_ == 1 and e_ < NE - 1:
                    for s_g in range(2):
                        emit_gather(e_ + 1, s_g)
                last_dn_mm = 0
                tr_todo = list(range(NSL)) if (h2_ == 1 and e_ < NE - 1) else []
                for tile in range(NSL):
                    ring = tile % 2
                    evd = [0, 0]
                    for dh in range(2):
                        gg = 4 + (tile % 2) * 2 + dh
                        if tile == 0 and dh == 0:
                            wait("tensor", "sWD%d" % pi, d_tok[ph])
                        wait("tensor", "sV", hid_last_V)
                        if dn_ev_V[(tile % 2) * 2 + dh]:
                            wait("tensor", "sV", dn_ev_V[(tile % 2) * 2 + dh])
                        md = 0
                        for ht in range(NHT):
                            fn = (lambda e, o_=pb[gg][:, 0:512],
                                  l=hid[:, ht * CAP + tile * 128: ht * CAP + (tile + 1) * 128],
                                  r=db_c[:, ht * 1024 + dh * 512: ht * 1024 + (dh + 1) * 512],
                                  st=(ht == 0), sp=(ht == NHT - 1):
                                  e.matmul(o_, l, r, start=st, stop=sp))
                            if ht == NHT - 1:
                                md = inc("tensor", "sT", fn)
                            else:
                                emit("tensor", fn)
                        last_dn_mm = md
                        wait("vector", "sT", md)
                        if sc_stage[ring]:
                            wait("vector", "sSC", sc_stage[ring])
                        elif fin_za_seed[ring]:
                            wait("vector", "sZA%d" % ring, fin_za_seed[ring])
                        evd[dh] = inc("vector", "sV",
                                      lambda e, rg=ring, d_=dh,
                                      s=pb[gg][:, 0:512],
                                      cs=cg_s[:, e_ * NSL + tile:e_ * NSL + tile + 1]:
                                      e.tensor_scalar_mul(stage[rg][:, d_ * 512:(d_ + 1) * 512],
                                                          s, cs))
                        dn_ev_V[(tile % 2) * 2 + dh] = evd[dh]
                    wait("gpsimd", "sV", evd[1])
                    if first_sc:
                        wait("gpsimd", "sZ", zr)
                        first_sc = False
                    if sc_prev:
                        wait("gpsimd", "sSC", sc_prev)
                    sc_stage[ring] = cnt["sSC"] + 16
                    sc_prev = inc("gpsimd", "sSC",
                                  lambda e, ee=e_, t_=tile, rg=ring: e.indirect_dma_start(
                                      out=y12,
                                      out_offset=IndirectOffsetOnAxis(
                                          ap=idxi[:, ee * NSL + t_:ee * NSL + t_ + 1], axis=0),
                                      in_=stage[rg], in_offset=None,
                                      compute_op=ALU.add), 16)
                    if tr_todo and tile >= 1:
                        emit_transpose(e_ + 1, tr_todo.pop(0),
                                       [gate_bank_A[0], gate_bank_A[1]], [0, 0])
                while tr_todo:
                    emit_transpose(e_ + 1, tr_todo.pop(0),
                                   [gate_bank_A[0], gate_bank_A[1]], [0, 0])
                prev_gu_mm[pi] = last_gu_mm
                prev_d_mm[pi] = last_dn_mm
                ffn_last_mm = last_dn_mm
                if ph == 15:
                    ffn_hid_last = hid_last_V

            if dbg and b == 0:
                wait("sync", "sV", idxi_done[NE - 1])
                inc("sync", "sC", lambda e: e.dma_start(out=dbg_m[:, 0:640], in_=scrf[:, 19456:20096]), 16)
                inc("sync", "sC", lambda e: e.dma_start(out=dbg_m[:, 640:1280], in_=iota_f), 16)
                inc("sync", "sC", lambda e: e.dma_start(out=dbg_idx, in_=idxf), 16)
                inc("sync", "sC", lambda e: e.dma_start(out=dbg_cg, in_=cg_s), 16)
                wait("sync", "sSC", cnt["sSC"])
                inc("sync", "sC", lambda e: e.dma_start(out=dbg_y12, in_=y12), 16)

            def make_final(B0_=B0, n_sc=cnt["sSC"],
                           n_st=tuple((s_, cnt[s_]) for s_ in
                                      ("sO0", "sO1", "sY0", "sY1", "sU0", "sU1"))):
                def emit_final():
                    wait("sync", "sSC", n_sc)
                    for s_, v_ in n_st:
                        wait("gpsimd", s_, v_)
                    rb_n = [0, 0]
                    za_n = [0, 0]
                    for i in range(NT):
                        p = i % 2
                        if za_n[p]:
                            wait("sync", "sZA%d" % p, za_n[p])
                        rb_n[p] = inc("sync", "sR%d" % p,
                                      lambda e, i_=i, pp=p: e.dma_start(
                                          out=stage[pp],
                                          in_=y12[i_ * 128:(i_ + 1) * 128, :]), 16)
                        wait("gpsimd", "sR%d" % p, rb_n[p])
                        za_n[p] = inc("gpsimd", "sZA%d" % p,
                                      lambda e,
                                      d=out_hbm[B0_ + i * 128:B0_ + (i + 1) * 128, :],
                                      pp=p: e.dma_start(out=d, in_=stage[pp],
                                                        accum_op=ALU.add), 16)
                return emit_final
            pending_final = make_final()

            resc_last = resc_j[NT - 1]
            end_prev = {"sT": ffn_last_mm, "sA": cnt["sA"], "sV": cnt["sV"],
                        "sO0": cnt["sO0"], "sO1": cnt["sO1"], "hid": ffn_hid_last,
                        "sY0": cnt["sY0"], "sY1": cnt["sY1"],
                        "sU0": cnt["sU0"], "sU1": cnt["sU1"],
                        "sSC": cnt["sSC"],
                        "sZA0": cnt["sZA0"], "sZA1": cnt["sZA1"]}

        if pending_final is not None:
            pending_final()
            pending_final = None
        for s_ in ("sO0", "sO1", "sY0", "sY1", "sU0", "sU1"):
            wait("sync", s_, cnt[s_])
        wait("sync", "sZA0", cnt["sZA0"])
        wait("sync", "sZA1", cnt["sZA1"])

        with nc.Block() as block:
            @block.sync
            def _(e):
                for f in ops["sync"]:
                    f(e)

            @block.gpsimd
            def _(e):
                for f in ops["gpsimd"]:
                    f(e)

            @block.tensor
            def _(e):
                for f in ops["tensor"]:
                    f(e)

            @block.scalar
            def _(e):
                for f in ops["scalar"]:
                    f(e)

            @block.vector
            def _(e):
                for f in ops["vector"]:
                    f(e)

    return nc


def make_consts():
    s = np.arange(128)[:, None]
    t = np.arange(128)[None, :]
    tid = (np.arange(16)[None, :] * 128 + np.arange(128)[:, None]).astype(np.float32)
    return {
        "ident": np.eye(128, dtype=ml_dtypes.bfloat16),
        "ident32": np.eye(128, dtype=np.float32),
        "maskd": np.ascontiguousarray(np.where(s <= t, 0.0, NEG).astype(np.float32)),
        "lt": np.ascontiguousarray(np.tril(np.ones((128, 128), np.float32), -1).T
                                   .astype(ml_dtypes.bfloat16)),
        "ones_c": np.ones((128, 1), dtype=ml_dtypes.bfloat16),
        "ones_r": np.ones((1, 128), dtype=np.float32),
        "eps": np.full((128, 1), 1e-5, dtype=np.float32),
        "tidc": np.ascontiguousarray(tid),
        "zsrc": np.zeros((T, D), dtype=np.float32),
    }


def make_in_maps(inputs, consts):
    x = np.asarray(inputs["L0_attn_x"], dtype=np.float32)
    shared = {
        "wq": np.ascontiguousarray(np.asarray(inputs["L0_attn_Wq"], np.float32)),
        "wk": np.ascontiguousarray(np.asarray(inputs["L0_attn_Wk"], np.float32)),
        "wv": np.ascontiguousarray(np.asarray(inputs["L0_attn_Wv"], np.float32)),
        "bqT": np.ascontiguousarray(np.asarray(inputs["L0_attn_bq"], np.float32).reshape(DC, 128).T),
        "bkT": np.ascontiguousarray(np.asarray(inputs["L0_attn_bk"], np.float32).reshape(DC, 128).T),
        "bvb": np.ascontiguousarray(np.broadcast_to(np.asarray(inputs["L0_attn_bv"], np.float32), (128, D))),
        "rw": np.ascontiguousarray(np.asarray(inputs["L0_ffn_router_w"], np.float32)),
        "gw": np.ascontiguousarray(np.asarray(inputs["L0_ffn_gate_w"], np.float32)),
        "uw": np.ascontiguousarray(np.asarray(inputs["L0_ffn_up_w"], np.float32)),
        "dw": np.ascontiguousarray(np.asarray(inputs["L0_ffn_down_w"], np.float32)),
    }
    shared.update(consts)
    in_maps = []
    for i in range(NCORES):
        m = dict(shared)
        m["x"] = np.ascontiguousarray(x[2 * i:2 * i + 2].reshape(TB * T, D))
        in_maps.append(m)
    return in_maps


def kernel(**inputs):
    nc = build_program(dbg=False)
    in_maps = make_in_maps(inputs, make_consts())
    res = run_bass_kernel_spmd(nc, in_maps, core_ids=list(range(NCORES)))
    outs = [res.results[i]["out"].reshape(TB, T, D) for i in range(NCORES)]
    return np.concatenate(outs, axis=0).astype(np.float32)


# version 13
# speedup vs baseline: 1.7857x; 1.1548x over previous
import numpy as np
import ml_dtypes
import concourse.bass as bass
import concourse.mybir as mybir
from concourse.bass_utils import run_bass_kernel_spmd
from concourse.bass_core_rust import IndirectOffsetOnAxis
from contextlib import ExitStack

F32 = mybir.dt.float32
BF16 = mybir.dt.bfloat16
I32 = mybir.dt.int32
AF = mybir.ActivationFunctionType
ALU = mybir.AluOpType
AX = mybir.AxisListType

NCORES = 8
TB = 2
T = 2048
D = 1024
DC = 8
NT = 16
NE = 8
HH = 1024
NHT = 8
NEG = -3.0e38
ENGS = ("sync", "gpsimd", "tensor", "scalar", "vector")


def build_program(dbg=False):
    nc = bass.Bass("TRN2", target_bir_lowering=False)

    x_hbm = nc.dram_tensor("x", [TB * T, D], F32, kind="ExternalInput").ap()
    wq_h = nc.dram_tensor("wq", [D, D], F32, kind="ExternalInput").ap()
    wk_h = nc.dram_tensor("wk", [D, D], F32, kind="ExternalInput").ap()
    wv_h = nc.dram_tensor("wv", [D, D], F32, kind="ExternalInput").ap()
    bqT_h = nc.dram_tensor("bqT", [128, DC], F32, kind="ExternalInput").ap()
    bkT_h = nc.dram_tensor("bkT", [128, DC], F32, kind="ExternalInput").ap()
    bvb_h = nc.dram_tensor("bvb", [128, D], F32, kind="ExternalInput").ap()
    rw_h = nc.dram_tensor("rw", [D, NE], F32, kind="ExternalInput").ap()
    gw_h = nc.dram_tensor("gw", [NE, D, 2 * HH], F32, kind="ExternalInput").ap()
    uw_h = nc.dram_tensor("uw", [NE, D, 2 * HH], F32, kind="ExternalInput").ap()
    dw_h = nc.dram_tensor("dw", [NE, 2 * HH, D], F32, kind="ExternalInput").ap()
    ident_h = nc.dram_tensor("ident", [128, 128], BF16, kind="ExternalInput").ap()
    ident32_h = nc.dram_tensor("ident32", [128, 128], F32, kind="ExternalInput").ap()
    maskd_h = nc.dram_tensor("maskd", [128, 128], F32, kind="ExternalInput").ap()
    lt_h = nc.dram_tensor("lt", [128, 128], BF16, kind="ExternalInput").ap()
    ones_c_h = nc.dram_tensor("ones_c", [128, 1], BF16, kind="ExternalInput").ap()
    ones_r_h = nc.dram_tensor("ones_r", [1, 128], F32, kind="ExternalInput").ap()
    eps_h = nc.dram_tensor("eps", [128, 1], F32, kind="ExternalInput").ap()
    zsrc_h = nc.dram_tensor("zsrc", [T, D], F32, kind="ExternalInput").ap()
    tidc_h = nc.dram_tensor("tidc", [128, 16], F32, kind="ExternalInput").ap()
    thr4_h = nc.dram_tensor("thr4", [128, 4], F32, kind="ExternalInput").ap()
    out_hbm = nc.dram_tensor("out", [TB * T, D], F32, kind="ExternalOutput").ap()
    xnd = [nc.dram_tensor("xnd%d" % i, [T, D], BF16, kind="Internal").ap()
           for i in range(2)]
    y12 = nc.dram_tensor("y12d", [T, D], F32, kind="Internal").ap()
    if dbg:
        dbg_xnT = nc.dram_tensor("dbg_xnT", [128, DC * T], BF16, kind="ExternalOutput").ap()
        dbg_yat = nc.dram_tensor("dbg_yat", [128, NT * D], F32, kind="ExternalOutput").ap()
        dbg_c = nc.dram_tensor("dbg_c", [128, NT * NE], F32, kind="ExternalOutput").ap()
        dbg_s = nc.dram_tensor("dbg_s", [1, 8], F32, kind="ExternalOutput").ap()
        dbg_idx = nc.dram_tensor("dbg_idx", [128, 40], F32, kind="ExternalOutput").ap()
        dbg_cg = nc.dram_tensor("dbg_cg", [128, 40], F32, kind="ExternalOutput").ap()
        dbg_y12 = nc.dram_tensor("dbg_y12", [T, D], F32, kind="ExternalOutput").ap()
        dbg_m = nc.dram_tensor("dbg_m", [128, 1280], F32, kind="ExternalOutput").ap()

    stack = ExitStack()
    with stack:
        xnT = stack.enter_context(nc.sbuf_tensor("xnT", [128, DC * T], BF16)).ap()
        y_t = stack.enter_context(nc.sbuf_tensor("y", [128, NT * D], F32))
        y = y_t.ap()
        yb = y_t.ap().bitcast(BF16)
        scr_t = stack.enter_context(nc.sbuf_tensor("scr", [128, 53248], BF16))
        scr = scr_t.ap()
        scrf = scr_t.ap().bitcast(F32)
        xnb_t = stack.enter_context(nc.sbuf_tensor("xnb_t", [128, 2048], BF16)).ap()

        identity = stack.enter_context(nc.sbuf_tensor("identity_s", [128, 128], BF16)).ap()
        ident32 = stack.enter_context(nc.sbuf_tensor("ident32_s", [128, 128], F32)).ap()
        maskd = stack.enter_context(nc.sbuf_tensor("maskd_s", [128, 128], F32)).ap()
        ones_c = stack.enter_context(nc.sbuf_tensor("ones_c_s", [128, 1], BF16)).ap()
        ones_r = stack.enter_context(nc.sbuf_tensor("ones_r_s", [1, 128], F32)).ap()
        eps_t = stack.enter_context(nc.sbuf_tensor("eps_t", [128, 1], F32)).ap()
        thr4_s = stack.enter_context(nc.sbuf_tensor("thr4_s", [128, 4], F32)).ap()
        bqT = stack.enter_context(nc.sbuf_tensor("bqT_s", [128, DC], F32)).ap()
        bkT = stack.enter_context(nc.sbuf_tensor("bkT_s", [128, DC], F32)).ap()
        rw_s = stack.enter_context(nc.sbuf_tensor("rw_s", [128, DC * NE], F32)).ap()
        c_all = stack.enter_context(nc.sbuf_tensor("c_all", [128, NT * NE], F32)).ap()
        lg_all = stack.enter_context(nc.sbuf_tensor("lg_all", [128, NT * NE], F32)).ap()
        l2_all = stack.enter_context(nc.sbuf_tensor("l2_all", [128, NT * NE], F32)).ap()
        rts = stack.enter_context(nc.sbuf_tensor("rts", [128, 48], F32)).ap()
        nacc = rts[0:1, 0:16]
        zacc = rts[0:1, 16:32]
        s1v = rts[0:1, 32:40]
        sca = rts[:, 40:42]

        stats = l2_all[:, 0:32]

        pb = [stack.enter_context(nc.psum_tensor("pb%d" % i, [128, 512], F32)).ap()
              for i in range(8)]
        pbh = [p.bitcast(BF16) for p in pb]

        kT = scr[:, 0:16384]
        v_s = scr[:, 16384:32768]
        w0 = scr[:, 32768:40960]
        w1 = scr[:, 40960:49152]
        qT = scr[:, 49152:53248]
        pt = [scr[:, 40960:41216], scr[:, 41216:41472]]
        sqb = [scr[:, 41472:41984], scr[:, 41984:42496]]
        xst = [scrf[:, 24576:25600], scrf[:, 25600:26624]]
        xr = [scrf[:, 24576:25600], scrf[:, 25600:26624],
              scrf[:, 21504:22528], scrf[:, 22528:23552], scrf[:, 23552:24576]]
        bvb_v = scrf[:, 24576:25600]
        xnb = [xnb_t[:, 0:1024], xnb_t[:, 1024:2048]]
        scri = scr_t.ap().bitcast(I32)
        CAP = 640
        NSL = 5
        gbuf = scr[:, 0:8192]
        ubuf = scr[:, 8192:16384]
        dbuf = scr[:, 16384:24576]
        gbufs = [gbuf, yb[:, 0:8192]]
        ubufs = [ubuf, yb[:, 8192:16384]]
        dbufs = [dbuf, yb[:, 16384:24576]]
        hid = scr[:, 24576:29696]
        ohAB = [scrf[:, 16384:17408], scrf[:, 17408:18432]]
        rhs15 = scrf[:, 19720:19960]
        mask5 = scrf[:, 19960:20040]
        q_s2 = scrf[:, 20044:20060]
        cmp4 = scrf[:, 20064:20128]
        r_s2 = scrf[:, 20128:20144]
        acc15 = scrf[:, 20144:20159]
        xeT = scr[:, 29696:34816]
        silu = [scrf[:, 17408:17920], scrf[:, 17920:18432]]
        xe_tm = [scr[:, 36864:37888], scr[:, 37888:38912]]
        idxf = scrf[:, 19456:19496]
        cg_s = scrf[:, 19496:19536]
        idxi = scri[:, 19536:19576]
        acc3 = scrf[:, 19576:19579]
        rhs48 = scrf[:, 19584:19632]
        pos_s = scrf[:, 19632:19648]
        m_s = scrf[:, 19648:19664]
        sc_s = scrf[:, 19664:19680]
        b_s = scrf[:, 19680:19681]
        rt_b = scr[:, 39424:39425]
        tidc = scrf[:, 19696:19712]
        stage = [scrf[:, 20480:21504], scrf[:, 21504:22528]]
        iota_f = scrf[:, 20480:21120]
        iota_i = scri[:, 21504:22144]
        lt_s = scr[:, 44288:44416]
        xn32 = scrf[:, 22528:23552]
        xnT32 = scrf[:, 23552:24576]

        ops = {e: [] for e in ENGS}
        sems = {}
        cnt = {}
        for nm in ("sC", "sX0", "sX1", "sW0", "sW1",
                   "sO0", "sO1", "sY0", "sY1", "sU0", "sU1",
                   "sGU0", "sGU1", "sWD0", "sWD1",
                   "sT", "sA", "sV", "sP",
                   "sR0", "sR1",
                   "sQ0", "sQ1", "sQ2", "sQ3",
                   "sXN0", "sXN1", "sZ", "sZA0", "sZA1", "sSC", "sGA0", "sGA1"):
            sems[nm] = stack.enter_context(nc.semaphore(nm))
            cnt[nm] = 0

        def add(e, fn):
            ops[e].append(fn)

        def wait(e, snm, val):
            if val > 0:
                add(e, lambda eng, s=sems[snm], v=val: eng.wait_ge(s, v))

        def inc(e, snm, fn, k=1):
            cnt[snm] += k
            add(e, lambda eng, f=fn, s=sems[snm], kk=k: f(eng).then_inc(s, kk))
            return cnt[snm]

        def emit(e, fn):
            add(e, fn)

        for dst, csrc in ((identity, ident_h), (ident32, ident32_h), (maskd, maskd_h),
                          (ones_c, ones_c_h), (ones_r, ones_r_h), (eps_t, eps_h),
                          (bqT, bqT_h), (bkT, bkT_h), (thr4_s, thr4_h)):
            inc("sync", "sC", lambda e, d=dst, s=csrc: e.dma_start(out=d, in_=s), 16)
        inc("sync", "sC",
            lambda e: e.dma_start(out=rw_s.rearrange("p (c e) -> p c e", c=DC),
                                  in_=rw_h.rearrange("(c p) e -> p c e", p=128)), 16)
        n_const = cnt["sC"]
        wait("scalar", "sC", n_const)
        wait("vector", "sC", n_const)
        wait("tensor", "sC", n_const)

        end_prev = {"sT": 0, "sA": 0, "sV": 0, "sO0": 0, "sO1": 0, "hid": 0,
                    "sY0": 0, "sY1": 0, "sU0": 0, "sU1": 0,
                    "sSC": 0, "sZA0": 0, "sZA1": 0}
        prev_gu_mm = [0, 0]
        prev_d_mm = [0, 0]
        pending_final = None
        fin_za_gate = (0, 0)
        resc_last = 0

        for b in range(TB):
            B0 = b * T

            xcast = [0] * NT
            xn_st = [0] * NT
            xnT_cp = [0] * NT
            tr_done = [0] * NT
            f32tr_done = [0] * NT
            xn32cp = [0] * NT
            rt_mm = [0] * NT
            rt_ev = [0] * NT
            for i in range(NT):
                p = i % 2
                o = p * 16
                MEAN, VAR, STD, RSTD, NB = o + 12, o + 13, o + 14, o + 15, o + 0
                if i >= 2:
                    wait("sync", "sA", xcast[i - 2])
                elif b > 0:
                    wait("sync", "sV", resc_last)
                sxp = "sX%d" % p
                nx = inc("sync", sxp,
                         lambda e, d=xst[p], s=x_hbm[B0 + i * 128:B0 + (i + 1) * 128, :]:
                         e.dma_start(out=d, in_=s), 16)
                wait("vector", sxp, nx)
                emit("vector", lambda e, d=stats[:, o:o + 6], s=xst[p][:, 0:512]:
                     e.bn_stats(d, s))
                wbn = inc("vector", "sV",
                          lambda e, d=stats[:, o + 6:o + 12], s=xst[p][:, 512:1024]:
                          e.bn_stats(d, s))
                wait("vector", "sV", wbn)
                wag = inc("vector", "sV",
                          lambda e, d=stats[:, MEAN:MEAN + 2], s=stats[:, o:o + 12]:
                          e.bn_aggr(d, s))
                wait("scalar", "sV", wag)
                wstd = inc("scalar", "sA",
                           lambda e, d=stats[:, STD:STD + 1], s=stats[:, VAR:VAR + 1]:
                           e.activation(out=d, in_=s, func=AF.Sqrt, bias=eps_t))
                wait("vector", "sA", wstd)
                wrs = inc("vector", "sV",
                          lambda e, d=stats[:, RSTD:RSTD + 1], s=stats[:, STD:STD + 1]:
                          e.reciprocal(out=d, in_=s))
                wait("vector", "sV", wrs)
                wnb = inc("vector", "sV",
                          lambda e, d=stats[:, NB:NB + 1], s=stats[:, MEAN:MEAN + 1],
                          r=stats[:, RSTD:RSTD + 1]:
                          e.scalar_tensor_tensor(out=d, in0=s, scalar=-1.0, in1=r,
                                                 op0=ALU.mult, op1=ALU.mult))
                wait("scalar", "sV", wnb)
                if i >= 1:
                    wait("scalar", "sT", f32tr_done[i - 1])
                a32 = inc("scalar", "sA",
                          lambda e, d=xn32, s=xst[p], sc=stats[:, RSTD:RSTD + 1],
                          bi=stats[:, NB:NB + 1]:
                          e.activation(out=d, in_=s, func=AF.Identity, scale=sc, bias=bi))
                wait("scalar", "sA", a32)
                if i >= 2:
                    wait("scalar", "sT", tr_done[i - 2])
                    wait("scalar", "sXN%d" % p, xn_st[i - 2])
                xcast[i] = inc("scalar", "sA",
                               lambda e, d=xnb[p], s=xn32:
                               e.activation(out=d, in_=s, func=AF.Copy))
                wait("sync", "sA", xcast[i])
                xn_st[i] = inc("sync", "sXN%d" % p,
                               lambda e, d=xnd[b % 2][i * 128:(i + 1) * 128, :],
                               s=xnb[p]: e.dma_start(out=d, in_=s), 16)
                wait("tensor", "sA", xcast[i])
                if i < 2:
                    if b > 0:
                        wait("tensor", "sA", end_prev["sA"])
                else:
                    wait("tensor", "sV", xnT_cp[i - 2])
                for dc in range(DC):
                    fn = (lambda e, d=pbh[p][:, dc * 128:(dc + 1) * 128],
                          s=xnb[p][:, dc * 128:(dc + 1) * 128]:
                          e.transpose(d, s, identity))
                    if dc == DC - 1:
                        tr_done[i] = inc("tensor", "sT", fn)
                    else:
                        emit("tensor", fn)
                if i > 0:
                    wait("tensor", "sV", xn32cp[i - 1])
                elif b > 0:
                    wait("tensor", "sV", end_prev["hid"])
                for dc in range(DC):
                    fn = (lambda e,
                          d=pb[2 + dc // 4][:, (dc % 4) * 128:((dc % 4) + 1) * 128],
                          s=xn32[:, dc * 128:(dc + 1) * 128]:
                          e.transpose(d, s, ident32))
                    if dc == DC - 1:
                        f32tr_done[i] = inc("tensor", "sT", fn)
                    else:
                        emit("tensor", fn)
                wait("vector", "sT", tr_done[i])
                xnT_cp[i] = inc("vector", "sV",
                                lambda e,
                                d=xnT.rearrange("p (c t) -> p c t", c=DC)[:, :, i * 128:(i + 1) * 128],
                                s=pbh[p].rearrange("p (c t) -> p c t", c=DC):
                                e.tensor_copy(out=d, in_=s))
                wait("vector", "sT", f32tr_done[i])
                if i > 0:
                    wait("vector", "sT", rt_mm[i - 1])
                emit("vector",
                     lambda e: e.tensor_copy(out=xnT32[:, 0:512], in_=pb[2][:, 0:512]))
                xn32cp[i] = inc("vector", "sV",
                                lambda e: e.tensor_copy(out=xnT32[:, 512:1024],
                                                        in_=pb[3][:, 0:512]))
                lb = 4 + (i % 2)
                wait("tensor", "sV", xn32cp[i])
                if i > 1:
                    wait("tensor", "sA", rt_ev[i - 2])
                elif b > 0:
                    wait("tensor", "sV", end_prev["sV"])
                for kc in range(DC):
                    fn = (lambda e, o_=pb[lb][:, 0:8],
                          l=xnT32[:, kc * 128:(kc + 1) * 128],
                          r=rw_s[:, kc * 8:(kc + 1) * 8],
                          st=(kc == 0), sp=(kc == DC - 1):
                          e.matmul(o_, l, r, start=st, stop=sp))
                    if kc == DC - 1:
                        rt_mm[i] = inc("tensor", "sT", fn)
                    else:
                        emit("tensor", fn)
                if i > 0:
                    wait("scalar", "sT", rt_mm[i - 1])
                    rt_ev[i - 1] = inc("scalar", "sA",
                                       lambda e, d=lg_all[:, (i - 1) * 8:i * 8],
                                       s=pb[4 + ((i - 1) % 2)][:, 0:8]:
                                       e.activation(out=d, in_=s, func=AF.Copy))
                if i == NT - 1:
                    wait("scalar", "sT", rt_mm[i])
                    rt_ev[i] = inc("scalar", "sA",
                                   lambda e, d=lg_all[:, i * 8:(i + 1) * 8],
                                   s=pb[lb][:, 0:8]:
                                   e.activation(out=d, in_=s, func=AF.Copy))

            lg3 = lg_all.rearrange("p (t e) -> p t e", t=NT)
            l23 = l2_all.rearrange("p (t e) -> p t e", t=NT)
            c3 = c_all.rearrange("p (t e) -> p t e", t=NT)
            def rbc(col):
                return rts[:, col:col + 16].rearrange("p (t o) -> p t o", o=1) \
                          .broadcast_to([128, NT, NE])
            wait("vector", "sA", rt_ev[NT - 1])
            wv_ = inc("vector", "sV",
                      lambda e: e.tensor_reduce(rts[:, 0:16], lg3, axis=AX.X, op=ALU.max))
            wait("vector", "sV", wv_)
            wv_ = inc("vector", "sV",
                      lambda e: e.tensor_tensor(out=c3, in0=lg3, in1=rbc(0), op=ALU.is_equal))
            wait("vector", "sV", wv_)
            wv_ = inc("vector", "sV",
                      lambda e: e.scalar_tensor_tensor(out=l23, in0=c3, scalar=NEG,
                                                       in1=lg3, op0=ALU.mult, op1=ALU.add))
            wait("vector", "sV", wv_)
            wv_ = inc("vector", "sV",
                      lambda e: e.tensor_reduce(rts[:, 16:32], l23, axis=AX.X, op=ALU.max))
            wait("vector", "sV", wv_)
            wv_ = inc("vector", "sV",
                      lambda e: e.tensor_sub(rts[:, 32:48], rts[:, 16:32], rts[:, 0:16]))
            wait("scalar", "sV", wv_)
            wa_ = inc("scalar", "sA",
                      lambda e: e.activation(out=rts[:, 0:16], in_=rts[:, 32:48],
                                             func=AF.Sigmoid, scale=-1.0))
            wait("scalar", "sA", wa_)
            wa_ = inc("scalar", "sA",
                      lambda e: e.activation(out=rts[:, 32:48], in_=rts[:, 32:48],
                                             func=AF.Sigmoid))
            wait("vector", "sA", wa_)
            wv_ = inc("vector", "sV",
                      lambda e: e.tensor_tensor(out=c3, in0=c3, in1=rbc(0), op=ALU.mult))
            wait("vector", "sV", wv_)
            wv_ = inc("vector", "sV",
                      lambda e: e.tensor_tensor(out=l23, in0=l23, in1=rbc(16), op=ALU.is_equal))
            wait("vector", "sV", wv_)
            wv_ = inc("vector", "sV",
                      lambda e: e.tensor_tensor(out=l23, in0=l23, in1=rbc(32), op=ALU.mult))
            wait("vector", "sV", wv_)
            c_rdy = inc("vector", "sV",
                        lambda e: e.tensor_add(c_all, c_all, l2_all))

            if dbg and b == 0:
                wait("sync", "sV", xnT_cp[NT - 1])
                inc("sync", "sC", lambda e: e.dma_start(out=dbg_xnT, in_=xnT), 16)
                wait("sync", "sV", c_rdy)
                inc("sync", "sC", lambda e: e.dma_start(out=dbg_c, in_=c_all), 16)

            if pending_final is not None:
                pending_final()
                pending_final = None
            fin_za_gate = (cnt["sZA0"], cnt["sZA1"])

            wait("sync", "sA", xcast[NT - 1])
            bvb_ld = inc("sync", "sC",
                         lambda e: e.dma_start(out=bvb_v, in_=bvb_h), 16)

            if b > 0:
                wait("gpsimd", "sT", end_prev["sT"])
                wait("gpsimd", "sSC", end_prev["sSC"])
            wk_ld = inc("gpsimd", "sW0",
                        lambda e: e.dma_start(
                            out=w0.rearrange("p (c d) -> p c d", c=DC),
                            in_=wk_h.rearrange("(c p) d -> p c d", p=128)), 16)
            wait("gpsimd", "sT", rt_mm[NT - 1])
            wait("gpsimd", "sA", xcast[NT - 1])
            if b > 0:
                wait("gpsimd", "sZA0", cnt["sZA0"])
                wait("gpsimd", "sZA1", cnt["sZA1"])
            wv_ld = inc("gpsimd", "sW1",
                        lambda e: e.dma_start(
                            out=w1.rearrange("p (c d) -> p c d", c=DC),
                            in_=wv_h.rearrange("(c p) d -> p c d", p=128)), 16)

            kT_ev = {}
            g_ev = [0, 0, 0, 0]
            gi = 0
            last_kT_mm = 0
            first_kt = True
            for dt in range(DC):
                for tc4 in range(4):
                    g = gi % 4
                    gi += 1
                    wait("tensor", "sW0", wk_ld)
                    wait("tensor", "sV", xnT_cp[tc4 * 4 + 3])
                    if first_kt:
                        wait("tensor", "sV", xn32cp[NT - 1])
                        first_kt = False
                    if g_ev[g]:
                        wait("tensor", "sA", g_ev[g])
                    for kc in range(DC):
                        fn = (lambda e, o_=pb[g][:, 0:512],
                              l=w0[:, kc * 1024 + dt * 128: kc * 1024 + (dt + 1) * 128],
                              r=xnT[:, kc * 2048 + tc4 * 512: kc * 2048 + (tc4 + 1) * 512],
                              st=(kc == 0), sp=(kc == DC - 1):
                              e.matmul(o_, l, r, start=st, stop=sp))
                        if kc == DC - 1:
                            last_kT_mm = inc("tensor", "sT", fn)
                        else:
                            emit("tensor", fn)
                    wait("scalar", "sT", last_kT_mm)
                    g_ev[g] = inc("scalar", "sA",
                                  lambda e,
                                  d=kT[:, dt * 2048 + tc4 * 512: dt * 2048 + (tc4 + 1) * 512],
                                  s=pb[g][:, 0:512], bi=bkT[:, dt:dt + 1]:
                                  e.activation(out=d, in_=s, func=AF.Identity, bias=bi))
                    kT_ev[(dt, tc4)] = g_ev[g]
            kT_done_A = cnt["sA"]

            wait("gpsimd", "sT", last_kT_mm)
            wq_ld = inc("gpsimd", "sW0",
                        lambda e: e.dma_start(
                            out=w0.rearrange("p (c d) -> p c d", c=DC),
                            in_=wq_h.rearrange("(c p) d -> p c d", p=128)), 16)

            g_evv = [0, 0, 0, 0]
            gi = 0
            last_v_mm = 0
            first_v = True
            for st_ in range(NT):
                for dh in range(2):
                    g = gi % 4
                    gi += 1
                    wait("tensor", "sW1", wv_ld)
                    wait("tensor", "sV", xnT_cp[st_])
                    if first_v:
                        wait("tensor", "sA", kT_done_A)
                    if g_evv[g]:
                        wait("tensor", "sV", g_evv[g])
                    for kc in range(DC):
                        fn = (lambda e, o_=pb[g][:, 0:512],
                              l=xnT[:, kc * 2048 + st_ * 128: kc * 2048 + (st_ + 1) * 128],
                              r=w1[:, kc * 1024 + dh * 512: kc * 1024 + (dh + 1) * 512],
                              st=(kc == 0), sp=(kc == DC - 1):
                              e.matmul(o_, l, r, start=st, stop=sp))
                        if kc == DC - 1:
                            last_v_mm = inc("tensor", "sT", fn)
                        else:
                            emit("tensor", fn)
                    wait("vector", "sT", last_v_mm)
                    if first_v:
                        wait("vector", "sC", bvb_ld)
                        first_v = False
                    g_evv[g] = inc("vector", "sV",
                                   lambda e,
                                   d=v_s[:, st_ * 1024 + dh * 512: st_ * 1024 + (dh + 1) * 512],
                                   s=pb[g][:, 0:512], bi=bvb_v[:, dh * 512:(dh + 1) * 512]:
                                   e.tensor_tensor(out=d, in0=s, in1=bi, op=ALU.add))
            v_done_V = cnt["sV"]

            sq_mmT = 0
            kn_ev = 0
            wait("scalar", "sT", last_v_mm)
            if b > 0:
                wait("scalar", "sSC", end_prev["sSC"])
                wait("scalar", "sZA0", fin_za_gate[0])
                wait("scalar", "sZA1", fin_za_gate[1])
            for j in range(4):
                for dt in range(DC):
                    jj = dt % 2
                    wait("scalar", "sA", kT_ev[(dt, j)])
                    if sq_mmT:
                        wait("scalar", "sT", sq_mmT)
                    a_k = inc("scalar", "sA",
                              lambda e, d=sqb[jj],
                              s=kT[:, dt * 2048 + j * 512: dt * 2048 + (j + 1) * 512]:
                              e.square(out=d, in_=s))
                    wait("tensor", "sA", a_k)
                    if j == 0 and dt == 0:
                        wait("tensor", "sA", rt_ev[NT - 1])
                    if dt == 0 and kn_ev:
                        wait("tensor", "sV", kn_ev)
                    sq_mmT = inc("tensor", "sT",
                                 lambda e, r=sqb[jj], st=(dt == 0), sp=(dt == DC - 1):
                                 e.matmul(pb[4][0:1, 0:512], ones_c, r, start=st, stop=sp))
                wait("vector", "sT", sq_mmT)
                kn_ev = inc("vector", "sV",
                            lambda e, d=nacc[0:1, 8 + j:9 + j]:
                            e.reduce_max(d, pb[4][0:1, 0:512], axis=AX.X))

            def qT_proj(cc, qb_wA, qb_wV, scalar_pre_V=0):
                evA = 0
                lastT = 0
                first = True
                for dt in range(DC):
                    g = 6 + dt % 2
                    wait("tensor", "sW0", wq_ld)
                    if qb_wA[dt % 2]:
                        wait("tensor", "sA", qb_wA[dt % 2])
                    if qb_wV[dt % 2]:
                        wait("tensor", "sV", qb_wV[dt % 2])
                        qb_wV[dt % 2] = 0
                    for kc in range(DC):
                        fn = (lambda e, o_=pb[g][:, 0:512],
                              l=w0[:, kc * 1024 + dt * 128: kc * 1024 + (dt + 1) * 128],
                              r=xnT[:, kc * 2048 + cc * 512: kc * 2048 + (cc + 1) * 512],
                              st=(kc == 0), sp=(kc == DC - 1):
                              e.matmul(o_, l, r, start=st, stop=sp))
                        if kc == DC - 1:
                            lastT = inc("tensor", "sT", fn)
                        else:
                            emit("tensor", fn)
                    wait("scalar", "sT", lastT)
                    if first and scalar_pre_V:
                        wait("scalar", "sV", scalar_pre_V)
                        first = False
                    evA = inc("scalar", "sA",
                              lambda e, d=qT[:, dt * 512:(dt + 1) * 512],
                              s=pb[g][:, 0:512], bi=bqT[:, dt:dt + 1]:
                              e.activation(out=d, in_=s, func=AF.Identity, bias=bi))
                    qb_wA[dt % 2] = evA
                return evA, lastT

            qb_wA = [0, 0]
            qn_ev = 0
            for cc in range(4):
                q_evA, _ = qT_proj(cc, qb_wA, [0, 0],
                                   scalar_pre_V=(v_done_V if cc == 0 else 0))
                for dt in range(DC):
                    jj = dt % 2
                    wait("scalar", "sA", q_evA)
                    if sq_mmT:
                        wait("scalar", "sT", sq_mmT)
                    a_q = inc("scalar", "sA",
                              lambda e, d=sqb[jj], s=qT[:, dt * 512:(dt + 1) * 512]:
                              e.square(out=d, in_=s))
                    wait("tensor", "sA", a_q)
                    if cc == 0 and dt == 0:
                        wait("tensor", "sV", kn_ev)
                    if dt == 0 and qn_ev:
                        wait("tensor", "sV", qn_ev)
                    sq_mmT = inc("tensor", "sT",
                                 lambda e, r=sqb[jj], st=(dt == 0), sp=(dt == DC - 1):
                                 e.matmul(pb[5][0:1, 0:512], ones_c, r, start=st, stop=sp))
                wait("vector", "sT", sq_mmT)
                qn_ev = inc("vector", "sV",
                            lambda e, d=nacc[0:1, cc:cc + 1]:
                            e.reduce_max(d, pb[5][0:1, 0:512], axis=AX.X))

            wait("vector", "sV", qn_ev)
            wv_ = inc("vector", "sV",
                      lambda e: e.reduce_max(s1v[0:1, 0:1], nacc[0:1, 0:4], axis=AX.X))
            wait("vector", "sV", wv_)
            wv_ = inc("vector", "sV",
                      lambda e: e.reduce_max(s1v[0:1, 1:2], nacc[0:1, 8:12], axis=AX.X))
            wait("scalar", "sV", wv_)
            wa_ = inc("scalar", "sA",
                      lambda e: e.activation(out=s1v[0:1, 2:3], in_=s1v[0:1, 0:1], func=AF.Sqrt))
            wait("scalar", "sA", wa_)
            wa_ = inc("scalar", "sA",
                      lambda e: e.activation(out=s1v[0:1, 3:4], in_=s1v[0:1, 1:2], func=AF.Sqrt))
            wait("vector", "sA", wa_)
            wv_ = inc("vector", "sV",
                      lambda e: e.scalar_tensor_tensor(out=s1v[0:1, 4:5], in0=s1v[0:1, 2:3],
                                                       scalar=-1.0 / 32.0, in1=s1v[0:1, 3:4],
                                                       op0=ALU.mult, op1=ALU.mult))
            wait("tensor", "sV", wv_)
            bc_mm = inc("tensor", "sT",
                        lambda e: e.matmul(pb[5][:, 0:1], ones_r, s1v[0:1, 4:5],
                                           start=True, stop=True))
            wait("vector", "sT", bc_mm)
            negC_rdy = inc("vector", "sV",
                           lambda e: e.tensor_copy(out=sca[:, 0:1], in_=pb[5][:, 0:1]))

            out_ev = [0, 0, 0, 0]
            sc_exA = [0, 0]
            pt_rdT = [0, 0]
            zr_evV = 0
            last_pv_mm = 0
            for cc in range(4):
                qb_wV2 = [zr_evV, zr_evV]
                q_evA, _ = qT_proj(cc, qb_wA, qb_wV2)
                for c2 in range(2):
                    c = 2 * cc + c2
                    toff = c2 * 256
                    nk = 2 * c + 2
                    for k in range(nk):
                        sb_ = 4 + (k % 2)
                        wait("tensor", "sA", q_evA)
                        if sc_exA[k % 2]:
                            wait("tensor", "sA", sc_exA[k % 2])
                        elif k % 2 == 0 and cc == 0 and c2 == 0:
                            wait("tensor", "sV", kn_ev)
                        elif k % 2 == 1 and cc == 0 and c2 == 0:
                            wait("tensor", "sV", negC_rdy)
                        if k == 0:
                            for gg in range(4):
                                if out_ev[gg]:
                                    wait("tensor", "sV", out_ev[gg])
                            if b > 0 and cc == 0 and c2 == 0:
                                for s_ in ("sO0", "sO1", "sY0", "sY1",
                                           "sU0", "sU1"):
                                    wait("tensor", s_, end_prev[s_])
                        lastS = 0
                        for dt in range(DC):
                            fn = (lambda e, o_=pb[sb_][:, 0:256],
                                  l=kT[:, dt * 2048 + k * 128: dt * 2048 + (k + 1) * 128],
                                  r=qT[:, dt * 512 + toff: dt * 512 + toff + 256],
                                  st=(dt == 0), sp=(dt == DC - 1):
                                  e.matmul(o_, l, r, start=st, stop=sp))
                            if dt == DC - 1:
                                lastS = inc("tensor", "sT", fn)
                            else:
                                emit("tensor", fn)
                        mk = 0
                        if k == 2 * c:
                            wait("vector", "sT", lastS)
                            mk = inc("vector", "sV",
                                     lambda e, o_=pb[sb_][:, 0:128]:
                                     e.tensor_tensor(out=o_, in0=o_, in1=maskd, op=ALU.add))
                        elif k == 2 * c + 1:
                            wait("vector", "sT", lastS)
                            emit("vector",
                                 lambda e, o_=pb[sb_][:, 0:128]: e.memset(o_, NEG))
                            mk = inc("vector", "sV",
                                     lambda e, o_=pb[sb_][:, 128:256]:
                                     e.tensor_tensor(out=o_, in0=o_, in1=maskd, op=ALU.add))
                        if mk:
                            wait("scalar", "sV", mk)
                        else:
                            wait("scalar", "sT", lastS)
                        if pt_rdT[k % 2]:
                            wait("scalar", "sT", pt_rdT[k % 2])
                        ex = inc("scalar", "sA",
                                 lambda e, d=pt[k % 2], s=pb[sb_][:, 0:256]:
                                 e.activation(out=d, in_=s, func=AF.Exp,
                                              scale=1.0 / 32.0, bias=sca[:, 0:1]))
                        sc_exA[k % 2] = ex
                        wait("tensor", "sA", ex)
                        if k == 0 and zr_evV:
                            wait("tensor", "sV", zr_evV)
                        emit("tensor",
                             lambda e, o_=pb[6][0:1, toff:toff + 256], r=pt[k % 2],
                             st=(k == 0), sp=(k == nk - 1):
                             e.matmul(o_, ones_c, r, start=st, stop=sp))
                        if cc == 0 and c2 == 0 and k == 0:
                            wait("tensor", "sV", v_done_V)
                        for sub in range(2):
                            gsub = 2 * c + sub
                            if k > gsub:
                                continue
                            for dh in range(2):
                                gg = sub * 2 + dh
                                fn = (lambda e, o_=pb[gg][:, 0:512],
                                      l=pt[k % 2][:, sub * 128:(sub + 1) * 128],
                                      r=v_s[:, k * 1024 + dh * 512: k * 1024 + (dh + 1) * 512],
                                      st=(k == 0), sp=(k == 2 * c + sub):
                                      e.matmul(o_, l, r, start=st, stop=sp))
                                if sub == 1 and dh == 1:
                                    pt_rdT[k % 2] = inc("tensor", "sT", fn)
                                else:
                                    emit("tensor", fn)
                        last_pv_mm = pt_rdT[k % 2]
                    for sub in range(2):
                        for dh in range(2):
                            gg = sub * 2 + dh
                            wait("vector", "sT", last_pv_mm)
                            out_ev[gg] = inc("vector", "sV",
                                             lambda e,
                                             d=y[:, (2 * c + sub) * 1024 + dh * 512:
                                                 (2 * c + sub) * 1024 + (dh + 1) * 512],
                                             s=pb[gg][:, 0:512]:
                                             e.tensor_copy(out=d, in_=s))
                    wait("vector", "sT", last_pv_mm)
                    zr_evV = inc("vector", "sV",
                                 lambda e, d=zacc[0:1, c:c + 1],
                                 s=pb[6][0:1, toff:toff + 256]:
                                 e.reduce_sum(d, s, axis=AX.X))

            wait("vector", "sV", zr_evV)
            wv_ = inc("vector", "sV",
                      lambda e: e.reduce_sum(s1v[0:1, 5:6], zacc[0:1, 0:8], axis=AX.X))
            wait("vector", "sV", wv_)
            wv_ = inc("vector", "sV",
                      lambda e: e.reciprocal(out=s1v[0:1, 6:7], in_=s1v[0:1, 5:6]))
            wait("tensor", "sV", wv_)
            wait("tensor", "sA", sc_exA[0])
            wait("tensor", "sA", sc_exA[1])
            bc2 = inc("tensor", "sT",
                      lambda e: e.matmul(pb[4][:, 0:1], ones_r, s1v[0:1, 6:7],
                                         start=True, stop=True))
            wait("vector", "sT", bc2)
            zinv_rdy = inc("vector", "sV",
                           lambda e: e.tensor_copy(out=sca[:, 1:2], in_=pb[4][:, 0:1]))

            def load_gu(e_, h2_, wait_T_gu, pi_):
                if wait_T_gu:
                    wait("gpsimd", "sT", wait_T_gu)
                inc("gpsimd", "sGU%d" % pi_,
                    lambda e, ee=e_, hh=h2_, d=gbufs[pi_]: e.dma_start(
                        out=d.rearrange("p (c h) -> p c h", c=DC),
                        in_=gw_h[ee].rearrange("(c p) h -> p c h", p=128)[:, :, hh * HH:(hh + 1) * HH]), 16)
                return inc("gpsimd", "sGU%d" % pi_,
                           lambda e, ee=e_, hh=h2_, d=ubufs[pi_]: e.dma_start(
                               out=d.rearrange("p (c h) -> p c h", c=DC),
                               in_=uw_h[ee].rearrange("(c p) h -> p c h", p=128)[:, :, hh * HH:(hh + 1) * HH]), 16)

            def load_d(e_, h2_, wait_T_d, pi_):
                if wait_T_d:
                    wait("gpsimd", "sT", wait_T_d)
                return inc("gpsimd", "sWD%d" % pi_,
                           lambda e, ee=e_, hh=h2_, d=dbufs[pi_]: e.dma_start(
                               out=d.rearrange("p (c d) -> p c d", c=DC),
                               in_=dw_h[ee][hh * HH:(hh + 1) * HH, :].rearrange("(c p) d -> p c d", p=128)), 16)

            wait("gpsimd", "sT", last_pv_mm)
            ph0_half = {}
            for snm, th, lo in (("sQ0", gw_h, 0), ("sQ2", uw_h, 0),
                                ("sQ1", gw_h, 4), ("sQ3", uw_h, 4)):
                dst = gbuf if th is gw_h else ubuf
                ph0_half[snm] = inc("gpsimd", snm,
                                    lambda e, d=dst, t=th, l_=lo: e.dma_start(
                                        out=d.rearrange("p (c h) -> p c h", c=DC)[:, l_:l_ + 4, :],
                                        in_=t[0].rearrange("(c p) h -> p c h", p=128)[:, l_:l_ + 4, 0:HH]), 16)
            d_ld = inc("gpsimd", "sWD0",
                       lambda e: e.dma_start(
                           out=dbuf.rearrange("p (c d) -> p c d", c=DC),
                           in_=dw_h[0][0:HH, :].rearrange("(c p) d -> p c d", p=128)), 16)

            resc_j = [0] * NT
            resc_nx = [0] * NT
            fin_st = [0, 0]

            def emit_resc_dma(j):
                p = j % 2
                srp = "sR%d" % p
                if j < 2:
                    wait("sync", "sT", last_pv_mm)
                else:
                    wait("sync", "sV", resc_j[j - 2])
                resc_nx[j] = inc("sync", srp,
                                 lambda e, d=xst[p], s=x_hbm[B0 + j * 128:B0 + (j + 1) * 128, :]:
                                 e.dma_start(out=d, in_=s), 16)

            def emit_resc_stt(j):
                p = j % 2
                wait("vector", "sR%d" % p, resc_nx[j])
                resc_j[j] = inc("vector", "sV",
                                lambda e, d=y[:, j * 1024:(j + 1) * 1024], x_=xst[p]:
                                e.scalar_tensor_tensor(out=d, in0=d, scalar=sca[:, 1:2],
                                                       in1=x_, op0=ALU.mult, op1=ALU.add))
                if j + 2 < NT:
                    emit_resc_dma(j + 2)
                grp = "sO" if j < 8 else ("sY" if j < 12 else "sU")
                wait("sync", "sV", resc_j[j])
                fin_st[j % 2] = inc("sync", "%s%d" % (grp, j % 2),
                                    lambda e, d=out_hbm[B0 + j * 128:B0 + (j + 1) * 128, :],
                                    s=y[:, j * 1024:(j + 1) * 1024]:
                                    e.dma_start(out=d, in_=s), 16)

            for j in range(2):
                emit_resc_dma(j)

            if dbg and b == 0:
                wait("sync", "sV", zinv_rdy)
                dbg_sX = inc("sync", "sC", lambda e: e.dma_start(out=dbg_s, in_=s1v), 16)

            if dbg and b == 0:
                wait("gpsimd", "sC", dbg_sX)
                wait("tensor", "sC", dbg_sX)


            zr = inc("sync", "sZ", lambda e: e.dma_start(out=y12, in_=zsrc_h), 16)
            wait("sync", "sT", last_pv_mm)
            lt_ld = inc("sync", "sC", lambda e: e.dma_start(out=lt_s, in_=lt_h), 16)
            tid_ld = inc("sync", "sC", lambda e: e.dma_start(out=tidc, in_=tidc_h), 16)
            wait("tensor", "sC", tid_ld)
            wait("vector", "sC", tid_ld)
            wait("gpsimd", "sT", last_pv_mm)
            g_io = inc("gpsimd", "sP",
                       lambda e: e.iota(iota_i, pattern=[[1, CAP]], base=0,
                                        channel_multiplier=0))
            wait("vector", "sP", g_io)
            wait("vector", "sT", last_pv_mm)
            wv_ = inc("vector", "sV", lambda e: e.tensor_copy(out=iota_f, in_=iota_i))
            wait("vector", "sV", wv_)
            wait("tensor", "sA", qb_wA[1] if qb_wA[1] else qb_wA[0])
            idxi_done = [0] * NE
            xeT_ev = [0] * NE
            bcop = 0
            evac_last = 0
            slab_mm = [0, 0]
            for e_ in range(NE):
                wv_ = inc("vector", "sV",
                          lambda e, ee=e_: e.tensor_single_scalar(
                              out=m_s, in_=c_all.rearrange("p (t e) -> p t e", e=NE)[:, :, ee],
                              scalar=0.0, op=ALU.is_gt))
                wait("vector", "sV", wv_)
                wv_ = inc("vector", "sV",
                          lambda e: e.tensor_tensor_scan(out=sc_s, data0=m_s, data1=m_s,
                                                         initial=0.0, op0=ALU.add,
                                                         op1=ALU.bypass))
                wait("vector", "sV", wv_)
                wv_ = inc("vector", "sV", lambda e: e.tensor_copy(out=rt_b, in_=sc_s[:, 15:16]))
                wait("tensor", "sV", wv_)
                if bcop:
                    wait("tensor", "sV", evac_last)
                bmm = inc("tensor", "sT",
                          lambda e: e.matmul(pb[7][:, 0:1], lt_s, rt_b, start=True, stop=True))
                wait("vector", "sT", bmm)
                bcop = inc("vector", "sV", lambda e: e.tensor_copy(out=b_s, in_=pb[7][:, 0:1]))
                wait("vector", "sV", bcop)
                wv_ = inc("vector", "sV",
                          lambda e: e.tensor_scalar(out=pos_s, in0=sc_s, scalar1=b_s,
                                                    scalar2=-10000.0, op0=ALU.add, op1=ALU.add))
                wait("vector", "sV", wv_)
                wv_ = inc("vector", "sV", lambda e: e.tensor_mul(pos_s, pos_s, m_s))
                wait("vector", "sV", wv_)
                wv_ = inc("vector", "sV", lambda e: e.tensor_scalar_add(pos_s, pos_s, 9999.0))
                wait("vector", "sV", wv_)
                r3 = rhs48.rearrange("p (t c) -> p t c", c=3)
                wv_ = inc("vector", "sV", lambda e: e.memset(r3[:, :, 1:2], 1.0))
                wait("vector", "sV", wv_)
                wv_ = inc("vector", "sV",
                          lambda e: e.tensor_copy(out=r3[:, :, 0:1],
                                                  in_=tidc.rearrange("p (t o) -> p t o", o=1)))
                wait("vector", "sV", wv_)
                wv_ = inc("vector", "sV",
                          lambda e, ee=e_: e.tensor_copy(
                              out=r3[:, :, 2:3],
                              in_=c_all.rearrange("p (t e) -> p t e", e=NE)[:, :, ee:ee + 1]))
                wv_ = inc("vector", "sV",
                          lambda e: e.tensor_tensor(
                              out=cmp4.rearrange("p (t s) -> p t s", s=4),
                              in0=pos_s.rearrange("p (t o) -> p t o", o=1)
                                  .broadcast_to([128, NT, 4]),
                              in1=thr4_s.rearrange("p (o s) -> p o s", o=1)
                                  .broadcast_to([128, NT, 4]),
                              op=ALU.is_ge))
                wait("vector", "sV", wv_)
                wv_ = inc("vector", "sV",
                          lambda e: e.tensor_reduce(
                              q_s2, cmp4.rearrange("p (t s) -> p t s", s=4),
                              axis=AX.X, op=ALU.add))
                wait("vector", "sV", wv_)
                wv_ = inc("vector", "sV",
                          lambda e: e.scalar_tensor_tensor(
                              out=r_s2, in0=q_s2, scalar=-128.0, in1=pos_s,
                              op0=ALU.mult, op1=ALU.add))
                wait("vector", "sV", wv_)
                wv_ = inc("vector", "sV",
                          lambda e: e.tensor_tensor(
                              out=mask5.rearrange("p (t q) -> p t q", q=NSL),
                              in0=q_s2.rearrange("p (t o) -> p t o", o=1)
                                  .broadcast_to([128, NT, NSL]),
                              in1=iota_f[:, 0:NSL].rearrange("p (o q) -> p o q", o=1)
                                  .broadcast_to([128, NT, NSL]),
                              op=ALU.is_equal))
                wait("vector", "sV", wv_)
                r15v = rhs15.rearrange("p (t w) -> p t w", w=15)
                m5v = mask5.rearrange("p (t q) -> p t q", q=NSL)
                for qc in range(NSL):
                    wv_ = inc("vector", "sV",
                              lambda e, q_=qc: e.tensor_tensor(
                                  out=r15v[:, :, q_ * 3:q_ * 3 + 3], in0=r3,
                                  in1=m5v[:, :, q_:q_ + 1].broadcast_to([128, NT, 3]),
                                  op=ALU.mult))
                wait("vector", "sV", wv_)
                mm_end = 0
                for half in range(2):
                    if slab_mm[half]:
                        wait("vector", "sT", slab_mm[half])
                    ohv = inc("vector", "sV",
                              lambda e, h8=half * 8, sl=ohAB[half]: e.tensor_tensor(
                                  out=sl.rearrange("p (t j) -> p t j", t=8),
                                  in0=r_s2[:, h8:h8 + 8]
                                      .rearrange("p (t o) -> p t o", o=1)
                                      .broadcast_to([128, 8, 128]),
                                  in1=iota_f[:, 0:128]
                                      .rearrange("p (o j) -> p o j", o=1)
                                      .broadcast_to([128, 8, 128]),
                                  op=ALU.is_equal))
                    wait("tensor", "sV", ohv)
                    if half == 0 and e_ > 0:
                        wait("tensor", "sV", evac_last)
                    for tl8 in range(8):
                        tl = half * 8 + tl8
                        fn = (lambda e, t_=tl, t8=tl8, sl=ohAB[half]: e.matmul(
                            pb[7][:, 4:19], sl[:, t8 * 128:(t8 + 1) * 128],
                            rhs15[:, t_ * 15:t_ * 15 + 15],
                            start=(t_ == 0), stop=(t_ == 15)))
                        if tl8 == 7:
                            slab_mm[half] = inc("tensor", "sT", fn)
                        else:
                            emit("tensor", fn)
                mm_end = slab_mm[1]
                wait("vector", "sT", mm_end)
                evac_last = inc("vector", "sV",
                                lambda e: e.tensor_copy(out=acc15, in_=pb[7][:, 4:19]))
                wait("vector", "sV", evac_last)
                a3 = acc15.rearrange("p (q k) -> p q k", k=3)
                wv_ = inc("vector", "sV",
                          lambda e, o_=idxf[:, e_ * NSL:(e_ + 1) * NSL]
                          .rearrange("p (q o) -> p q o", o=1):
                          e.scalar_tensor_tensor(
                              out=o_, in0=a3[:, :, 1:2], scalar=-2047.0,
                              in1=a3[:, :, 0:1], op0=ALU.mult, op1=ALU.add))
                wait("vector", "sV", wv_)
                wv_ = inc("vector", "sV",
                          lambda e, c0=e_ * NSL: e.tensor_scalar_add(
                              idxf[:, c0:c0 + NSL], idxf[:, c0:c0 + NSL], 2047.0))
                wait("vector", "sV", wv_)
                wv_ = inc("vector", "sV",
                          lambda e, o_=cg_s[:, e_ * NSL:(e_ + 1) * NSL]
                          .rearrange("p (q o) -> p q o", o=1):
                          e.tensor_copy(out=o_, in_=a3[:, :, 2:3]))
                wait("vector", "sV", wv_)
                if dbg and b == 0 and e_ == 0:
                    wait("sync", "sT", slab_mm[1])
                    inc("sync", "sC", lambda e: e.dma_start(
                        out=dbg_yat[:, 0:1024], in_=ohAB[0]), 16)
                    inc("sync", "sC", lambda e: e.dma_start(
                        out=dbg_yat[:, 1024:2048], in_=ohAB[1]), 16)
                    inc("sync", "sC", lambda e: e.dma_start(
                        out=dbg_yat[:, 2048:2288], in_=rhs15), 16)
                    inc("sync", "sC", lambda e: e.dma_start(
                        out=dbg_yat[:, 2288:2304], in_=r_s2), 16)
                    inc("sync", "sC", lambda e: e.dma_start(
                        out=dbg_yat[:, 2304:2319], in_=acc15), 16)
                idxi_done[e_] = inc("vector", "sV",
                                    lambda e, ee=e_: e.tensor_copy(
                                        out=idxi[:, ee * NSL:(ee + 1) * NSL],
                                        in_=idxf[:, ee * NSL:(ee + 1) * NSL]))

            ga_ring = [0, 0]
            tr_prev = [0, 0]


            ga_pend = {}

            def emit_gather(e_, s):
                ring = s % 2
                wait("gpsimd", "sV", idxi_done[e_])
                wait("gpsimd", "sXN0", xn_st[NT - 2])
                wait("gpsimd", "sXN1", xn_st[NT - 1])
                if ga_ring[ring]:
                    wait("gpsimd", "sT", ga_ring[ring])
                ga_pend[(e_, s)] = inc(
                    "gpsimd", "sGA%d" % ring,
                    lambda e, ee=e_, s_=s, xb=xnd[b % 2]: e.indirect_dma_start(
                        out=xe_tm[s_ % 2], out_offset=None, in_=xb,
                        in_offset=IndirectOffsetOnAxis(
                            ap=idxi[:, ee * NSL + s_:ee * NSL + s_ + 1], axis=0)), 16)

            def emit_transpose(e_, s, bank_waitA, bank_waitV):
                ring = s % 2
                wait("tensor", "sGA%d" % ring, ga_pend[(e_, s)])
                if bank_waitA[ring]:
                    wait("tensor", "sA", bank_waitA[ring])
                    bank_waitA[ring] = 0
                if bank_waitV[ring]:
                    wait("tensor", "sV", bank_waitV[ring])
                    bank_waitV[ring] = 0
                if tr_prev[ring]:
                    wait("tensor", "sV", tr_prev[ring])
                trd = 0
                for dc in range(DC):
                    fn = (lambda e, d=pbh[ring][:, dc * 128:(dc + 1) * 128],
                          m_=xe_tm[ring][:, dc * 128:(dc + 1) * 128]:
                          e.transpose(d, m_, identity))
                    if dc == DC - 1:
                        trd = inc("tensor", "sT", fn)
                    else:
                        emit("tensor", fn)
                ga_ring[ring] = trd
                wait("vector", "sT", trd)
                evs = inc("vector", "sV",
                          lambda e, s_=s:
                          e.tensor_copy(
                              out=xeT.rearrange("p (c t) -> p c t", c=DC)[:, :, s_ * 128:(s_ + 1) * 128],
                              in_=pbh[s_ % 2].rearrange("p (c t) -> p c t", c=DC)))
                tr_prev[ring] = evs
                xeT_ev[e_] = evs
                if s + 2 < NSL:
                    emit_gather(e_, s + 2)

            def emit_xeT(e_, bank_waitA, bank_waitV):
                emit_gather(e_, 0)
                emit_gather(e_, 1)
                for s in range(NSL):
                    emit_transpose(e_, s, bank_waitA, bank_waitV)

            emit_xeT(0, [0, 0], [zinv_rdy, zinv_rdy])

            gate_bank_A = [0, 0]
            up_bank_V = [0, 0]
            silu_war_V = [0, 0]
            dn_ev_V = [0, 0, 0, 0]
            ffn_last_mm = 0
            ffn_hid_last = 0
            first_ffn_mm = True
            sc_prev = 0
            sc_stage = [0, 0]
            fin_za_seed = (cnt["sZA0"], cnt["sZA1"])
            first_sc = True
            TCS = ((0, 512), (512, 128))
            gu_tok = [0] * 17
            d_tok = [0] * 17
            d_tok[0] = d_ld
            for ph in range(16):
                e_, h2_ = ph // 2, ph % 2
                pi = ph % 2
                gb_c, ub_c, db_c = gbufs[pi], ubufs[pi], dbufs[pi]
                if 1 <= ph < 15:
                    npi = (ph + 1) % 2
                    gu_tok[ph + 1] = load_gu((ph + 1) // 2, (ph + 1) % 2,
                                             prev_gu_mm[npi], npi)
                    d_tok[ph + 1] = load_d((ph + 1) // 2, (ph + 1) % 2,
                                           prev_d_mm[npi], npi)
                hid_last_V = 0
                last_gu_mm = 0
                resc_todo = list(range(NT)) if ph == 0 else []
                first_in_ph = True
                for tof, tsz in TCS:
                    for ht in range(NHT):
                        pgi = ht % 2
                        first_grp = (ph == 0 and tof == 0 and ht == 0)
                        if first_in_ph:
                            wait("tensor", "sV", xeT_ev[e_])
                            if ph > 0:
                                wait("tensor", "sGU%d" % pi, gu_tok[ph])
                            first_in_ph = False
                        if ph == 0 and tof == 0 and ht == 1:
                            wait("tensor", "sQ1", ph0_half["sQ1"])
                            wait("tensor", "sQ3", ph0_half["sQ3"])
                        if first_ffn_mm:
                            wait("tensor", "sV", zinv_rdy)
                            first_ffn_mm = False
                        if gate_bank_A[pgi]:
                            wait("tensor", "sA", gate_bank_A[pgi])
                        mg = 0
                        for kc in range(DC):
                            if first_grp and kc == 0:
                                wait("tensor", "sQ0", ph0_half["sQ0"])
                            if first_grp and kc == 4:
                                wait("tensor", "sQ1", ph0_half["sQ1"])
                            fn = (lambda e, o_=pb[pgi][:, 0:tsz],
                                  l=gb_c[:, kc * 1024 + ht * 128: kc * 1024 + (ht + 1) * 128],
                                  r=xeT[:, kc * CAP + tof: kc * CAP + tof + tsz],
                                  st=(kc == 0), sp=(kc == DC - 1):
                                  e.matmul(o_, l, r, start=st, stop=sp))
                            if kc == DC - 1:
                                mg = inc("tensor", "sT", fn)
                            else:
                                emit("tensor", fn)
                        if up_bank_V[pgi]:
                            wait("tensor", "sV", up_bank_V[pgi])
                        mu = 0
                        for kc in range(DC):
                            if first_grp and kc == 0:
                                wait("tensor", "sQ2", ph0_half["sQ2"])
                            if first_grp and kc == 4:
                                wait("tensor", "sQ3", ph0_half["sQ3"])
                            fn = (lambda e, o_=pb[2 + pgi][:, 0:tsz],
                                  l=ub_c[:, kc * 1024 + ht * 128: kc * 1024 + (ht + 1) * 128],
                                  r=xeT[:, kc * CAP + tof: kc * CAP + tof + tsz],
                                  st=(kc == 0), sp=(kc == DC - 1):
                                  e.matmul(o_, l, r, start=st, stop=sp))
                            if kc == DC - 1:
                                mu = inc("tensor", "sT", fn)
                            else:
                                emit("tensor", fn)
                        last_gu_mm = mu
                        wait("scalar", "sT", mg)
                        if silu_war_V[pgi]:
                            wait("scalar", "sV", silu_war_V[pgi])
                        asl = inc("scalar", "sA",
                                  lambda e, d=silu[pgi][:, 0:tsz], s=pb[pgi][:, 0:tsz]:
                                  e.activation(out=d, in_=s, func=AF.Silu))
                        gate_bank_A[pgi] = asl
                        wait("vector", "sA", asl)
                        wait("vector", "sT", mu)
                        hv = inc("vector", "sV",
                                 lambda e,
                                 d=hid[:, ht * CAP + tof: ht * CAP + tof + tsz],
                                 s=silu[pgi][:, 0:tsz], u=pb[2 + pgi][:, 0:tsz]:
                                 e.tensor_tensor(out=d, in0=s, in1=u, op=ALU.mult))
                        up_bank_V[pgi] = hv
                        silu_war_V[pgi] = hv
                        hid_last_V = hv
                        if resc_todo:
                            jj = resc_todo.pop(0)
                            emit_resc_stt(jj)
                            if ph == 0 and jj == 7:
                                wait("gpsimd", "sO0", cnt["sO0"])
                                wait("gpsimd", "sO1", cnt["sO1"])
                                gu_tok[1] = load_gu(0, 1, prev_gu_mm[1], 1)
                            elif ph == 0 and jj == 11:
                                wait("gpsimd", "sY0", cnt["sY0"])
                                wait("gpsimd", "sY1", cnt["sY1"])
                                d_tok[1] = load_d(0, 1, prev_d_mm[1], 1)
                if h2_ == 1 and e_ < NE - 1:
                    for s_g in range(2):
                        emit_gather(e_ + 1, s_g)
                last_dn_mm = 0
                tr_todo = list(range(NSL)) if (h2_ == 1 and e_ < NE - 1) else []
                for tile in range(NSL):
                    ring = tile % 2
                    evd = [0, 0]
                    for dh in range(2):
                        gg = 4 + (tile % 2) * 2 + dh
                        if tile == 0 and dh == 0:
                            wait("tensor", "sWD%d" % pi, d_tok[ph])
                        wait("tensor", "sV", hid_last_V)
                        if dn_ev_V[(tile % 2) * 2 + dh]:
                            wait("tensor", "sV", dn_ev_V[(tile % 2) * 2 + dh])
                        md = 0
                        for ht in range(NHT):
                            fn = (lambda e, o_=pb[gg][:, 0:512],
                                  l=hid[:, ht * CAP + tile * 128: ht * CAP + (tile + 1) * 128],
                                  r=db_c[:, ht * 1024 + dh * 512: ht * 1024 + (dh + 1) * 512],
                                  st=(ht == 0), sp=(ht == NHT - 1):
                                  e.matmul(o_, l, r, start=st, stop=sp))
                            if ht == NHT - 1:
                                md = inc("tensor", "sT", fn)
                            else:
                                emit("tensor", fn)
                        last_dn_mm = md
                        wait("vector", "sT", md)
                        if sc_stage[ring]:
                            wait("vector", "sSC", sc_stage[ring])
                        elif fin_za_seed[ring]:
                            wait("vector", "sZA%d" % ring, fin_za_seed[ring])
                        evd[dh] = inc("vector", "sV",
                                      lambda e, rg=ring, d_=dh,
                                      s=pb[gg][:, 0:512],
                                      cs=cg_s[:, e_ * NSL + tile:e_ * NSL + tile + 1]:
                                      e.tensor_scalar_mul(stage[rg][:, d_ * 512:(d_ + 1) * 512],
                                                          s, cs))
                        dn_ev_V[(tile % 2) * 2 + dh] = evd[dh]
                    wait("gpsimd", "sV", evd[1])
                    if first_sc:
                        wait("gpsimd", "sZ", zr)
                        first_sc = False
                    if sc_prev:
                        wait("gpsimd", "sSC", sc_prev)
                    sc_stage[ring] = cnt["sSC"] + 16
                    sc_prev = inc("gpsimd", "sSC",
                                  lambda e, ee=e_, t_=tile, rg=ring: e.indirect_dma_start(
                                      out=y12,
                                      out_offset=IndirectOffsetOnAxis(
                                          ap=idxi[:, ee * NSL + t_:ee * NSL + t_ + 1], axis=0),
                                      in_=stage[rg], in_offset=None,
                                      compute_op=ALU.add), 16)
                    if tr_todo and tile >= 1:
                        emit_transpose(e_ + 1, tr_todo.pop(0),
                                       [gate_bank_A[0], gate_bank_A[1]], [0, 0])
                while tr_todo:
                    emit_transpose(e_ + 1, tr_todo.pop(0),
                                   [gate_bank_A[0], gate_bank_A[1]], [0, 0])
                prev_gu_mm[pi] = last_gu_mm
                prev_d_mm[pi] = last_dn_mm
                ffn_last_mm = last_dn_mm
                if ph == 15:
                    ffn_hid_last = hid_last_V

            if dbg and b == 0:
                wait("sync", "sV", idxi_done[NE - 1])
                inc("sync", "sC", lambda e: e.dma_start(out=dbg_m[:, 0:640], in_=scrf[:, 19456:20096]), 16)
                inc("sync", "sC", lambda e: e.dma_start(out=dbg_m[:, 640:656], in_=r_s2), 16)
                inc("sync", "sC", lambda e: e.dma_start(out=dbg_m[:, 656:671], in_=acc15), 16)
                inc("sync", "sC", lambda e: e.dma_start(out=dbg_m[:, 671:1280], in_=ohAB[1][:, 0:609]), 16)
                inc("sync", "sC", lambda e: e.dma_start(out=dbg_idx, in_=idxf), 16)
                inc("sync", "sC", lambda e: e.dma_start(out=dbg_cg, in_=cg_s), 16)
                wait("sync", "sSC", cnt["sSC"])
                inc("sync", "sC", lambda e: e.dma_start(out=dbg_y12, in_=y12), 16)

            def make_final(B0_=B0, n_sc=cnt["sSC"],
                           n_st=tuple((s_, cnt[s_]) for s_ in
                                      ("sO0", "sO1", "sY0", "sY1", "sU0", "sU1"))):
                def emit_final():
                    wait("sync", "sSC", n_sc)
                    for s_, v_ in n_st:
                        wait("gpsimd", s_, v_)
                    rb_n = [0, 0]
                    za_n = [0, 0]
                    for i in range(NT):
                        p = i % 2
                        if za_n[p]:
                            wait("sync", "sZA%d" % p, za_n[p])
                        rb_n[p] = inc("sync", "sR%d" % p,
                                      lambda e, i_=i, pp=p: e.dma_start(
                                          out=stage[pp],
                                          in_=y12[i_ * 128:(i_ + 1) * 128, :]), 16)
                        wait("gpsimd", "sR%d" % p, rb_n[p])
                        za_n[p] = inc("gpsimd", "sZA%d" % p,
                                      lambda e,
                                      d=out_hbm[B0_ + i * 128:B0_ + (i + 1) * 128, :],
                                      pp=p: e.dma_start(out=d, in_=stage[pp],
                                                        accum_op=ALU.add), 16)
                return emit_final
            pending_final = make_final()

            resc_last = resc_j[NT - 1]
            end_prev = {"sT": ffn_last_mm, "sA": cnt["sA"], "sV": cnt["sV"],
                        "sO0": cnt["sO0"], "sO1": cnt["sO1"], "hid": ffn_hid_last,
                        "sY0": cnt["sY0"], "sY1": cnt["sY1"],
                        "sU0": cnt["sU0"], "sU1": cnt["sU1"],
                        "sSC": cnt["sSC"],
                        "sZA0": cnt["sZA0"], "sZA1": cnt["sZA1"]}

        if pending_final is not None:
            pending_final()
            pending_final = None
        for s_ in ("sO0", "sO1", "sY0", "sY1", "sU0", "sU1"):
            wait("sync", s_, cnt[s_])
        wait("sync", "sZA0", cnt["sZA0"])
        wait("sync", "sZA1", cnt["sZA1"])

        with nc.Block() as block:
            @block.sync
            def _(e):
                for f in ops["sync"]:
                    f(e)

            @block.gpsimd
            def _(e):
                for f in ops["gpsimd"]:
                    f(e)

            @block.tensor
            def _(e):
                for f in ops["tensor"]:
                    f(e)

            @block.scalar
            def _(e):
                for f in ops["scalar"]:
                    f(e)

            @block.vector
            def _(e):
                for f in ops["vector"]:
                    f(e)

    return nc


def make_consts():
    s = np.arange(128)[:, None]
    t = np.arange(128)[None, :]
    tid = (np.arange(16)[None, :] * 128 + np.arange(128)[:, None]).astype(np.float32)
    return {
        "ident": np.eye(128, dtype=ml_dtypes.bfloat16),
        "ident32": np.eye(128, dtype=np.float32),
        "maskd": np.ascontiguousarray(np.where(s <= t, 0.0, NEG).astype(np.float32)),
        "lt": np.ascontiguousarray(np.tril(np.ones((128, 128), np.float32), -1).T
                                   .astype(ml_dtypes.bfloat16)),
        "ones_c": np.ones((128, 1), dtype=ml_dtypes.bfloat16),
        "ones_r": np.ones((1, 128), dtype=np.float32),
        "eps": np.full((128, 1), 1e-5, dtype=np.float32),
        "tidc": np.ascontiguousarray(tid),
        "thr4": np.ascontiguousarray(np.broadcast_to(
            (np.arange(1, 5, dtype=np.float32) * 128.0)[None, :], (128, 4))),
        "zsrc": np.zeros((T, D), dtype=np.float32),
    }


def make_in_maps(inputs, consts):
    x = np.asarray(inputs["L0_attn_x"], dtype=np.float32)
    shared = {
        "wq": np.ascontiguousarray(np.asarray(inputs["L0_attn_Wq"], np.float32)),
        "wk": np.ascontiguousarray(np.asarray(inputs["L0_attn_Wk"], np.float32)),
        "wv": np.ascontiguousarray(np.asarray(inputs["L0_attn_Wv"], np.float32)),
        "bqT": np.ascontiguousarray(np.asarray(inputs["L0_attn_bq"], np.float32).reshape(DC, 128).T),
        "bkT": np.ascontiguousarray(np.asarray(inputs["L0_attn_bk"], np.float32).reshape(DC, 128).T),
        "bvb": np.ascontiguousarray(np.broadcast_to(np.asarray(inputs["L0_attn_bv"], np.float32), (128, D))),
        "rw": np.ascontiguousarray(np.asarray(inputs["L0_ffn_router_w"], np.float32)),
        "gw": np.ascontiguousarray(np.asarray(inputs["L0_ffn_gate_w"], np.float32)),
        "uw": np.ascontiguousarray(np.asarray(inputs["L0_ffn_up_w"], np.float32)),
        "dw": np.ascontiguousarray(np.asarray(inputs["L0_ffn_down_w"], np.float32)),
    }
    shared.update(consts)
    in_maps = []
    for i in range(NCORES):
        m = dict(shared)
        m["x"] = np.ascontiguousarray(x[2 * i:2 * i + 2].reshape(TB * T, D))
        in_maps.append(m)
    return in_maps


def kernel(**inputs):
    nc = build_program(dbg=False)
    in_maps = make_in_maps(inputs, make_consts())
    res = run_bass_kernel_spmd(nc, in_maps, core_ids=list(range(NCORES)))
    outs = [res.results[i]["out"].reshape(TB, T, D) for i in range(NCORES)]
    return np.concatenate(outs, axis=0).astype(np.float32)
